# version 06
# baseline (speedup 1.0000x reference)
import numpy as np
from contextlib import ExitStack

import concourse.bass as bass
import concourse.mybir as mybir
from concourse.bass_utils import run_bass_kernel_spmd

F32 = mybir.dt.float32
BF16 = mybir.dt.bfloat16
AF = mybir.ActivationFunctionType
ALU = mybir.AluOpType
AX = mybir.AxisListType.X

D = 2048
T = 4096
NB = 4
NH = 16
DH = 128
KT = 16
TL = 2048
INTER = 1408
IT = 11
NE = 16
NEXP = 18
EPS = 1e-5
EXP_BIAS = -15.0
ISCALE = float(1.0 / np.sqrt(DH))

DEBUG_SCRATCH = False
PHASES = "ABC"
A_CHUNKS = 8
A2_ON = True
A_STAGE = 9
A_ROPE = 4

ENGS = ("tensor", "scalar", "vector", "gpsimd", "sync")


class Res:

    __slots__ = ("name", "writer", "readers", "dsem", "xread")

    def __init__(self, name, xread=False):
        self.name = name
        self.writer = None
        self.readers = {}
        self.dsem = None
        self.xread = xread


class B:

    def __init__(self, nc, ctx):
        self.nc = nc
        self.ctx = ctx
        self.streams = {e: [] for e in ENGS}
        self.sem = {e: self._mk_sem("sm_" + e) for e in ENGS}
        self.semv = {s: 0 for s in self.sem.values()}
        self.hw = {e: {} for e in ENGS}
        self._ndsem = 0

    def _mk_sem(self, name):
        return self.ctx.enter_context(self.nc.semaphore(name))

    def res(self, name, xread=False):
        return Res(name, xread)

    def _wait(self, eng, s, v):
        if v <= self.hw[eng].get(s, 0):
            return
        self.hw[eng][s] = v
        self.streams[eng].append(lambda E, s=s, v=v: E.wait_ge(s, v))

    def op(self, eng, fn, reads=(), writes=(), dma=False):
        assert eng in ENGS
        for r in reads:
            if r.xread:
                for s2, (v2, reng2) in r.readers.items():
                    if reng2 != eng:
                        self._wait(eng, s2, v2)
            w = r.writer
            if w is None:
                continue
            s, v, weng = w
            if weng == eng == "tensor":
                continue
            if weng == eng and not dma:
                self._wait(eng, s, v)
            else:
                self._wait(eng, s, v)
        for wres in writes:
            for s, (v, reng) in wres.readers.items():
                if reng == eng == "tensor":
                    continue
                self._wait(eng, s, v)
            w = wres.writer
            if w is not None:
                s, v, weng = w
                if weng == eng == "tensor":
                    pass
                else:
                    self._wait(eng, s, v)
        if dma:
            assert len(writes) == 1, "each DMA must write exactly one resource"
            wres = writes[0]
            if wres.dsem is None:
                wres.dsem = self._mk_sem(f"sd{self._ndsem}_{wres.name}")
                self._ndsem += 1
                self.semv[wres.dsem] = 0
            s = wres.dsem
            self.semv[s] += 16
            v = self.semv[s]
            def run(E, fn=fn, s=s):
                ins = fn(E)
                ins.then_inc(s, 16)
            self.streams[eng].append(run)
        else:
            s = self.sem[eng]
            self.semv[s] += 1
            v = self.semv[s]
            def run(E, fn=fn, s=s):
                ins = fn(E)
                ins.then_inc(s, 1)
            self.streams[eng].append(run)
        for wres in writes:
            wres.writer = (s, v, eng)
            wres.readers = {}
        for r in reads:
            cur = r.readers.get(s)
            if cur is None or cur[0] < v:
                r.readers[s] = (v, eng)
        return (s, v)

    def barrier(self):
        marks = [(s, v) for s, v in self.semv.items() if v > 0]
        for e in ENGS:
            for s, v in marks:
                self._wait(e, s, v)

    def emit(self):
        with self.nc.Block() as block:
            @block.tensor
            def _(E):
                for fn in self.streams["tensor"]:
                    fn(E)

            @block.scalar
            def _(E):
                for fn in self.streams["scalar"]:
                    fn(E)

            @block.vector
            def _(E):
                for fn in self.streams["vector"]:
                    fn(E)

            @block.gpsimd
            def _(E):
                for fn in self.streams["gpsimd"]:
                    fn(E)

            @block.sync
            def _(E):
                for fn in self.streams["sync"]:
                    fn(E)



def build():
    nc = bass.Bass("TRN2", target_bir_lowering=False)

    DI = dict()

    def din(name, shape, dt=F32):
        DI[name] = nc.dram_tensor(name, shape, dt, kind="ExternalInput")
        return DI[name]

    din("xTc", [8, 128, KT * 512])
    din("xTqc", [4, 128, KT * 512])
    din("wqS", [8, 128, KT * 256])
    din("wkS", [16, 128, KT * 128]); din("wvS", [4, 128, KT * 512]); din("woT", [D, D])
    din("gwS", [128, KT * 16])
    din("w1S", [NE, IT, 128, KT * 128])
    din("w3S", [NE, IT, 128, KT * 128])
    din("w2S", [NE, 8, 128, IT * 256])
    din("sw1S", [2, IT, 128, KT * 128])
    din("sw3S", [2, IT, 128, KT * 128])
    din("sw2S", [2, 8, 128, IT * 256])
    din("ropeCk", [128, T]); din("ropeSk", [128, T])
    din("ropeCq", [128, TL]); din("ropeSq", [128, TL])
    din("maskT", [8, 128, 512], BF16)
    din("identity", [128, 128], BF16)

    skind = "ExternalOutput" if DEBUG_SCRATCH else "Internal"
    d_t1 = nc.dram_tensor("t1_hbm", [D, T], BF16, kind=skind)
    d_k = nc.dram_tensor("k_hbm", [D, T], BF16, kind=skind)
    d_v = nc.dram_tensor("v_hbm", [T // 128, 128, D], BF16, kind=skind)
    d_t1q = nc.dram_tensor("t1q_hbm", [D, TL], BF16, kind=skind)
    d_o = nc.dram_tensor("o_hbm", [D, TL], BF16, kind=skind)
    d_x2 = nc.dram_tensor("x2_hbm", [D, TL], F32, kind=skind)
    d_t2 = nc.dram_tensor("t2_hbm", [D, TL], BF16, kind=skind)
    d_cr = nc.dram_tensor("crows_hbm", [16, 1024], BF16, kind=skind)
    d_out = nc.dram_tensor("outT", [D, TL], F32, kind="ExternalOutput")

    with ExitStack() as ctx:
        b = B(nc, ctx)

        def sb(name, shape, dt):
            t = ctx.enter_context(nc.sbuf_tensor(name, shape, dt))
            return t, b.res(name)

        ones_c, r_ones_c = sb("ones_c", [128, 1], BF16)
        ones_r, r_ones_r = sb("ones_r", [1, 128], BF16)
        ones_rf, r_ones_rf = sb("ones_rf", [1, 128], F32)
        ident, r_ident = sb("identS", [128, 128], BF16)
        cst, r_cst = sb("cst", [128, 4], F32)
        srow, r_srow = sb("srow", [1, 2560], F32)
        ln_sq, r_ln_sq = sb("ln_sq", [128, KT * 512], BF16)
        brow_g, r_brow_g = sb("brow_g", [1, 1024], BF16)
        ln_rb, r_ln_rb = sb("ln_rb", [128, 512], F32)
        ln_mb, r_ln_mb = sb("ln_mb", [128, 512], F32)
        ln_rb16, r_ln_rb16 = sb("ln_rb16", [128, 512], BF16)
        ln_mb16, r_ln_mb16 = sb("ln_mb16", [128, 512], BF16)

        rD = {n: b.res("dram_" + n) for n in
              ("t1", "k", "v", "t1q", "o", "x2", "t2", "cr", "out")}
        dram = dict(t1=d_t1, k=d_k, v=d_v, t1q=d_t1q, o=d_o, x2=d_x2,
                    t2=d_t2, cr=d_cr, out=d_out)

        b.op("gpsimd", lambda E: E.memset(ones_c[:, :], 1.0), writes=[r_ones_c])
        b.op("gpsimd", lambda E: E.memset(ones_r[:, :], 1.0), writes=[r_ones_r])
        b.op("gpsimd", lambda E: E.memset(ones_rf[:, :], 1.0), writes=[r_ones_rf])
        b.op("vector", lambda E: E.memset(cst[:, 0:1], EXP_BIAS), writes=[r_cst])
        b.op("vector", lambda E: E.memset(cst[:, 1:2], EPS), writes=[r_cst])
        b.op("vector", lambda E: E.memset(cst[:, 2:3], 0.0), writes=[r_cst])
        b.op("sync", lambda E: E.dma_start(ident[:, :], DI["identity"].ap()),
             writes=[r_ident], dma=True)

        G = dict(nc=nc, b=b, ctx=ctx, DI=DI, dram=dram, rD=rD, sb=sb,
                 ones_c=ones_c, r_ones_c=r_ones_c, ones_r=ones_r,
                 r_ones_r=r_ones_r, ones_rf=ones_rf, r_ones_rf=r_ones_rf,
                 ident=ident, r_ident=r_ident, cst=cst, r_cst=r_cst,
                 srow=srow, r_srow=r_srow,
                 ln_sq=ln_sq, r_ln_sq=r_ln_sq, brow_g=brow_g, r_brow_g=r_brow_g,
                 ln_rb=ln_rb, r_ln_rb=r_ln_rb, ln_mb=ln_mb, r_ln_mb=r_ln_mb,
                 ln_rb16=ln_rb16, r_ln_rb16=r_ln_rb16,
                 ln_mb16=ln_mb16, r_ln_mb16=r_ln_mb16)

        if "A" in PHASES:
            phase_A(G)
        b.barrier()
        if "B" in PHASES:
            phase_B(G)
        b.barrier()
        if "C" in PHASES:
            phase_C(G)

        for s, v in list(b.semv.items()):
            if v > 0:
                b._wait("sync", s, v)

        b.emit()
    return nc



def ln_stats(G, xb, r_xb, ncols, bS, rbS, bQ, rbQ, nkt=KT):
    b = G["b"]
    srow, r_srow, cst, r_cst = G["srow"], G["r_srow"], G["cst"], G["r_cst"]
    ones_c, r_ones_c = G["ones_c"], G["r_ones_c"]
    ln_sq, r_ln_sq = G["ln_sq"], G["r_ln_sq"]

    b.op("vector", lambda E: E.tensor_tensor(
        ln_sq[:, 0:nkt * ncols], xb[:, 0:nkt * ncols], xb[:, 0:nkt * ncols],
        op=ALU.mult), reads=[r_xb], writes=[r_ln_sq])

    def mm_s(E):
        for k in range(nkt):
            ins = E.matmul(bS[0:1, 0:ncols], ones_c[:, :],
                           xb[:, k * ncols:(k + 1) * ncols],
                           start=(k == 0), stop=(k == nkt - 1))
        return ins
    b.op("tensor", mm_s, reads=[r_ones_c, r_xb], writes=[rbS])

    def mm_q(E):
        for k in range(nkt):
            ins = E.matmul(bQ[0:1, 0:ncols], ones_c[:, :],
                           ln_sq[:, k * ncols:(k + 1) * ncols],
                           start=(k == 0), stop=(k == nkt - 1))
        return ins
    b.op("tensor", mm_q, reads=[r_ones_c, r_ln_sq], writes=[rbQ])

    b.op("vector", lambda E: E.tensor_scalar(
        out=srow[0:1, 1024:1024 + ncols], in0=bS[0:1, 0:ncols],
        scalar1=1.0 / D, scalar2=None, op0=ALU.mult),
        reads=[rbS], writes=[r_srow])
    b.op("vector", lambda E: E.tensor_scalar(
        out=srow[0:1, 0:ncols], in0=bQ[0:1, 0:ncols],
        scalar1=1.0 / D, scalar2=None, op0=ALU.mult),
        reads=[rbQ], writes=[r_srow])
    b.op("vector", lambda E: E.tensor_tensor(
        srow[0:1, 512:512 + ncols], srow[0:1, 1024:1024 + ncols],
        srow[0:1, 1024:1024 + ncols], op=ALU.mult),
        reads=[r_srow], writes=[r_srow])
    b.op("vector", lambda E: E.tensor_tensor(
        srow[0:1, 0:ncols], srow[0:1, 0:ncols], srow[0:1, 512:512 + ncols],
        op=ALU.subtract), reads=[r_srow], writes=[r_srow])
    b.op("scalar", lambda E: E.activation(
        srow[0:1, 512:512 + ncols], srow[0:1, 0:ncols], AF.Ln,
        bias=cst[0:1, 1:2], scale=1.0), reads=[r_srow, r_cst], writes=[r_srow])
    b.op("scalar", lambda E: E.activation(
        srow[0:1, 1536:1536 + ncols], srow[0:1, 512:512 + ncols], AF.Exp,
        bias=cst[0:1, 2:3], scale=-0.5), reads=[r_srow, r_cst], writes=[r_srow])
    b.op("vector", lambda E: E.tensor_tensor(
        srow[0:1, 2048:2048 + ncols], srow[0:1, 1024:1024 + ncols],
        srow[0:1, 1536:1536 + ncols], op=ALU.mult),
        reads=[r_srow], writes=[r_srow])


def ln_bcast(G, ncols, bS, rbS, bQ, rbQ, f32=False):
    b = G["b"]
    srow, r_srow = G["srow"], G["r_srow"]
    if f32:
        ones, r_ones = G["ones_rf"], G["r_ones_rf"]
        rb, r_rb, mb, r_mb = G["ln_rb"], G["r_ln_rb"], G["ln_mb"], G["r_ln_mb"]
        def mmb(E):
            E.matmul(bS[:, 0:ncols], ones[0:1, :], srow[0:1, 1536:1536 + ncols],
                     start=True, stop=True)
            return E.matmul(bQ[:, 0:ncols], ones[0:1, :],
                            srow[0:1, 2048:2048 + ncols], start=True, stop=True)
        b.op("tensor", mmb, reads=[r_ones, r_srow], writes=[rbS, rbQ])
    else:
        ones, r_ones = G["ones_r"], G["r_ones_r"]
        rb, r_rb, mb, r_mb = (G["ln_rb16"], G["r_ln_rb16"],
                              G["ln_mb16"], G["r_ln_mb16"])
        brow, r_brow = G["brow_g"], G["r_brow_g"]
        b.op("scalar", lambda E: E.copy(
            brow[0:1, 0:ncols], srow[0:1, 1536:1536 + ncols]),
            reads=[r_srow], writes=[r_brow])
        b.op("scalar", lambda E: E.copy(
            brow[0:1, 512:512 + ncols], srow[0:1, 2048:2048 + ncols]),
            reads=[r_srow], writes=[r_brow])
        def mmb(E):
            E.matmul(bS[:, 0:ncols], ones[0:1, :], brow[0:1, 0:ncols],
                     start=True, stop=True)
            return E.matmul(bQ[:, 0:ncols], ones[0:1, :],
                            brow[0:1, 512:512 + ncols], start=True, stop=True)
        b.op("tensor", mmb, reads=[r_ones, r_brow], writes=[rbS, rbQ])
    b.op("vector", lambda E: E.tensor_copy(rb[:, 0:ncols], bS[:, 0:ncols]),
         reads=[rbS], writes=[r_rb])
    b.op("vector", lambda E: E.tensor_copy(mb[:, 0:ncols], bQ[:, 0:ncols]),
         reads=[rbQ], writes=[r_mb])
    return rb, r_rb, mb, r_mb


def ln_apply_bf16(G, xb, r_xb, out, r_out, out_off, ncols, nkt=KT):
    b = G["b"]
    ln_tmp, r_ln_tmp = G["ln_sq"], G["r_ln_sq"]
    rb, r_rb = G["ln_rb16"], G["r_ln_rb16"]
    mb, r_mb = G["ln_mb16"], G["r_ln_mb16"]
    b.op("vector", lambda E: E.tensor_tensor(
        ln_tmp[:, 0:nkt * ncols].rearrange("p (k n) -> p k n", k=nkt),
        xb[:, 0:nkt * ncols].rearrange("p (k n) -> p k n", k=nkt),
        rb[:, None, 0:ncols].broadcast_to([128, nkt, ncols]), op=ALU.mult),
        reads=[r_xb, r_rb], writes=[r_ln_tmp])
    b.op("vector", lambda E: E.tensor_tensor(
        out[:, out_off:out_off + nkt * ncols].rearrange("p (k n) -> p k n", k=nkt),
        ln_tmp[:, 0:nkt * ncols].rearrange("p (k n) -> p k n", k=nkt),
        mb[:, None, 0:ncols].broadcast_to([128, nkt, ncols]), op=ALU.subtract),
        reads=[r_ln_tmp, r_mb], writes=[r_out])



def phase_A(G):
    nc, b, ctx, DI = G["nc"], G["b"], G["ctx"], G["DI"]
    dram, rD = G["dram"], G["rD"]
    sb = G["sb"]

    with ExitStack() as pc:
        def psb(name, shape, dt):
            t = pc.enter_context(nc.sbuf_tensor(name, shape, dt))
            return t, b.res(name)

        def pps(name, shape):
            t = pc.enter_context(nc.psum_tensor(name, shape, F32))
            return t, b.res(name, xread=True)

        bS, rbS = pps("pa_s", [128, 512])
        bQ, rbQ = pps("pa_q", [128, 512])
        bK = [pps("pa_k0", [128, 512]), pps("pa_k1", [128, 512])]

        xa = [psb("xa0", [128, KT * 512], BF16), psb("xa1", [128, KT * 512], BF16)]
        t1a = [psb("t1a0", [128, KT * 512], BF16), psb("t1a1", [128, KT * 512], BF16)]
        kout = [psb("ko0", [128, KT * 512], BF16)]
        wkb = [psb(f"wkb{i}", [128, KT * 128], BF16) for i in range(3)]
        wvb = [psb(f"wvb{i}", [128, KT * 512], BF16) for i in range(2)]
        ropeC = [psb("ropeC0", [128, 512], F32), psb("ropeC1", [128, 512], F32)]
        ropeS = [psb("ropeS0", [128, 512], F32), psb("ropeS1", [128, 512], F32)]
        krot = [psb("krot0", [128, 512], F32), psb("krot1", [128, 512], F32)]
        ta = [psb("ta0", [128, 512], F32), psb("ta1", [128, 512], F32)]
        tb = [psb("tb0", [128, 512], F32), psb("tb1", [128, 512], F32)]


        def a1_ln_pre(ch):
            cols = slice(512 * ch, 512 * ch + 512)
            xat, r_xat = xa[ch % 2]
            rC, r_rC = ropeC[ch % 2]
            rS, r_rS = ropeS[ch % 2]
            b.op("sync", lambda E, rC=rC, cols=cols: E.dma_start(
                rC[:, :], DI["ropeCk"].ap()[:, cols]), writes=[r_rC], dma=True)
            b.op("sync", lambda E, rS=rS, cols=cols: E.dma_start(
                rS[:, :], DI["ropeSk"].ap()[:, cols]), writes=[r_rS], dma=True)
            b.op("gpsimd", lambda E, xat=xat, ch=ch: E.dma_start(
                xat[:, :], DI["xTc"].ap()[ch]), writes=[r_xat], dma=True)
            ln_stats(G, xat, r_xat, 512, bS, rbS, bQ, rbQ)

        def a1_ln_post(ch):
            cols = slice(512 * ch, 512 * ch + 512)
            xat, r_xat = xa[ch % 2]
            t1t, r_t1t = t1a[ch % 2]
            ln_bcast(G, 512, bS, rbS, bQ, rbQ, f32=False)
            ln_apply_bf16(G, xat, r_xat, t1t, r_t1t, 0, 512)
            b.op("scalar", lambda E, t1t=t1t, cols=cols: E.dma_start(
                dram["t1"].ap()[:, cols].rearrange("(k p) n -> p k n", p=128),
                t1t[:, :].rearrange("p (k n) -> p k n", k=KT)),
                reads=[r_t1t], writes=[rD["t1"]], dma=True)

        def a1_k(ch, mid=None):
            cols = slice(512 * ch, 512 * ch + 512)
            t1t, r_t1t = t1a[ch % 2]
            kot, r_kot = kout[0]
            rC, r_rC = ropeC[ch % 2]
            rS, r_rS = ropeS[ch % 2]
            for h in range(KT):
                if h == 4 and mid is not None:
                    mid()
                pk, rpk = bK[h % 2]
                wkt, r_wkt = wkb[(ch * KT + h) % 3]
                b.op("gpsimd", lambda E, wkt=wkt, h=h: E.dma_start(
                    wkt[:, :], DI["wkS"].ap()[h]), writes=[r_wkt], dma=True)
                def mmk(E, h=h, t1t=t1t, pk=pk, wkt=wkt):
                    for k in range(KT):
                        ins = E.matmul(pk[:, :],
                                       wkt[:, k * 128:(k + 1) * 128],
                                       t1t[:, k * 512:(k + 1) * 512],
                                       start=(k == 0), stop=(k == KT - 1))
                    return ins
                b.op("tensor", mmk, reads=[r_wkt, r_t1t], writes=[rpk])
                kr, r_kr = krot[h % 2]
                tat, r_tat = ta[h % 2]
                tbt, r_tbt = tb[h % 2]
                def rot(E, pk=pk, kr=kr):
                    E.copy(kr[0:64, :], pk[64:128, 0:512])
                    return E.copy(kr[64:128, :], pk[0:64, 0:512])
                b.op("scalar", rot, reads=[rpk], writes=[r_kr])
                b.op("vector", lambda E, pk=pk, tat=tat, rC=rC: E.tensor_tensor(
                    tat[:, :], rC[:, :], pk[:, 0:512], op=ALU.mult),
                    reads=[rpk, r_rC], writes=[r_tat])
                b.op("gpsimd", lambda E, kr=kr, tbt=tbt, rS=rS: E.tensor_tensor(
                    tbt[:, :], kr[:, :], rS[:, :], op=ALU.mult),
                    reads=[r_kr, r_rS], writes=[r_tbt])
                b.op("vector", lambda E, tat=tat, tbt=tbt, kot=kot, h=h:
                     E.tensor_tensor(kot[:, h * 512:(h + 1) * 512], tat[:, :],
                                     tbt[:, :], op=ALU.add),
                     reads=[r_tat, r_tbt], writes=[r_kot])
            b.op("scalar", lambda E, kot=kot, cols=cols: E.dma_start(
                dram["k"].ap()[:, cols].rearrange("(k p) n -> p k n", p=128),
                kot[:, :].rearrange("p (k n) -> p k n", k=KT)),
                reads=[r_kot], writes=[rD["k"]], dma=True)

        a1_ln_pre(0)
        a1_ln_post(0)
        for ch in range(1, A_CHUNKS + 1):
            if ch < A_CHUNKS:
                a1_ln_pre(ch)
                a1_k(ch - 1, mid=lambda ch=ch: a1_ln_post(ch))
            else:
                a1_k(ch - 1)

        def b0_chunk(qc):
            cols = slice(512 * qc, 512 * qc + 512)
            xqt, r_xqt = xa[qc % 2]
            tqt, r_tqt = kout[0]
            b.op("gpsimd", lambda E, xqt=xqt, qc=qc: E.dma_start(
                xqt[:, :], DI["xTqc"].ap()[qc]), writes=[r_xqt], dma=True)
            ln_stats(G, xqt, r_xqt, 512, bS, rbS, bQ, rbQ)
            ln_bcast(G, 512, bS, rbS, bQ, rbQ, f32=False)
            ln_apply_bf16(G, xqt, r_xqt, tqt, r_tqt, 0, 512)
            b.op("scalar", lambda E, tqt=tqt, cols=cols: E.dma_start(
                dram["t1q"].ap()[:, cols].rearrange("(k p) n -> p k n", p=128),
                tqt[:, :].rearrange("p (k n) -> p k n", k=KT)),
                reads=[r_tqt], writes=[rD["t1q"]], dma=True)

        vout = [psb(f"vo{i}", [128, 2048], BF16) for i in range(4)]
        for ch in range(A_CHUNKS):
            cols = slice(512 * ch, 512 * ch + 512)
            t1t, r_t1t = t1a[ch % 2]
            b.op("sync", lambda E, t1t=t1t, cols=cols: E.dma_start(
                t1t[:, :].rearrange("p (k n) -> p k n", k=KT),
                dram["t1"].ap()[:, cols].rearrange("(k p) n -> p k n", p=128)),
                reads=[rD["t1"]], writes=[r_t1t], dma=True)
            for dc in range(4):
                wvt, r_wvt = wvb[(ch * 4 + dc) % 2]
                b.op("gpsimd", lambda E, wvt=wvt, dc=dc: E.dma_start(
                    wvt[:, :], DI["wvS"].ap()[dc]), writes=[r_wvt], dma=True)
                for tt in range(4):
                    vo, r_vo = vout[tt]
                    pk, rpk = bK[(dc * 4 + tt) % 2]
                    def mmv(E, t1t=t1t, tt=tt, dc=dc, pk=pk, wvt=wvt):
                        for k in range(KT):
                            ins = E.matmul(
                                pk[:, :],
                                t1t[:, k * 512 + tt * 128:k * 512 + tt * 128 + 128],
                                wvt[:, k * 512:(k + 1) * 512],
                                start=(k == 0), stop=(k == KT - 1))
                        return ins
                    b.op("tensor", mmv, reads=[r_wvt, r_t1t], writes=[rpk])
                    b.op("vector", lambda E, pk=pk, vo=vo, dc=dc: E.tensor_copy(
                        vo[:, dc * 512:(dc + 1) * 512], pk[:, :]),
                        reads=[rpk], writes=[r_vo])
            for tt in range(4):
                vo, r_vo = vout[tt]
                b.op("scalar", lambda E, vo=vo, ch=ch, tt=tt: E.dma_start(
                    dram["v"].ap()[ch * 4 + tt], vo[:, :]),
                    reads=[r_vo], writes=[rD["v"]], dma=True)
            if ch % 2 == 1 and ch // 2 < 4:
                b0_chunk(ch // 2)


def phase_B(G):
    nc, b, DI = G["nc"], G["b"], G["DI"]
    dram, rD = G["dram"], G["rD"]


    with ExitStack() as pc:
        def psb(name, shape, dt):
            t = pc.enter_context(nc.sbuf_tensor(name, shape, dt))
            return t, b.res(name)

        def pps(name, shape, dt=F32):
            t = pc.enter_context(nc.psum_tensor(name, shape, dt))
            return t, b.res(name, xread=True)

        pgA, rpgA = pps("pb1_ga", [128, 1536])
        pgB, rpgB = pps("pb1_gb", [128, 1536])
        pSum, rpSum = pps("pb1_sum", [128, 512])
        pO, rpO = pps("pb1_o", [128, 512])

        cst, r_cst = G["cst"], G["r_cst"]
        srow, r_srow = G["srow"], G["r_srow"]
        ident, r_ident = G["ident"], G["r_ident"]
        ones_c, r_ones_c = G["ones_c"], G["r_ones_c"]
        ones_rf, r_ones_rf = G["ones_rf"], G["r_ones_rf"]

        mask, r_mask = psb("mask_sb", [128, 8 * 512], BF16)
        ropeCq, r_ropeCq = psb("ropeCq_sb", [128, TL], F32)
        ropeSq, r_ropeSq = psb("ropeSq_sb", [128, TL], F32)
        b.op("sync", lambda E: E.dma_start(
            mask[:, :].rearrange("p (j n) -> p j n", j=8),
            DI["maskT"].ap().rearrange("j p n -> p j n")),
            writes=[r_mask], dma=True)
        b.op("sync", lambda E: E.dma_start(ropeCq[:, :], DI["ropeCq"].ap()),
             writes=[r_ropeCq], dma=True)
        b.op("sync", lambda E: E.dma_start(ropeSq[:, :], DI["ropeSq"].ap()),
             writes=[r_ropeSq], dma=True)

        wq = [psb("wq0", [128, KT * 256], BF16), psb("wq1", [128, KT * 256], BF16)]
        K2a = [psb("K2a0", [128, 2 * 1024], BF16), psb("K2a1", [128, 2 * 1024], BF16)]
        K2b = [psb("K2b0", [128, 2 * 3072], BF16), psb("K2b1", [128, 2 * 3072], BF16)]
        V2a = [psb("V2a0", [128, 2 * 1024], BF16), psb("V2a1", [128, 2 * 1024], BF16)]
        V2b = [psb("V2b0", [128, 2 * 3072], BF16), psb("V2b1", [128, 2 * 3072], BF16)]

        def kv_slice(slot, which, hh, kt):
            if kt < 8:
                t_, r_ = (K2a if which == "k" else V2a)[slot]
                return t_, r_, hh * 1024 + kt * 128
            t_, r_ = (K2b if which == "k" else V2b)[slot]
            return t_, r_, hh * 3072 + (kt - 8) * 128
        t1qc = [psb("t1qc0", [128, KT * 512], BF16),
                psb("t1qc1", [128, KT * 512], BF16)]
        qsb = [psb("qsb0", [128, 512], BF16), psb("qsb1", [128, 512], BF16)]
        qrot = psb("qrot", [128, 512], F32)
        qta = psb("qta", [128, 512], F32)
        qtb = psb("qtb", [128, 512], F32)
        PT = [psb(f"pt{i}", [128, 1536], BF16) for i in range(3)]
        rrow = psb("rrow", [1, 512], F32)
        rbo = psb("rbo", [128, 512], F32)
        Opair = psb("Opair", [128, 2 * TL], BF16)

        def load_pair(hp, slot):
            wqt, r_wqt = wq[slot]
            b.op("gpsimd", lambda E, wqt=wqt, hp=hp: E.dma_start(
                wqt[:, :], DI["wqS"].ap()[hp]), writes=[r_wqt], dma=True)
            for hh in range(2):
                h = 2 * hp + hh
                ka, r_ka = K2a[slot]
                kb, r_kb = K2b[slot]
                va, r_va = V2a[slot]
                vb, r_vb = V2b[slot]
                b.op("sync", lambda E, ka=ka, h=h, hh=hh: E.dma_start(
                    ka[:, hh * 1024:(hh + 1) * 1024],
                    dram["k"].ap()[128 * h:128 * h + 128, 0:1024]),
                    reads=[rD["k"]], writes=[r_ka], dma=True)
                b.op("sync", lambda E, va=va, h=h, hh=hh: E.dma_start(
                    va[:, hh * 1024:(hh + 1) * 1024].rearrange("p (kt d) -> p kt d", kt=8),
                    dram["v"].ap()[0:8, :, 128 * h:128 * h + 128]
                    .rearrange("kt p d -> p kt d"),),
                    reads=[rD["v"]], writes=[r_va], dma=True)
            for hh in range(2):
                h = 2 * hp + hh
                kb, r_kb = K2b[slot]
                vb, r_vb = V2b[slot]
                b.op("sync", lambda E, kb=kb, h=h, hh=hh: E.dma_start(
                    kb[:, hh * 3072:(hh + 1) * 3072],
                    dram["k"].ap()[128 * h:128 * h + 128, 1024:4096]),
                    reads=[rD["k"]], writes=[r_kb], dma=True)
                b.op("sync", lambda E, vb=vb, h=h, hh=hh: E.dma_start(
                    vb[:, hh * 3072:(hh + 1) * 3072].rearrange("p (kt d) -> p kt d", kt=24),
                    dram["v"].ap()[8:32, :, 128 * h:128 * h + 128]
                    .rearrange("kt p d -> p kt d"),),
                    reads=[rD["v"]], writes=[r_vb], dma=True)

        load_pair(0, 0)
        for hp in range(8):
            slot = hp % 2
            if hp + 1 < 8:
                load_pair(hp + 1, (hp + 1) % 2)
            wqt, r_wqt = wq[slot]
            for qc in range(4):
                qcols = slice(512 * qc, 512 * qc + 512)
                tq, r_tq = t1qc[qc % 2]
                b.op("scalar", lambda E, tq=tq, qcols=qcols: E.dma_start(
                    tq[:, :].rearrange("p (k n) -> p k n", k=KT),
                    dram["t1q"].ap()[:, qcols].rearrange("(k p) n -> p k n", p=128)),
                    reads=[rD["t1q"]], writes=[r_tq], dma=True)
                nkt = 8 * qc + 8
                for hh in range(2):
                    def mmq(E, wqt=wqt, tq=tq, hh=hh):
                        for k in range(KT):
                            ins = E.matmul(
                                pgA[:, 0:512],
                                wqt[:, k * 256 + hh * 128:k * 256 + hh * 128 + 128],
                                tq[:, k * 512:(k + 1) * 512],
                                start=(k == 0), stop=(k == KT - 1))
                        return ins
                    b.op("tensor", mmq, reads=[r_wqt, r_tq], writes=[rpgA])
                    qr, r_qr = qrot
                    qa, r_qa = qta
                    qb, r_qb = qtb
                    def qrots(E, qr=qr):
                        E.copy(qr[0:64, :], pgA[64:128, 0:512])
                        return E.copy(qr[64:128, :], pgA[0:64, 0:512])
                    b.op("scalar", qrots, reads=[rpgA], writes=[r_qr])
                    b.op("vector", lambda E, qa=qa, qcols=qcols: E.tensor_tensor(
                        qa[:, :], ropeCq[:, qcols], pgA[:, 0:512], op=ALU.mult),
                        reads=[rpgA, r_ropeCq], writes=[r_qa])
                    b.op("gpsimd", lambda E, qr=qr, qb=qb, qcols=qcols:
                         E.tensor_tensor(qb[:, :], qr[:, :], ropeSq[:, qcols],
                                         op=ALU.mult),
                         reads=[r_qr, r_ropeSq], writes=[r_qb])
                    qs, r_qs = qsb[hh]
                    b.op("vector", lambda E, qa=qa, qb=qb, qs=qs: E.tensor_tensor(
                        qs[:, :], qa[:, :], qb[:, :], op=ALU.add),
                        reads=[r_qa, r_qb], writes=[r_qs])
                for hh in range(2):
                    qs, r_qs = qsb[hh]
                    groups = [list(range(g, min(g + 3, nkt)))
                              for g in range(0, nkt, 3)]
                    pend = None

                    def sums_av(tiles, slot_idx, first, last):
                        pts, r_pts = PT[slot_idx]
                        vtensors = [kv_slice(slot, "v", hh, kt) for kt in tiles]
                        def mm2(E, tiles=tiles, pts=pts, first=first, last=last,
                                vtensors=vtensors):
                            for i, kt in enumerate(tiles):
                                E.matmul(pSum[0:1, :], ones_c[:, :],
                                         pts[:, i * 512:(i + 1) * 512],
                                         start=(first and i == 0),
                                         stop=(last and i == len(tiles) - 1))
                                v_t, _, voff = vtensors[i]
                                ins = E.matmul(
                                    pO[:, :], v_t[:, voff:voff + 128],
                                    pts[:, i * 512:(i + 1) * 512],
                                    start=(first and i == 0),
                                    stop=(last and i == len(tiles) - 1))
                            return ins
                        b.op("tensor", mm2,
                             reads=[r_ones_c, PT[slot_idx][1]] + [vv[1] for vv in vtensors],
                             writes=[rpSum, rpO])

                    for gi, tiles in enumerate(groups):
                        grp, rgrp = (pgA, rpgA) if gi % 2 == 0 else (pgB, rpgB)
                        wid = len(tiles) * 512
                        ktensors = [kv_slice(slot, "k", hh, kt) for kt in tiles]
                        def mms(E, tiles=tiles, grp=grp, qs=qs, qc=qc,
                                ktensors=ktensors):
                            for i, kt in enumerate(tiles):
                                sl = grp[:, i * 512:(i + 1) * 512]
                                kt_t, _, koff = ktensors[i]
                                if kt >= 8 * qc:
                                    E.matmul(sl, kt_t[:, koff:koff + 128],
                                             qs[:, :], start=True, stop=False)
                                    ins = E.matmul(
                                        sl, ident[:, :],
                                        mask[:, (kt - 8 * qc) * 512:
                                             (kt - 8 * qc) * 512 + 512],
                                        start=False, stop=True)
                                else:
                                    ins = E.matmul(
                                        sl, kt_t[:, koff:koff + 128],
                                        qs[:, :], start=True, stop=True)
                            return ins
                        b.op("tensor", mms,
                             reads=[r_qs, r_ident, r_mask] + [kk[1] for kk in ktensors],
                             writes=[rgrp])
                        if gi == 0 and G.get("b1_pending") is not None:
                            G["b1_pending"]()
                            G["b1_pending"] = None
                        pts, r_pts = PT[gi % 3]
                        b.op("scalar", lambda E, grp=grp, pts=pts, wid=wid:
                             E.activation(pts[:, 0:wid], grp[:, 0:wid], AF.Exp,
                                          bias=cst[:, 0:1], scale=ISCALE),
                             reads=[rgrp, r_cst], writes=[r_pts])
                        if pend is not None:
                            sums_av(*pend)
                        pend = (tiles, gi % 3,
                                gi == 0, gi == len(groups) - 1)
                    if pend is not None:
                        sums_av(*pend)

                    def mknorm(hh=hh, qc=qc):
                        def donorm():
                            rr, r_rr = rrow
                            b.op("vector", lambda E, rr=rr: E.reciprocal(
                                rr[0:1, :], pSum[0:1, :]),
                                reads=[rpSum], writes=[r_rr])
                            def mmbc(E, rr=rr):
                                return E.matmul(pSum[:, :], ones_rf[0:1, :],
                                                rr[0:1, :], start=True, stop=True)
                            b.op("tensor", mmbc, reads=[r_ones_rf, r_rr],
                                 writes=[rpSum])
                            rb, r_rb = rbo
                            b.op("vector", lambda E, rb=rb: E.tensor_copy(
                                rb[:, :], pSum[:, :]), reads=[rpSum], writes=[r_rb])
                            Op, r_Op = Opair
                            b.op("vector", lambda E, rb=rb, Op=Op, hh=hh, qc=qc:
                                 E.tensor_tensor(
                                     Op[:, hh * TL + qc * 512:hh * TL + qc * 512 + 512],
                                     rb[:, :], pO[:, :], op=ALU.mult),
                                 reads=[rpO, r_rb], writes=[r_Op])
                        return donorm
                    G["b1_pending"] = mknorm()
            if G.get("b1_pending") is not None:
                G["b1_pending"]()
                G["b1_pending"] = None
            Op, r_Op = Opair
            for hh in range(2):
                h = 2 * hp + hh
                b.op("scalar", lambda E, Op=Op, h=h, hh=hh: E.dma_start(
                    dram["o"].ap()[128 * h:128 * h + 128, :],
                    Op[:, hh * TL:(hh + 1) * TL]),
                    reads=[r_Op], writes=[rD["o"]], dma=True)

    b.barrier()

    with ExitStack() as pc:
        def psb(name, shape, dt):
            t = pc.enter_context(nc.sbuf_tensor(name, shape, dt))
            return t, b.res(name)

        def pps(name, shape, dt=F32):
            t = pc.enter_context(nc.psum_tensor(name, shape, dt))
            return t, b.res(name, xread=True)

        bS, rbS = pps("pb2_s", [128, 512])
        bQ, rbQ = pps("pb2_q", [128, 512])
        bW = [pps("pb2_w0", [128, 256]), pps("pb2_w1", [128, 256])]

        wo, r_wo = psb("wo_sb", [128, KT * 2048], BF16)
        b.op("gpsimd", lambda E: E.dma_start(
            wo[:, :].rearrange("p (k e) -> p k e", k=KT),
            DI["woT"].ap().rearrange("(k p) e -> p k e", p=128)),
            writes=[r_wo], dma=True)
        o2, r_o2 = psb("o2_0", [128, KT * 256], BF16)
        xqf, r_xqf = psb("xqf0", [128, KT * 256], F32)
        attf = [psb("attf0", [128, KT * 256], F32), psb("attf1", [128, KT * 256], F32)]
        attb = [psb("attb0", [128, KT * 256], BF16), psb("attb1", [128, KT * 256], BF16)]
        x2f, r_x2f = psb("x2f", [128, KT * 256], F32)
        x2b, r_x2b = psb("x2b16", [128, KT * 256], BF16)
        t2o, r_t2o = psb("t2o", [128, KT * 256], BF16)

        def b2_wo(u, mid=None):
            cols = slice(256 * u, 256 * u + 256)
            b.op("sync", lambda E, cols=cols: E.dma_start(
                o2[:, :].rearrange("p (k n) -> p k n", k=KT),
                dram["o"].ap()[:, cols].rearrange("(k p) n -> p k n", p=128)),
                reads=[rD["o"]], writes=[r_o2], dma=True)
            atf, r_atf = attf[u % 2]
            atb, r_atb = attb[u % 2]
            for dt in range(KT):
                if mid is not None and dt in mid:
                    mid[dt]()
                pw, rpw = bW[dt % 2]
                def mmo(E, dt=dt, pw=pw):
                    for k in range(KT):
                        ins = E.matmul(
                            pw[:, :],
                            wo[:, k * 2048 + dt * 128:k * 2048 + dt * 128 + 128],
                            o2[:, k * 256:(k + 1) * 256],
                            start=(k == 0), stop=(k == KT - 1))
                    return ins
                b.op("tensor", mmo, reads=[r_wo, r_o2], writes=[rpw])
                b.op("scalar", lambda E, pw=pw, atf=atf, dt=dt: E.copy(
                    atf[:, dt * 256:(dt + 1) * 256], pw[:, :]),
                    reads=[rpw], writes=[r_atf])
                b.op("vector", lambda E, pw=pw, atb=atb, dt=dt: E.tensor_copy(
                    atb[:, dt * 256:(dt + 1) * 256], pw[:, :]),
                    reads=[rpw], writes=[r_atb])

        def b2_ln_a(u):
            atb, r_atb = attb[u % 2]
            b.op("sync", lambda E, u=u: E.dma_start(
                xqf[:, :].rearrange("p (k n) -> p k n", k=KT),
                DI["xTqc"].ap()[u // 2]
                .rearrange("p (k n) -> p k n", k=KT)[:, :, (u % 2) * 256:(u % 2) * 256 + 256]),
                writes=[r_xqf], dma=True)
            ln_stats(G, atb, r_atb, 256, bS, rbS, bQ, rbQ)

        def b2_ln_b(u):
            cols = slice(256 * u, 256 * u + 256)
            atf, r_atf = attf[u % 2]
            rb, r_rb, mb, r_mb = ln_bcast(G, 256, bS, rbS, bQ, rbQ, f32=True)
            b.op("vector", lambda E, atf=atf, rb=rb: E.tensor_tensor(
                x2f[:, :].rearrange("p (k n) -> p k n", k=KT),
                atf[:, :].rearrange("p (k n) -> p k n", k=KT),
                rb[:, None, 0:256].broadcast_to([128, KT, 256]), op=ALU.mult),
                reads=[r_atf, r_rb], writes=[r_x2f])
            b.op("vector", lambda E, mb=mb: E.tensor_tensor(
                x2f[:, :].rearrange("p (k n) -> p k n", k=KT),
                x2f[:, :].rearrange("p (k n) -> p k n", k=KT),
                mb[:, None, 0:256].broadcast_to([128, KT, 256]), op=ALU.subtract),
                reads=[r_x2f, r_mb], writes=[r_x2f])
            b.op("vector", lambda E: E.tensor_tensor(
                x2f[:, :], x2f[:, :], xqf[:, :], op=ALU.add),
                reads=[r_x2f, r_xqf], writes=[r_x2f])
            b.op("scalar", lambda E, cols=cols: E.dma_start(
                dram["x2"].ap()[:, cols].rearrange("(k p) n -> p k n", p=128),
                x2f[:, :].rearrange("p (k n) -> p k n", k=KT)),
                reads=[r_x2f], writes=[rD["x2"]], dma=True)

        def b2_ln_c(u):
            b.op("scalar", lambda E: E.copy(x2b[:, :], x2f[:, :]),
                 reads=[r_x2f], writes=[r_x2b])
            ln_stats(G, x2b, r_x2b, 256, bS, rbS, bQ, rbQ)

        def b2_ln_d(u):
            cols = slice(256 * u, 256 * u + 256)
            ln_bcast(G, 256, bS, rbS, bQ, rbQ, f32=False)
            ln_apply_bf16(G, x2b, r_x2b, t2o, r_t2o, 0, 256)
            b.op("scalar", lambda E, cols=cols: E.dma_start(
                dram["t2"].ap()[:, cols].rearrange("(k p) n -> p k n", p=128),
                t2o[:, :].rearrange("p (k n) -> p k n", k=KT)),
                reads=[r_t2o], writes=[rD["t2"]], dma=True)

        def mids(u):
            return {2: lambda: b2_ln_a(u - 1), 6: lambda: b2_ln_b(u - 1),
                    9: lambda: b2_ln_c(u - 1), 13: lambda: b2_ln_d(u - 1)}

        for u in range(9):
            if u < 8:
                b2_wo(u, mid=mids(u) if u >= 1 else None)
            else:
                b2_ln_a(7); b2_ln_b(7); b2_ln_c(7); b2_ln_d(7)



def phase_C(G):
    nc, b, DI = G["nc"], G["b"], G["DI"]
    dram, rD = G["dram"], G["rD"]
    cst, r_cst = G["cst"], G["r_cst"]
    ident, r_ident = G["ident"], G["r_ident"]
    ones_c, r_ones_c = G["ones_c"], G["r_ones_c"]
    ones_r, r_ones_r = G["ones_r"], G["r_ones_r"]

    with ExitStack() as pc:
        def psb(name, shape, dt):
            t = pc.enter_context(nc.sbuf_tensor(name, shape, dt))
            return t, b.res(name)

        def pps(name, shape, dt=F32):
            t = pc.enter_context(nc.psum_tensor(name, shape, dt))
            return t, b.res(name, xread=True)

        bH1 = [pps("pc_h1a", [128, 512]), pps("pc_h1b", [128, 512])]
        bH3 = [pps("pc_h3a", [128, 512]), pps("pc_h3b", [128, 512])]
        bY = [pps("pc_ya", [128, 512]), pps("pc_yb", [128, 512])]
        bS, rbS = pps("pc_s", [128, 512])
        bQ, rbQ = pps("pc_q", [128, 512])

        t2c, r_t2c = psb("t2c", [128, KT * 1024], BF16)
        ysb, r_ysb = psb("ysb", [128, KT * 1024], F32)
        hsb, r_hsb = psb("hsb", [128, IT * 1024], BF16)
        w1b = [psb(f"w1b{i}", [128, KT * 128], BF16) for i in range(3)]
        w3b = [psb(f"w3b{i}", [128, KT * 128], BF16) for i in range(3)]
        w2b = [psb(f"w2b{i}", [128, IT * 256], BF16) for i in range(2)]
        silu = [psb("silu0", [128, 512], BF16), psb("silu1", [128, 512], BF16)]
        gw, r_gw = psb("gw_sb", [128, KT * 16], BF16)
        glog, r_glog = psb("glog", [128, 16], F32)
        gsc, r_gsc = psb("gsc", [128, 24], F32)
        cbf, r_cbf = psb("cbf", [128, 16], BF16)
        crs, r_crs = psb("crs", [1, 1024], BF16)
        crl, r_crl = psb("crl", [1, 1024], BF16)
        cbc, r_cbc = psb("cbc", [128, 1024], BF16)
        xo = [psb("xo0", [128, 512], F32), psb("xo1", [128, 512], F32)]
        oo = [psb("oo0", [128, 512], F32), psb("oo1", [128, 512], F32)]
        ybf = [psb("ybf0", [128, 512], BF16), psb("ybf1", [128, 512], BF16)]

        b.op("gpsimd", lambda E: E.dma_start(
            gw[:, :], DI["gwS"].ap()), writes=[r_gw], dma=True)

        def load_t2c(cc):
            ccols = slice(1024 * cc, 1024 * cc + 1024)
            b.op("sync", lambda E, ccols=ccols: E.dma_start(
                t2c[:, :].rearrange("p (k n) -> p k n", k=KT),
                dram["t2"].ap()[:, ccols].rearrange("(k p) n -> p k n", p=128)),
                reads=[rD["t2"]], writes=[r_t2c], dma=True)

        load_t2c(0)
        for cc in range(2):

            for tt in range(8):
                def mmg(E, tt=tt):
                    for k in range(KT):
                        ins = E.matmul(
                            bS[:, 0:16],
                            t2c[:, k * 1024 + tt * 128:k * 1024 + tt * 128 + 128],
                            gw[:, k * 16:(k + 1) * 16],
                            start=(k == 0), stop=(k == KT - 1))
                    return ins
                b.op("tensor", mmg, reads=[r_t2c, r_gw], writes=[rbS])
                b.op("vector", lambda E: E.tensor_copy(glog[:, :], bS[:, 0:16]),
                     reads=[rbS], writes=[r_glog])
                b.op("vector", lambda E: E.reduce_max(
                    gsc[:, 0:1], glog[:, :], axis=AX, negate=True),
                    reads=[r_glog], writes=[r_gsc])
                b.op("scalar", lambda E: E.activation(
                    glog[:, :], glog[:, :], AF.Exp, bias=gsc[:, 0:1], scale=1.0,
                    accum_out=gsc[:, 1:2]),
                    reads=[r_glog, r_gsc], writes=[r_glog, r_gsc])
                b.op("vector", lambda E: E.reciprocal(gsc[:, 2:3], gsc[:, 1:2]),
                     reads=[r_gsc], writes=[r_gsc])
                b.op("vector", lambda E: E.tensor_scalar(
                    out=glog[:, :], in0=glog[:, :], scalar1=gsc[:, 2:3],
                    scalar2=None, op0=ALU.mult),
                    reads=[r_glog, r_gsc], writes=[r_glog])
                b.op("vector", lambda E: E.reduce_max(
                    gsc[:, 3:4], glog[:, :], axis=AX),
                    reads=[r_glog], writes=[r_gsc])
                b.op("vector", lambda E: E.tensor_scalar(
                    out=gsc[:, 4:20], in0=glog[:, :], scalar1=gsc[:, 3:4],
                    scalar2=None, op0=ALU.is_ge),
                    reads=[r_glog, r_gsc], writes=[r_gsc])
                b.op("vector", lambda E: E.tensor_scalar(
                    out=gsc[:, 4:20], in0=gsc[:, 4:20], scalar1=-2.0,
                    scalar2=None, op0=ALU.mult),
                    reads=[r_gsc], writes=[r_gsc])
                b.op("vector", lambda E: E.tensor_tensor(
                    gsc[:, 4:20], gsc[:, 4:20], glog[:, :], op=ALU.add),
                    reads=[r_gsc, r_glog], writes=[r_gsc])
                b.op("vector", lambda E: E.reduce_max(
                    gsc[:, 20:21], gsc[:, 4:20], axis=AX),
                    reads=[r_gsc], writes=[r_gsc])
                b.op("vector", lambda E: E.tensor_scalar(
                    out=gsc[:, 4:20], in0=glog[:, :], scalar1=gsc[:, 20:21],
                    scalar2=None, op0=ALU.is_ge),
                    reads=[r_glog, r_gsc], writes=[r_gsc])
                b.op("vector", lambda E: E.tensor_tensor(
                    glog[:, :], glog[:, :], gsc[:, 4:20], op=ALU.mult),
                    reads=[r_glog, r_gsc], writes=[r_glog])
                b.op("vector", lambda E: E.tensor_copy(cbf[:, :], glog[:, :]),
                     reads=[r_glog], writes=[r_cbf])
                for eh in range(2):
                    def mmtr(E, eh=eh):
                        for j in range(8):
                            ins = E.transpose(
                                bQ.ap().bitcast(BF16)[0:1, j * 128:(j + 1) * 128],
                                cbf[:, eh * 8 + j:eh * 8 + j + 1], ident[:, :])
                        return ins
                    b.op("tensor", mmtr, reads=[r_cbf, r_ident], writes=[rbQ])
                    def evtr(E, eh=eh):
                        for j in range(8):
                            ins = E.tensor_copy(
                                crs[0:1, j * 128:(j + 1) * 128],
                                bQ.ap().bitcast(BF16)[0:1, j * 128:(j + 1) * 128])
                        return ins
                    b.op("vector", evtr, reads=[rbQ], writes=[r_crs])
                    b.op("scalar", lambda E, tt=tt, eh=eh: E.dma_start(
                        dram["cr"].ap()[eh * 8:(eh + 1) * 8,
                                        tt * 128:(tt + 1) * 128].unsqueeze(0),
                        crs[0:1, :].rearrange("o (j n) -> o j n", j=8)),
                        reads=[r_crs], writes=[rD["cr"]], dma=True)

            for e in range(NEXP):
                hooks = G.pop("c_hooks", None) if e == 0 else None
                if e < NE:
                    src1 = DI["w1S"].ap()[e]
                    src3 = DI["w3S"].ap()[e]
                    src2 = DI["w2S"].ap()[e]
                else:
                    sh = e - NE
                    src1 = DI["sw1S"].ap()[sh]
                    src3 = DI["sw3S"].ap()[sh]
                    src2 = DI["sw2S"].ap()[sh]

                if e < NE:
                    b.op("sync", lambda E, e=e: E.dma_start(
                        crl[0:1, :], dram["cr"].ap()[e:e + 1, :]),
                        reads=[rD["cr"]], writes=[r_crl], dma=True)
                    def mmcb(E):
                        E.matmul(bS[:, 0:512], ones_r[0:1, :],
                                 crl[0:1, 0:512], start=True, stop=True)
                        return E.matmul(bQ[:, 0:512], ones_r[0:1, :],
                                        crl[0:1, 512:1024], start=True, stop=True)
                    b.op("tensor", mmcb, reads=[r_ones_r, r_crl],
                         writes=[rbS, rbQ])
                    b.op("vector", lambda E: E.tensor_copy(
                        cbc[:, 0:512], bS[:, 0:512]), reads=[rbS], writes=[r_cbc])
                    b.op("vector", lambda E: E.tensor_copy(
                        cbc[:, 512:1024], bQ[:, 0:512]), reads=[rbQ], writes=[r_cbc])

                for it in range(IT):
                    if hooks and it in hooks:
                        hooks.pop(it)()
                    gi = e * IT + it
                    w1t, r_w1t = w1b[gi % 3]
                    w3t, r_w3t = w3b[gi % 3]
                    b.op("gpsimd", lambda E, w1t=w1t, src1=src1, it=it: E.dma_start(
                        w1t[:, :], src1[it]), writes=[r_w1t], dma=True)
                    b.op("gpsimd", lambda E, w3t=w3t, src3=src3, it=it: E.dma_start(
                        w3t[:, :], src3[it]), writes=[r_w3t], dma=True)
                    for sub in range(2):
                        p1, rp1 = bH1[(2 * it + sub) % 2]
                        p3, rp3 = bH3[(2 * it + sub) % 2]
                        def mmh(E, w1t=w1t, w3t=w3t, sub=sub, p1=p1, p3=p3):
                            for k in range(KT):
                                E.matmul(p1[:, :], w1t[:, k * 128:(k + 1) * 128],
                                         t2c[:, k * 1024 + sub * 512:
                                             k * 1024 + sub * 512 + 512],
                                         start=(k == 0), stop=(k == KT - 1))
                            for k in range(KT):
                                ins = E.matmul(
                                    p3[:, :], w3t[:, k * 128:(k + 1) * 128],
                                    t2c[:, k * 1024 + sub * 512:
                                        k * 1024 + sub * 512 + 512],
                                    start=(k == 0), stop=(k == KT - 1))
                            return ins
                        b.op("tensor", mmh, reads=[r_w1t, r_w3t, r_t2c],
                             writes=[rp1, rp3])
                        sl, r_sl = silu[sub]
                        b.op("scalar", lambda E, p1=p1, sl=sl: E.activation(
                            sl[:, :], p1[:, :], AF.Silu),
                            reads=[rp1], writes=[r_sl])
                        hoff = it * 1024 + sub * 512
                        if e < NE:
                            b.op("vector", lambda E, sl=sl, p3=p3:
                                 E.tensor_tensor(sl[:, :], sl[:, :], p3[:, :],
                                                 op=ALU.mult),
                                 reads=[r_sl, rp3], writes=[r_sl])
                            b.op("vector", lambda E, sl=sl, hoff=hoff, sub=sub:
                                 E.tensor_tensor(
                                     hsb[:, hoff:hoff + 512], sl[:, :],
                                     cbc[:, sub * 512:sub * 512 + 512],
                                     op=ALU.mult),
                                 reads=[r_sl, r_cbc], writes=[r_hsb])
                        else:
                            b.op("vector", lambda E, sl=sl, p3=p3, hoff=hoff:
                                 E.tensor_tensor(hsb[:, hoff:hoff + 512],
                                                 sl[:, :], p3[:, :], op=ALU.mult),
                                 reads=[r_sl, rp3], writes=[r_hsb])
                if e == NEXP - 1 and cc == 0:
                    load_t2c(1)
                for q8 in range(8):
                    w2t, r_w2t = w2b[q8 % 2]
                    b.op("gpsimd", lambda E, w2t=w2t, src2=src2, q8=q8: E.dma_start(
                        w2t[:, :], src2[q8]), writes=[r_w2t], dma=True)
                    for dt in range(2):
                        dtg = q8 * 2 + dt
                        for sub in range(2):
                            py, rpy = bY[(2 * dt + sub) % 2]
                            def mmy(E, dt=dt, sub=sub, py=py, w2t=w2t):
                                for i in range(IT):
                                    ins = E.matmul(
                                        py[:, :],
                                        w2t[:, i * 256 + dt * 128:
                                            i * 256 + dt * 128 + 128],
                                        hsb[:, i * 1024 + sub * 512:
                                            i * 1024 + sub * 512 + 512],
                                        start=(i == 0), stop=(i == IT - 1))
                                return ins
                            b.op("tensor", mmy, reads=[r_w2t, r_hsb], writes=[rpy])
                            yoff = dtg * 1024 + sub * 512
                            if e == 0:
                                b.op("vector", lambda E, py=py, yoff=yoff:
                                     E.tensor_copy(ysb[:, yoff:yoff + 512],
                                                   py[:, :]),
                                     reads=[rpy], writes=[r_ysb])
                            else:
                                b.op("vector", lambda E, py=py, yoff=yoff:
                                     E.tensor_tensor(ysb[:, yoff:yoff + 512],
                                                     ysb[:, yoff:yoff + 512],
                                                     py[:, :], op=ALU.add),
                                     reads=[rpy, r_ysb], writes=[r_ysb])

            def ln4_a(sub, cc=cc):
                srow, r_srow = G["srow"], G["r_srow"]
                cstl, r_cstl = G["cst"], G["r_cst"]
                for k in range(KT):
                    yb, r_yb = ybf[k % 2]
                    yoff = k * 1024 + sub * 512
                    b.op("scalar", lambda E, yb=yb, yoff=yoff: E.copy(
                        yb[:, :], ysb[:, yoff:yoff + 512]),
                        reads=[r_ysb], writes=[r_yb])
                    b.op("tensor", lambda E, yb=yb, k=k: E.matmul(
                        bS[0:1, 0:512], ones_c[:, :], yb[:, :],
                        start=(k == 0), stop=(k == KT - 1)),
                        reads=[r_yb, r_ones_c], writes=[rbS])
                    b.op("vector", lambda E, yb=yb: E.tensor_tensor(
                        yb[:, :], yb[:, :], yb[:, :], op=ALU.mult),
                        reads=[r_yb], writes=[r_yb])
                    b.op("tensor", lambda E, yb=yb, k=k: E.matmul(
                        bQ[0:1, 0:512], ones_c[:, :], yb[:, :],
                        start=(k == 0), stop=(k == KT - 1)),
                        reads=[r_yb, r_ones_c], writes=[rbQ])
                b.op("vector", lambda E: E.tensor_scalar(
                    out=srow[0:1, 1024:1024 + 512], in0=bS[0:1, 0:512],
                    scalar1=1.0 / D, scalar2=None, op0=ALU.mult),
                    reads=[rbS], writes=[r_srow])
                b.op("vector", lambda E: E.tensor_scalar(
                    out=srow[0:1, 0:512], in0=bQ[0:1, 0:512],
                    scalar1=1.0 / D, scalar2=None, op0=ALU.mult),
                    reads=[rbQ], writes=[r_srow])
                b.op("vector", lambda E: E.tensor_tensor(
                    srow[0:1, 512:1024], srow[0:1, 1024:1536],
                    srow[0:1, 1024:1536], op=ALU.mult),
                    reads=[r_srow], writes=[r_srow])
                b.op("vector", lambda E: E.tensor_tensor(
                    srow[0:1, 0:512], srow[0:1, 0:512], srow[0:1, 512:1024],
                    op=ALU.subtract), reads=[r_srow], writes=[r_srow])
                b.op("scalar", lambda E: E.activation(
                    srow[0:1, 512:1024], srow[0:1, 0:512], AF.Ln,
                    bias=cstl[0:1, 1:2], scale=1.0),
                    reads=[r_srow, r_cstl], writes=[r_srow])
                b.op("scalar", lambda E: E.activation(
                    srow[0:1, 1536:2048], srow[0:1, 512:1024], AF.Exp,
                    bias=cstl[0:1, 2:3], scale=-0.5),
                    reads=[r_srow, r_cstl], writes=[r_srow])
                b.op("vector", lambda E: E.tensor_tensor(
                    srow[0:1, 2048:2560], srow[0:1, 1024:1536],
                    srow[0:1, 1536:2048], op=ALU.mult),
                    reads=[r_srow], writes=[r_srow])

            def ln4_b(sub, cc=cc):
                scols = slice(1024 * cc + 512 * sub, 1024 * cc + 512 * sub + 512)
                rb, r_rb, mb, r_mb = ln_bcast(G, 512, bS, rbS, bQ, rbQ, f32=True)
                for dt in range(KT):
                    xt, r_xt = xo[dt % 2]
                    ot, r_ot = oo[dt % 2]
                    b.op("sync", lambda E, xt=xt, dt=dt, scols=scols: E.dma_start(
                        xt[:, :], dram["x2"].ap()[128 * dt:128 * dt + 128, scols]),
                        reads=[rD["x2"]], writes=[r_xt], dma=True)
                    yoff = dt * 1024 + sub * 512
                    b.op("vector", lambda E, ot=ot, yoff=yoff, rb=rb:
                         E.tensor_tensor(ot[:, :], ysb[:, yoff:yoff + 512],
                                         rb[:, 0:512], op=ALU.mult),
                         reads=[r_ysb, r_rb], writes=[r_ot])
                    b.op("vector", lambda E, ot=ot, mb=mb: E.tensor_tensor(
                        ot[:, :], ot[:, :], mb[:, 0:512], op=ALU.subtract),
                        reads=[r_ot, r_mb], writes=[r_ot])
                    b.op("vector", lambda E, ot=ot, xt=xt: E.tensor_tensor(
                        ot[:, :], ot[:, :], xt[:, :], op=ALU.add),
                        reads=[r_ot, r_xt], writes=[r_ot])
                    b.op("scalar", lambda E, ot=ot, dt=dt, scols=scols: E.dma_start(
                        dram["out"].ap()[128 * dt:128 * dt + 128, scols], ot[:, :]),
                        reads=[r_ot], writes=[rD["out"]], dma=True)

            if cc == 0:
                G["c_hooks"] = {1: lambda: ln4_a(0), 3: lambda: ln4_b(0),
                                6: lambda: ln4_a(1), 8: lambda: ln4_b(1)}
            else:
                for sub in range(2):
                    ln4_a(sub)
                    ln4_b(sub)



def _host_prep(inputs):
    import ml_dtypes
    x = np.asarray(inputs["L0_attn_x"], dtype=np.float32)
    shared = {}

    def tile_w1like(w):
        E_ = w.shape[0]
        r = w.reshape(E_, IT, 128, KT, 128)
        r = r.transpose(0, 1, 4, 3, 2)
        return np.ascontiguousarray(r.reshape(E_, IT, 128, KT * 128))

    def tile_w2like(w):
        E_ = w.shape[0]
        r = w.reshape(E_, 8, 256, IT, 128)
        r = r.transpose(0, 1, 4, 3, 2)
        return np.ascontiguousarray(r.reshape(E_, 8, 128, IT * 256))

    wkT = np.asarray(inputs["L0_attn_Wk"], dtype=np.float32).T
    shared["wkS"] = np.ascontiguousarray(
        wkT.reshape(KT, 128, 16, 128).transpose(2, 1, 0, 3).reshape(16, 128, KT * 128))
    wvT = np.asarray(inputs["L0_attn_Wv"], dtype=np.float32).T
    shared["wvS"] = np.ascontiguousarray(
        wvT.reshape(KT, 128, 4, 512).transpose(2, 1, 0, 3).reshape(4, 128, KT * 512))
    shared["woT"] = np.ascontiguousarray(np.asarray(inputs["L0_attn_Wo"], dtype=np.float32).T)
    wqT = np.asarray(inputs["L0_attn_Wq"], dtype=np.float32).T
    r = wqT.reshape(KT, 128, 8, 256).transpose(2, 1, 0, 3)
    shared["wqS"] = np.ascontiguousarray(r.reshape(8, 128, KT * 256))
    gwT = np.asarray(inputs["L0_ffn_gate_w"], dtype=np.float32).T
    shared["gwS"] = np.ascontiguousarray(
        gwT.reshape(KT, 128, 16).transpose(1, 0, 2).reshape(128, KT * 16))
    shared["w1S"] = tile_w1like(np.asarray(inputs["L0_ffn_w1"], dtype=np.float32))
    shared["w3S"] = tile_w1like(np.asarray(inputs["L0_ffn_w3"], dtype=np.float32))
    shared["w2S"] = tile_w2like(np.asarray(inputs["L0_ffn_w2"], dtype=np.float32))
    sw1 = np.asarray(inputs["L0_ffn_sw1"], dtype=np.float32).reshape(2, INTER, D)
    sw3 = np.asarray(inputs["L0_ffn_sw3"], dtype=np.float32).reshape(2, INTER, D)
    shared["sw1S"] = tile_w1like(sw1)
    shared["sw3S"] = tile_w1like(sw3)
    sw2 = np.asarray(inputs["L0_ffn_sw2"], dtype=np.float32)
    sw2e = np.stack([sw2[:, 0:INTER], sw2[:, INTER:2 * INTER]], axis=0)
    shared["sw2S"] = tile_w2like(sw2e)

    half = 64
    pos = np.arange(T, dtype=np.float32)
    inv_freq = 1.0 / (10000.0 ** (np.arange(half, dtype=np.float32) / half))
    ang = inv_freq[:, None] * pos[None, :]
    sin, cos = np.sin(ang), np.cos(ang)
    Ck = np.ascontiguousarray(np.concatenate([cos, cos], axis=0).astype(np.float32))
    Sk = np.ascontiguousarray(np.concatenate([-sin, sin], axis=0).astype(np.float32))
    shared["ropeCk"] = Ck
    shared["ropeSk"] = Sk
    shared["identity"] = np.eye(128, dtype=ml_dtypes.bfloat16)

    per_core = []
    for c in range(8):
        bidx, o = c // 2, c % 2
        idx = np.concatenate([np.arange(512 * (2 * qc + o), 512 * (2 * qc + o) + 512)
                              for qc in range(4)])
        m = dict(shared)
        xb = x[bidx].reshape(8, 512, KT, 128)
        m["xTc"] = np.ascontiguousarray(
            xb.transpose(0, 3, 2, 1).reshape(8, 128, KT * 512))
        xq = x[bidx][idx].reshape(4, 512, KT, 128)
        m["xTqc"] = np.ascontiguousarray(
            xq.transpose(0, 3, 2, 1).reshape(4, 128, KT * 512))
        m["ropeCq"] = np.ascontiguousarray(Ck[:, idx])
        m["ropeSq"] = np.ascontiguousarray(Sk[:, idx])
        j = np.arange(8)[:, None, None]
        p = np.arange(128)[None, :, None]
        n = np.arange(512)[None, None, :]
        mask = np.where(n >= p + 128 * j - 512 * o, 0.0, -1e9)
        m["maskT"] = np.ascontiguousarray(mask.astype(ml_dtypes.bfloat16))
        per_core.append((m, idx))
    return per_core


def kernel(**inputs):
    per_core = _host_prep(inputs)
    nc = build()
    in_maps = [m for (m, _) in per_core]
    res = run_bass_kernel_spmd(nc, in_maps, core_ids=list(range(8)))
    out = np.empty((NB, T, D), dtype=np.float32)
    for c in range(8):
        bidx = c // 2
        _, idx = per_core[c]
        out[bidx, idx, :] = np.asarray(res.results[c]["outT"], dtype=np.float32).T
    return out


# version 07
# speedup vs baseline: 1.1006x; 1.1006x over previous
import numpy as np
from contextlib import ExitStack

import concourse.bass as bass
import concourse.mybir as mybir
from concourse.bass_utils import run_bass_kernel_spmd

F32 = mybir.dt.float32
BF16 = mybir.dt.bfloat16
AF = mybir.ActivationFunctionType
ALU = mybir.AluOpType
AX = mybir.AxisListType.X

D = 2048
T = 4096
NB = 4
NH = 16
DH = 128
KT = 16
TL = 2048
INTER = 1408
IT = 11
NE = 16
NEXP = 18
EPS = 1e-5
EXP_BIAS = -15.0
ISCALE = float(1.0 / np.sqrt(DH))

DEBUG_SCRATCH = False
PHASES = "ABC"
A_CHUNKS = 8
A2_ON = True
A_STAGE = 9
A_ROPE = 4

ENGS = ("tensor", "scalar", "vector", "gpsimd", "sync")


class Res:

    __slots__ = ("name", "writer", "readers", "dsem", "xread")

    def __init__(self, name, xread=False):
        self.name = name
        self.writer = None
        self.readers = {}
        self.dsem = None
        self.xread = xread


class B:

    def __init__(self, nc, ctx):
        self.nc = nc
        self.ctx = ctx
        self.streams = {e: [] for e in ENGS}
        self.sem = {e: self._mk_sem("sm_" + e) for e in ENGS}
        self.semv = {s: 0 for s in self.sem.values()}
        self.hw = {e: {} for e in ENGS}
        self._ndsem = 0

    def _mk_sem(self, name):
        return self.ctx.enter_context(self.nc.semaphore(name))

    def res(self, name, xread=False):
        return Res(name, xread)

    def _wait(self, eng, s, v):
        if v <= self.hw[eng].get(s, 0):
            return
        self.hw[eng][s] = v
        self.streams[eng].append(lambda E, s=s, v=v: E.wait_ge(s, v))

    def op(self, eng, fn, reads=(), writes=(), dma=False):
        assert eng in ENGS
        for r in reads:
            if r.xread:
                for s2, (v2, reng2) in r.readers.items():
                    if reng2 != eng:
                        self._wait(eng, s2, v2)
            w = r.writer
            if w is None:
                continue
            s, v, weng = w
            if weng == eng == "tensor":
                continue
            if weng == eng and not dma:
                self._wait(eng, s, v)
            else:
                self._wait(eng, s, v)
        for wres in writes:
            for s, (v, reng) in wres.readers.items():
                if reng == eng == "tensor":
                    continue
                self._wait(eng, s, v)
            w = wres.writer
            if w is not None:
                s, v, weng = w
                if weng == eng == "tensor":
                    pass
                else:
                    self._wait(eng, s, v)
        if dma:
            assert len(writes) == 1, "each DMA must write exactly one resource"
            wres = writes[0]
            if wres.dsem is None:
                wres.dsem = self._mk_sem(f"sd{self._ndsem}_{wres.name}")
                self._ndsem += 1
                self.semv[wres.dsem] = 0
            s = wres.dsem
            self.semv[s] += 16
            v = self.semv[s]
            def run(E, fn=fn, s=s):
                ins = fn(E)
                ins.then_inc(s, 16)
            self.streams[eng].append(run)
        else:
            s = self.sem[eng]
            self.semv[s] += 1
            v = self.semv[s]
            def run(E, fn=fn, s=s):
                ins = fn(E)
                ins.then_inc(s, 1)
            self.streams[eng].append(run)
        for wres in writes:
            wres.writer = (s, v, eng)
            wres.readers = {}
        for r in reads:
            cur = r.readers.get(s)
            if cur is None or cur[0] < v:
                r.readers[s] = (v, eng)
        return (s, v)

    def barrier(self):
        marks = [(s, v) for s, v in self.semv.items() if v > 0]
        for e in ENGS:
            for s, v in marks:
                self._wait(e, s, v)

    def emit(self):
        with self.nc.Block() as block:
            @block.tensor
            def _(E):
                for fn in self.streams["tensor"]:
                    fn(E)

            @block.scalar
            def _(E):
                for fn in self.streams["scalar"]:
                    fn(E)

            @block.vector
            def _(E):
                for fn in self.streams["vector"]:
                    fn(E)

            @block.gpsimd
            def _(E):
                for fn in self.streams["gpsimd"]:
                    fn(E)

            @block.sync
            def _(E):
                for fn in self.streams["sync"]:
                    fn(E)



def build():
    nc = bass.Bass("TRN2", target_bir_lowering=False)

    DI = dict()

    def din(name, shape, dt=F32):
        DI[name] = nc.dram_tensor(name, shape, dt, kind="ExternalInput")
        return DI[name]

    din("xTc", [8, 128, KT * 512])
    din("xTqc", [4, 128, KT * 512])
    din("wqS", [8, 128, KT * 256])
    din("wkS4", [4, 128, 4 * KT * 128]); din("wvS", [4, 128, KT * 512]); din("woT", [D, D])
    din("gwS", [128, KT * 16])
    din("w1S", [NE, IT, 128, KT * 128])
    din("w3S", [NE, IT, 128, KT * 128])
    din("w2S", [NE, 8, 128, IT * 256])
    din("sw1S", [2, IT, 128, KT * 128])
    din("sw3S", [2, IT, 128, KT * 128])
    din("sw2S", [2, 8, 128, IT * 256])
    din("ropeCk", [128, T]); din("ropeSk", [128, T])
    din("ropeCq", [128, TL]); din("ropeSq", [128, TL])
    din("maskT", [8, 128, 512], BF16)
    din("identity", [128, 128], BF16)

    skind = "ExternalOutput" if DEBUG_SCRATCH else "Internal"
    d_t1 = nc.dram_tensor("t1_hbm", [D, T], BF16, kind=skind)
    d_k = nc.dram_tensor("k_hbm", [D, T], BF16, kind=skind)
    d_v = nc.dram_tensor("v_hbm", [T // 128, 128, D], BF16, kind=skind)
    d_t1q = nc.dram_tensor("t1q_hbm", [D, TL], BF16, kind=skind)
    d_o = nc.dram_tensor("o_hbm", [D, TL], BF16, kind=skind)
    d_x2 = nc.dram_tensor("x2_hbm", [D, TL], F32, kind=skind)
    d_t2 = nc.dram_tensor("t2_hbm", [D, TL], BF16, kind=skind)
    d_cr = nc.dram_tensor("crows_hbm", [16, 1024], BF16, kind=skind)
    d_out = nc.dram_tensor("outT", [D, TL], F32, kind="ExternalOutput")

    with ExitStack() as ctx:
        b = B(nc, ctx)

        def sb(name, shape, dt):
            t = ctx.enter_context(nc.sbuf_tensor(name, shape, dt))
            return t, b.res(name)

        ones_c, r_ones_c = sb("ones_c", [128, 1], BF16)
        ones_r, r_ones_r = sb("ones_r", [1, 128], BF16)
        ones_rf, r_ones_rf = sb("ones_rf", [1, 128], F32)
        ident, r_ident = sb("identS", [128, 128], BF16)
        cst, r_cst = sb("cst", [128, 4], F32)
        srow, r_srow = sb("srow", [1, 2560], F32)
        ln_sq, r_ln_sq = sb("ln_sq", [128, KT * 512], BF16)
        brow_g, r_brow_g = sb("brow_g", [1, 1024], BF16)
        ln_rb, r_ln_rb = sb("ln_rb", [128, 512], F32)
        ln_mb, r_ln_mb = sb("ln_mb", [128, 512], F32)
        ln_rb16, r_ln_rb16 = sb("ln_rb16", [128, 512], BF16)
        ln_mb16, r_ln_mb16 = sb("ln_mb16", [128, 512], BF16)

        rD = {n: b.res("dram_" + n) for n in
              ("t1", "k", "v", "t1q", "o", "x2", "t2", "cr", "out")}
        dram = dict(t1=d_t1, k=d_k, v=d_v, t1q=d_t1q, o=d_o, x2=d_x2,
                    t2=d_t2, cr=d_cr, out=d_out)

        b.op("gpsimd", lambda E: E.memset(ones_c[:, :], 1.0), writes=[r_ones_c])
        b.op("gpsimd", lambda E: E.memset(ones_r[:, :], 1.0), writes=[r_ones_r])
        b.op("gpsimd", lambda E: E.memset(ones_rf[:, :], 1.0), writes=[r_ones_rf])
        b.op("vector", lambda E: E.memset(cst[:, 0:1], EXP_BIAS), writes=[r_cst])
        b.op("vector", lambda E: E.memset(cst[:, 1:2], EPS), writes=[r_cst])
        b.op("vector", lambda E: E.memset(cst[:, 2:3], 0.0), writes=[r_cst])
        b.op("sync", lambda E: E.dma_start(ident[:, :], DI["identity"].ap()),
             writes=[r_ident], dma=True)

        G = dict(nc=nc, b=b, ctx=ctx, DI=DI, dram=dram, rD=rD, sb=sb,
                 ones_c=ones_c, r_ones_c=r_ones_c, ones_r=ones_r,
                 r_ones_r=r_ones_r, ones_rf=ones_rf, r_ones_rf=r_ones_rf,
                 ident=ident, r_ident=r_ident, cst=cst, r_cst=r_cst,
                 srow=srow, r_srow=r_srow,
                 ln_sq=ln_sq, r_ln_sq=r_ln_sq, brow_g=brow_g, r_brow_g=r_brow_g,
                 ln_rb=ln_rb, r_ln_rb=r_ln_rb, ln_mb=ln_mb, r_ln_mb=r_ln_mb,
                 ln_rb16=ln_rb16, r_ln_rb16=r_ln_rb16,
                 ln_mb16=ln_mb16, r_ln_mb16=r_ln_mb16)

        if "A" in PHASES:
            phase_A(G)
        b.barrier()
        if "B" in PHASES:
            phase_B(G)
        b.barrier()
        if "C" in PHASES:
            phase_C(G)

        for s, v in list(b.semv.items()):
            if v > 0:
                b._wait("sync", s, v)

        b.emit()
    return nc



def ln_stats(G, xb, r_xb, ncols, bS, rbS, bQ, rbQ, nkt=KT):
    b = G["b"]
    srow, r_srow, cst, r_cst = G["srow"], G["r_srow"], G["cst"], G["r_cst"]
    ones_c, r_ones_c = G["ones_c"], G["r_ones_c"]
    ln_sq, r_ln_sq = G["ln_sq"], G["r_ln_sq"]

    b.op("vector", lambda E: E.tensor_tensor(
        ln_sq[:, 0:nkt * ncols], xb[:, 0:nkt * ncols], xb[:, 0:nkt * ncols],
        op=ALU.mult), reads=[r_xb], writes=[r_ln_sq])

    def mm_s(E):
        for k in range(nkt):
            ins = E.matmul(bS[0:1, 0:ncols], ones_c[:, :],
                           xb[:, k * ncols:(k + 1) * ncols],
                           start=(k == 0), stop=(k == nkt - 1))
        return ins
    b.op("tensor", mm_s, reads=[r_ones_c, r_xb], writes=[rbS])

    def mm_q(E):
        for k in range(nkt):
            ins = E.matmul(bQ[0:1, 0:ncols], ones_c[:, :],
                           ln_sq[:, k * ncols:(k + 1) * ncols],
                           start=(k == 0), stop=(k == nkt - 1))
        return ins
    b.op("tensor", mm_q, reads=[r_ones_c, r_ln_sq], writes=[rbQ])

    b.op("vector", lambda E: E.tensor_scalar(
        out=srow[0:1, 1024:1024 + ncols], in0=bS[0:1, 0:ncols],
        scalar1=1.0 / D, scalar2=None, op0=ALU.mult),
        reads=[rbS], writes=[r_srow])
    b.op("vector", lambda E: E.tensor_scalar(
        out=srow[0:1, 0:ncols], in0=bQ[0:1, 0:ncols],
        scalar1=1.0 / D, scalar2=None, op0=ALU.mult),
        reads=[rbQ], writes=[r_srow])
    b.op("vector", lambda E: E.tensor_tensor(
        srow[0:1, 512:512 + ncols], srow[0:1, 1024:1024 + ncols],
        srow[0:1, 1024:1024 + ncols], op=ALU.mult),
        reads=[r_srow], writes=[r_srow])
    b.op("vector", lambda E: E.tensor_tensor(
        srow[0:1, 0:ncols], srow[0:1, 0:ncols], srow[0:1, 512:512 + ncols],
        op=ALU.subtract), reads=[r_srow], writes=[r_srow])
    b.op("scalar", lambda E: E.activation(
        srow[0:1, 512:512 + ncols], srow[0:1, 0:ncols], AF.Ln,
        bias=cst[0:1, 1:2], scale=1.0), reads=[r_srow, r_cst], writes=[r_srow])
    b.op("scalar", lambda E: E.activation(
        srow[0:1, 1536:1536 + ncols], srow[0:1, 512:512 + ncols], AF.Exp,
        bias=cst[0:1, 2:3], scale=-0.5), reads=[r_srow, r_cst], writes=[r_srow])
    b.op("vector", lambda E: E.tensor_tensor(
        srow[0:1, 2048:2048 + ncols], srow[0:1, 1024:1024 + ncols],
        srow[0:1, 1536:1536 + ncols], op=ALU.mult),
        reads=[r_srow], writes=[r_srow])


def ln_bcast(G, ncols, bS, rbS, bQ, rbQ, f32=False):
    b = G["b"]
    srow, r_srow = G["srow"], G["r_srow"]
    if f32:
        ones, r_ones = G["ones_rf"], G["r_ones_rf"]
        rb, r_rb, mb, r_mb = G["ln_rb"], G["r_ln_rb"], G["ln_mb"], G["r_ln_mb"]
        def mmb(E):
            E.matmul(bS[:, 0:ncols], ones[0:1, :], srow[0:1, 1536:1536 + ncols],
                     start=True, stop=True)
            return E.matmul(bQ[:, 0:ncols], ones[0:1, :],
                            srow[0:1, 2048:2048 + ncols], start=True, stop=True)
        b.op("tensor", mmb, reads=[r_ones, r_srow], writes=[rbS, rbQ])
    else:
        ones, r_ones = G["ones_r"], G["r_ones_r"]
        rb, r_rb, mb, r_mb = (G["ln_rb16"], G["r_ln_rb16"],
                              G["ln_mb16"], G["r_ln_mb16"])
        brow, r_brow = G["brow_g"], G["r_brow_g"]
        b.op("scalar", lambda E: E.copy(
            brow[0:1, 0:ncols], srow[0:1, 1536:1536 + ncols]),
            reads=[r_srow], writes=[r_brow])
        b.op("scalar", lambda E: E.copy(
            brow[0:1, 512:512 + ncols], srow[0:1, 2048:2048 + ncols]),
            reads=[r_srow], writes=[r_brow])
        def mmb(E):
            E.matmul(bS[:, 0:ncols], ones[0:1, :], brow[0:1, 0:ncols],
                     start=True, stop=True)
            return E.matmul(bQ[:, 0:ncols], ones[0:1, :],
                            brow[0:1, 512:512 + ncols], start=True, stop=True)
        b.op("tensor", mmb, reads=[r_ones, r_brow], writes=[rbS, rbQ])
    b.op("vector", lambda E: E.tensor_copy(rb[:, 0:ncols], bS[:, 0:ncols]),
         reads=[rbS], writes=[r_rb])
    b.op("vector", lambda E: E.tensor_copy(mb[:, 0:ncols], bQ[:, 0:ncols]),
         reads=[rbQ], writes=[r_mb])
    return rb, r_rb, mb, r_mb


def ln_apply_bf16(G, xb, r_xb, out, r_out, out_off, ncols, nkt=KT):
    b = G["b"]
    ln_tmp, r_ln_tmp = G["ln_sq"], G["r_ln_sq"]
    rb, r_rb = G["ln_rb16"], G["r_ln_rb16"]
    mb, r_mb = G["ln_mb16"], G["r_ln_mb16"]
    b.op("vector", lambda E: E.tensor_tensor(
        ln_tmp[:, 0:nkt * ncols].rearrange("p (k n) -> p k n", k=nkt),
        xb[:, 0:nkt * ncols].rearrange("p (k n) -> p k n", k=nkt),
        rb[:, None, 0:ncols].broadcast_to([128, nkt, ncols]), op=ALU.mult),
        reads=[r_xb, r_rb], writes=[r_ln_tmp])
    b.op("vector", lambda E: E.tensor_tensor(
        out[:, out_off:out_off + nkt * ncols].rearrange("p (k n) -> p k n", k=nkt),
        ln_tmp[:, 0:nkt * ncols].rearrange("p (k n) -> p k n", k=nkt),
        mb[:, None, 0:ncols].broadcast_to([128, nkt, ncols]), op=ALU.subtract),
        reads=[r_ln_tmp, r_mb], writes=[r_out])



def phase_A(G):
    nc, b, ctx, DI = G["nc"], G["b"], G["ctx"], G["DI"]
    dram, rD = G["dram"], G["rD"]
    sb = G["sb"]

    with ExitStack() as pc:
        def psb(name, shape, dt):
            t = pc.enter_context(nc.sbuf_tensor(name, shape, dt))
            return t, b.res(name)

        def pps(name, shape):
            t = pc.enter_context(nc.psum_tensor(name, shape, F32))
            return t, b.res(name, xread=True)

        bS, rbS = pps("pa_s", [128, 512])
        bQ, rbQ = pps("pa_q", [128, 512])
        bK = [pps("pa_k0", [128, 512]), pps("pa_k1", [128, 512])]

        xa = [psb("xa0", [128, KT * 512], BF16), psb("xa1", [128, KT * 512], BF16)]
        t1a = [psb("t1a0", [128, KT * 512], BF16), psb("t1a1", [128, KT * 512], BF16)]
        kout = [psb("ko0", [128, KT * 512], BF16)]
        wslab = [psb(f"wslab{i}", [128, 4 * KT * 128], BF16) for i in range(4)]
        for t in range(4):
            wst, r_wst = wslab[t]
            b.op("gpsimd", lambda E, wst=wst, t=t: E.dma_start(
                wst[:, :], DI["wkS4"].ap()[t]), writes=[r_wst], dma=True)
        ropeC = [psb("ropeC0", [128, 512], F32), psb("ropeC1", [128, 512], F32)]
        ropeS = [psb("ropeS0", [128, 512], F32), psb("ropeS1", [128, 512], F32)]
        krot = [psb("krot0", [128, 512], F32), psb("krot1", [128, 512], F32)]
        ta = [psb("ta0", [128, 512], F32), psb("ta1", [128, 512], F32)]
        tb = [psb("tb0", [128, 512], F32), psb("tb1", [128, 512], F32)]


        def a1_ln_pre(ch):
            cols = slice(512 * ch, 512 * ch + 512)
            xat, r_xat = xa[ch % 2]
            rC, r_rC = ropeC[ch % 2]
            rS, r_rS = ropeS[ch % 2]
            b.op("sync", lambda E, rC=rC, cols=cols: E.dma_start(
                rC[:, :], DI["ropeCk"].ap()[:, cols]), writes=[r_rC], dma=True)
            b.op("sync", lambda E, rS=rS, cols=cols: E.dma_start(
                rS[:, :], DI["ropeSk"].ap()[:, cols]), writes=[r_rS], dma=True)
            b.op("gpsimd", lambda E, xat=xat, ch=ch: E.dma_start(
                xat[:, :], DI["xTc"].ap()[ch]), writes=[r_xat], dma=True)
            ln_stats(G, xat, r_xat, 512, bS, rbS, bQ, rbQ)

        def a1_ln_post(ch):
            cols = slice(512 * ch, 512 * ch + 512)
            xat, r_xat = xa[ch % 2]
            t1t, r_t1t = t1a[ch % 2]
            ln_bcast(G, 512, bS, rbS, bQ, rbQ, f32=False)
            ln_apply_bf16(G, xat, r_xat, t1t, r_t1t, 0, 512)
            b.op("scalar", lambda E, t1t=t1t, cols=cols: E.dma_start(
                dram["t1"].ap()[:, cols].rearrange("(k p) n -> p k n", p=128),
                t1t[:, :].rearrange("p (k n) -> p k n", k=KT)),
                reads=[r_t1t], writes=[rD["t1"]], dma=True)

        def a1_k(ch, mid=None):
            cols = slice(512 * ch, 512 * ch + 512)
            t1t, r_t1t = t1a[ch % 2]
            kot, r_kot = kout[0]
            rC, r_rC = ropeC[ch % 2]
            rS, r_rS = ropeS[ch % 2]
            for h in range(KT):
                if h == 4 and mid is not None:
                    mid()
                pk, rpk = bK[h % 2]
                wkt, r_wkt = wslab[h // 4]
                hoff = (h % 4) * (KT * 128)
                def mmk(E, h=h, t1t=t1t, pk=pk, wkt=wkt, hoff=hoff):
                    for k in range(KT):
                        ins = E.matmul(pk[:, :],
                                       wkt[:, hoff + k * 128:hoff + (k + 1) * 128],
                                       t1t[:, k * 512:(k + 1) * 512],
                                       start=(k == 0), stop=(k == KT - 1))
                    return ins
                b.op("tensor", mmk, reads=[r_wkt, r_t1t], writes=[rpk])
                kr, r_kr = krot[h % 2]
                tat, r_tat = ta[h % 2]
                tbt, r_tbt = tb[h % 2]
                def rot(E, pk=pk, kr=kr):
                    E.copy(kr[0:64, :], pk[64:128, 0:512])
                    return E.copy(kr[64:128, :], pk[0:64, 0:512])
                b.op("scalar", rot, reads=[rpk], writes=[r_kr])
                b.op("vector", lambda E, pk=pk, tat=tat, rC=rC: E.tensor_tensor(
                    tat[:, :], rC[:, :], pk[:, 0:512], op=ALU.mult),
                    reads=[rpk, r_rC], writes=[r_tat])
                b.op("gpsimd", lambda E, kr=kr, tbt=tbt, rS=rS: E.tensor_tensor(
                    tbt[:, :], kr[:, :], rS[:, :], op=ALU.mult),
                    reads=[r_kr, r_rS], writes=[r_tbt])
                b.op("vector", lambda E, tat=tat, tbt=tbt, kot=kot, h=h:
                     E.tensor_tensor(kot[:, h * 512:(h + 1) * 512], tat[:, :],
                                     tbt[:, :], op=ALU.add),
                     reads=[r_tat, r_tbt], writes=[r_kot])
            b.op("scalar", lambda E, kot=kot, cols=cols: E.dma_start(
                dram["k"].ap()[:, cols].rearrange("(k p) n -> p k n", p=128),
                kot[:, :].rearrange("p (k n) -> p k n", k=KT)),
                reads=[r_kot], writes=[rD["k"]], dma=True)

        a1_ln_pre(0)
        a1_ln_post(0)
        for ch in range(1, A_CHUNKS + 1):
            if ch < A_CHUNKS:
                a1_ln_pre(ch)
                a1_k(ch - 1, mid=lambda ch=ch: a1_ln_post(ch))
            else:
                a1_k(ch - 1)

        def b0_chunk(qc):
            cols = slice(512 * qc, 512 * qc + 512)
            xqt, r_xqt = xa[qc % 2]
            tqt, r_tqt = kout[0]
            b.op("gpsimd", lambda E, xqt=xqt, qc=qc: E.dma_start(
                xqt[:, :], DI["xTqc"].ap()[qc]), writes=[r_xqt], dma=True)
            ln_stats(G, xqt, r_xqt, 512, bS, rbS, bQ, rbQ)
            ln_bcast(G, 512, bS, rbS, bQ, rbQ, f32=False)
            ln_apply_bf16(G, xqt, r_xqt, tqt, r_tqt, 0, 512)
            b.op("scalar", lambda E, tqt=tqt, cols=cols: E.dma_start(
                dram["t1q"].ap()[:, cols].rearrange("(k p) n -> p k n", p=128),
                tqt[:, :].rearrange("p (k n) -> p k n", k=KT)),
                reads=[r_tqt], writes=[rD["t1q"]], dma=True)

        vout = [psb(f"vo{i}", [128, 2048], BF16) for i in range(2)]
        for dc in range(4):
            wvt, r_wvt = wslab[dc]
            b.op("gpsimd", lambda E, wvt=wvt, dc=dc: E.dma_start(
                wvt[:, :], DI["wvS"].ap()[dc]), writes=[r_wvt], dma=True)
        for ch in range(A_CHUNKS):
            cols = slice(512 * ch, 512 * ch + 512)
            t1t, r_t1t = t1a[ch % 2]
            b.op("sync", lambda E, t1t=t1t, cols=cols: E.dma_start(
                t1t[:, :].rearrange("p (k n) -> p k n", k=KT),
                dram["t1"].ap()[:, cols].rearrange("(k p) n -> p k n", p=128)),
                reads=[rD["t1"]], writes=[r_t1t], dma=True)
            for tt in range(4):
                vo, r_vo = vout[tt % 2]
                for dc in range(4):
                    wvt, r_wvt = wslab[dc]
                    pk, rpk = bK[(tt * 4 + dc) % 2]
                    def mmv(E, t1t=t1t, tt=tt, dc=dc, pk=pk, wvt=wvt):
                        for k in range(KT):
                            ins = E.matmul(
                                pk[:, :],
                                t1t[:, k * 512 + tt * 128:k * 512 + tt * 128 + 128],
                                wvt[:, k * 512:(k + 1) * 512],
                                start=(k == 0), stop=(k == KT - 1))
                        return ins
                    b.op("tensor", mmv, reads=[r_wvt, r_t1t], writes=[rpk])
                    b.op("vector", lambda E, pk=pk, vo=vo, dc=dc: E.tensor_copy(
                        vo[:, dc * 512:(dc + 1) * 512], pk[:, :]),
                        reads=[rpk], writes=[r_vo])
                b.op("scalar", lambda E, vo=vo, ch=ch, tt=tt: E.dma_start(
                    dram["v"].ap()[ch * 4 + tt], vo[:, :]),
                    reads=[r_vo], writes=[rD["v"]], dma=True)
            if ch % 2 == 1 and ch // 2 < 4:
                b0_chunk(ch // 2)


def phase_B(G):
    nc, b, DI = G["nc"], G["b"], G["DI"]
    dram, rD = G["dram"], G["rD"]


    with ExitStack() as pc:
        def psb(name, shape, dt):
            t = pc.enter_context(nc.sbuf_tensor(name, shape, dt))
            return t, b.res(name)

        def pps(name, shape, dt=F32):
            t = pc.enter_context(nc.psum_tensor(name, shape, dt))
            return t, b.res(name, xread=True)

        pgA, rpgA = pps("pb1_ga", [128, 1536])
        pgB, rpgB = pps("pb1_gb", [128, 1536])
        pSum, rpSum = pps("pb1_sum", [128, 512])
        pO, rpO = pps("pb1_o", [128, 512])

        cst, r_cst = G["cst"], G["r_cst"]
        srow, r_srow = G["srow"], G["r_srow"]
        ident, r_ident = G["ident"], G["r_ident"]
        ones_c, r_ones_c = G["ones_c"], G["r_ones_c"]
        ones_rf, r_ones_rf = G["ones_rf"], G["r_ones_rf"]

        mask, r_mask = psb("mask_sb", [128, 8 * 512], BF16)
        ropeCq, r_ropeCq = psb("ropeCq_sb", [128, TL], F32)
        ropeSq, r_ropeSq = psb("ropeSq_sb", [128, TL], F32)
        b.op("sync", lambda E: E.dma_start(
            mask[:, :].rearrange("p (j n) -> p j n", j=8),
            DI["maskT"].ap().rearrange("j p n -> p j n")),
            writes=[r_mask], dma=True)
        b.op("sync", lambda E: E.dma_start(ropeCq[:, :], DI["ropeCq"].ap()),
             writes=[r_ropeCq], dma=True)
        b.op("sync", lambda E: E.dma_start(ropeSq[:, :], DI["ropeSq"].ap()),
             writes=[r_ropeSq], dma=True)

        wq = [psb("wq0", [128, KT * 256], BF16), psb("wq1", [128, KT * 256], BF16)]
        K2a = [psb("K2a0", [128, 2 * 1024], BF16), psb("K2a1", [128, 2 * 1024], BF16)]
        K2b = [psb("K2b0", [128, 2 * 3072], BF16), psb("K2b1", [128, 2 * 3072], BF16)]
        V2a = [psb("V2a0", [128, 2 * 1024], BF16), psb("V2a1", [128, 2 * 1024], BF16)]
        V2b = [psb("V2b0", [128, 2 * 3072], BF16), psb("V2b1", [128, 2 * 3072], BF16)]

        def kv_slice(slot, which, hh, kt):
            if kt < 8:
                t_, r_ = (K2a if which == "k" else V2a)[slot]
                return t_, r_, hh * 1024 + kt * 128
            t_, r_ = (K2b if which == "k" else V2b)[slot]
            return t_, r_, hh * 3072 + (kt - 8) * 128
        t1qc = [psb("t1qc0", [128, KT * 512], BF16),
                psb("t1qc1", [128, KT * 512], BF16)]
        qsb = [psb("qsb0", [128, 512], BF16), psb("qsb1", [128, 512], BF16)]
        qrot = psb("qrot", [128, 512], F32)
        qta = psb("qta", [128, 512], F32)
        qtb = psb("qtb", [128, 512], F32)
        PT = [psb(f"pt{i}", [128, 1536], BF16) for i in range(3)]
        rrow = psb("rrow", [1, 512], F32)
        rbo = psb("rbo", [128, 512], F32)
        Opair = psb("Opair", [128, 2 * TL], BF16)

        def load_pair(hp, slot):
            wqt, r_wqt = wq[slot]
            b.op("gpsimd", lambda E, wqt=wqt, hp=hp: E.dma_start(
                wqt[:, :], DI["wqS"].ap()[hp]), writes=[r_wqt], dma=True)
            for hh in range(2):
                h = 2 * hp + hh
                ka, r_ka = K2a[slot]
                kb, r_kb = K2b[slot]
                va, r_va = V2a[slot]
                vb, r_vb = V2b[slot]
                b.op("sync", lambda E, ka=ka, h=h, hh=hh: E.dma_start(
                    ka[:, hh * 1024:(hh + 1) * 1024],
                    dram["k"].ap()[128 * h:128 * h + 128, 0:1024]),
                    reads=[rD["k"]], writes=[r_ka], dma=True)
                b.op("sync", lambda E, va=va, h=h, hh=hh: E.dma_start(
                    va[:, hh * 1024:(hh + 1) * 1024].rearrange("p (kt d) -> p kt d", kt=8),
                    dram["v"].ap()[0:8, :, 128 * h:128 * h + 128]
                    .rearrange("kt p d -> p kt d"),),
                    reads=[rD["v"]], writes=[r_va], dma=True)
            for hh in range(2):
                h = 2 * hp + hh
                kb, r_kb = K2b[slot]
                vb, r_vb = V2b[slot]
                b.op("sync", lambda E, kb=kb, h=h, hh=hh: E.dma_start(
                    kb[:, hh * 3072:(hh + 1) * 3072],
                    dram["k"].ap()[128 * h:128 * h + 128, 1024:4096]),
                    reads=[rD["k"]], writes=[r_kb], dma=True)
                b.op("sync", lambda E, vb=vb, h=h, hh=hh: E.dma_start(
                    vb[:, hh * 3072:(hh + 1) * 3072].rearrange("p (kt d) -> p kt d", kt=24),
                    dram["v"].ap()[8:32, :, 128 * h:128 * h + 128]
                    .rearrange("kt p d -> p kt d"),),
                    reads=[rD["v"]], writes=[r_vb], dma=True)

        load_pair(0, 0)
        for hp in range(8):
            slot = hp % 2
            if hp + 1 < 8:
                load_pair(hp + 1, (hp + 1) % 2)
            wqt, r_wqt = wq[slot]
            for qc in range(4):
                qcols = slice(512 * qc, 512 * qc + 512)
                tq, r_tq = t1qc[qc % 2]
                b.op("scalar", lambda E, tq=tq, qcols=qcols: E.dma_start(
                    tq[:, :].rearrange("p (k n) -> p k n", k=KT),
                    dram["t1q"].ap()[:, qcols].rearrange("(k p) n -> p k n", p=128)),
                    reads=[rD["t1q"]], writes=[r_tq], dma=True)
                nkt = 8 * qc + 8
                for hh in range(2):
                    def mmq(E, wqt=wqt, tq=tq, hh=hh):
                        for k in range(KT):
                            ins = E.matmul(
                                pgA[:, 0:512],
                                wqt[:, k * 256 + hh * 128:k * 256 + hh * 128 + 128],
                                tq[:, k * 512:(k + 1) * 512],
                                start=(k == 0), stop=(k == KT - 1))
                        return ins
                    b.op("tensor", mmq, reads=[r_wqt, r_tq], writes=[rpgA])
                    qr, r_qr = qrot
                    qa, r_qa = qta
                    qb, r_qb = qtb
                    def qrots(E, qr=qr):
                        E.copy(qr[0:64, :], pgA[64:128, 0:512])
                        return E.copy(qr[64:128, :], pgA[0:64, 0:512])
                    b.op("scalar", qrots, reads=[rpgA], writes=[r_qr])
                    b.op("vector", lambda E, qa=qa, qcols=qcols: E.tensor_tensor(
                        qa[:, :], ropeCq[:, qcols], pgA[:, 0:512], op=ALU.mult),
                        reads=[rpgA, r_ropeCq], writes=[r_qa])
                    b.op("gpsimd", lambda E, qr=qr, qb=qb, qcols=qcols:
                         E.tensor_tensor(qb[:, :], qr[:, :], ropeSq[:, qcols],
                                         op=ALU.mult),
                         reads=[r_qr, r_ropeSq], writes=[r_qb])
                    qs, r_qs = qsb[hh]
                    b.op("vector", lambda E, qa=qa, qb=qb, qs=qs: E.tensor_tensor(
                        qs[:, :], qa[:, :], qb[:, :], op=ALU.add),
                        reads=[r_qa, r_qb], writes=[r_qs])
                for hh in range(2):
                    qs, r_qs = qsb[hh]
                    groups = [list(range(g, min(g + 3, nkt)))
                              for g in range(0, nkt, 3)]
                    pend = None

                    def sums_av(tiles, slot_idx, first, last):
                        pts, r_pts = PT[slot_idx]
                        vtensors = [kv_slice(slot, "v", hh, kt) for kt in tiles]
                        def mm2(E, tiles=tiles, pts=pts, first=first, last=last,
                                vtensors=vtensors):
                            for i, kt in enumerate(tiles):
                                E.matmul(pSum[0:1, :], ones_c[:, :],
                                         pts[:, i * 512:(i + 1) * 512],
                                         start=(first and i == 0),
                                         stop=(last and i == len(tiles) - 1))
                                v_t, _, voff = vtensors[i]
                                ins = E.matmul(
                                    pO[:, :], v_t[:, voff:voff + 128],
                                    pts[:, i * 512:(i + 1) * 512],
                                    start=(first and i == 0),
                                    stop=(last and i == len(tiles) - 1))
                            return ins
                        b.op("tensor", mm2,
                             reads=[r_ones_c, PT[slot_idx][1]] + [vv[1] for vv in vtensors],
                             writes=[rpSum, rpO])

                    for gi, tiles in enumerate(groups):
                        grp, rgrp = (pgA, rpgA) if gi % 2 == 0 else (pgB, rpgB)
                        wid = len(tiles) * 512
                        ktensors = [kv_slice(slot, "k", hh, kt) for kt in tiles]
                        def mms(E, tiles=tiles, grp=grp, qs=qs, qc=qc,
                                ktensors=ktensors):
                            for i, kt in enumerate(tiles):
                                sl = grp[:, i * 512:(i + 1) * 512]
                                kt_t, _, koff = ktensors[i]
                                if kt >= 8 * qc:
                                    E.matmul(sl, kt_t[:, koff:koff + 128],
                                             qs[:, :], start=True, stop=False)
                                    ins = E.matmul(
                                        sl, ident[:, :],
                                        mask[:, (kt - 8 * qc) * 512:
                                             (kt - 8 * qc) * 512 + 512],
                                        start=False, stop=True)
                                else:
                                    ins = E.matmul(
                                        sl, kt_t[:, koff:koff + 128],
                                        qs[:, :], start=True, stop=True)
                            return ins
                        b.op("tensor", mms,
                             reads=[r_qs, r_ident, r_mask] + [kk[1] for kk in ktensors],
                             writes=[rgrp])
                        if gi == 0 and G.get("b1_pending") is not None:
                            G["b1_pending"]()
                            G["b1_pending"] = None
                        pts, r_pts = PT[gi % 3]
                        b.op("scalar", lambda E, grp=grp, pts=pts, wid=wid:
                             E.activation(pts[:, 0:wid], grp[:, 0:wid], AF.Exp,
                                          bias=cst[:, 0:1], scale=ISCALE),
                             reads=[rgrp, r_cst], writes=[r_pts])
                        if pend is not None:
                            sums_av(*pend)
                        pend = (tiles, gi % 3,
                                gi == 0, gi == len(groups) - 1)
                    if pend is not None:
                        sums_av(*pend)

                    def mknorm(hh=hh, qc=qc):
                        def donorm():
                            rr, r_rr = rrow
                            b.op("vector", lambda E, rr=rr: E.reciprocal(
                                rr[0:1, :], pSum[0:1, :]),
                                reads=[rpSum], writes=[r_rr])
                            def mmbc(E, rr=rr):
                                return E.matmul(pSum[:, :], ones_rf[0:1, :],
                                                rr[0:1, :], start=True, stop=True)
                            b.op("tensor", mmbc, reads=[r_ones_rf, r_rr],
                                 writes=[rpSum])
                            rb, r_rb = rbo
                            b.op("vector", lambda E, rb=rb: E.tensor_copy(
                                rb[:, :], pSum[:, :]), reads=[rpSum], writes=[r_rb])
                            Op, r_Op = Opair
                            b.op("vector", lambda E, rb=rb, Op=Op, hh=hh, qc=qc:
                                 E.tensor_tensor(
                                     Op[:, hh * TL + qc * 512:hh * TL + qc * 512 + 512],
                                     rb[:, :], pO[:, :], op=ALU.mult),
                                 reads=[rpO, r_rb], writes=[r_Op])
                        return donorm
                    G["b1_pending"] = mknorm()
            if G.get("b1_pending") is not None:
                G["b1_pending"]()
                G["b1_pending"] = None
            Op, r_Op = Opair
            for hh in range(2):
                h = 2 * hp + hh
                b.op("scalar", lambda E, Op=Op, h=h, hh=hh: E.dma_start(
                    dram["o"].ap()[128 * h:128 * h + 128, :],
                    Op[:, hh * TL:(hh + 1) * TL]),
                    reads=[r_Op], writes=[rD["o"]], dma=True)

    b.barrier()

    with ExitStack() as pc:
        def psb(name, shape, dt):
            t = pc.enter_context(nc.sbuf_tensor(name, shape, dt))
            return t, b.res(name)

        def pps(name, shape, dt=F32):
            t = pc.enter_context(nc.psum_tensor(name, shape, dt))
            return t, b.res(name, xread=True)

        bS, rbS = pps("pb2_s", [128, 512])
        bQ, rbQ = pps("pb2_q", [128, 512])
        bW = [pps("pb2_w0", [128, 256]), pps("pb2_w1", [128, 256])]

        wo, r_wo = psb("wo_sb", [128, KT * 2048], BF16)
        b.op("gpsimd", lambda E: E.dma_start(
            wo[:, :].rearrange("p (k e) -> p k e", k=KT),
            DI["woT"].ap().rearrange("(k p) e -> p k e", p=128)),
            writes=[r_wo], dma=True)
        o2, r_o2 = psb("o2_0", [128, KT * 256], BF16)
        xqf, r_xqf = psb("xqf0", [128, KT * 256], F32)
        attf = [psb("attf0", [128, KT * 256], F32), psb("attf1", [128, KT * 256], F32)]
        attb = [psb("attb0", [128, KT * 256], BF16), psb("attb1", [128, KT * 256], BF16)]
        x2f, r_x2f = psb("x2f", [128, KT * 256], F32)
        x2b, r_x2b = psb("x2b16", [128, KT * 256], BF16)
        t2o, r_t2o = psb("t2o", [128, KT * 256], BF16)

        def b2_wo(u, mid=None):
            cols = slice(256 * u, 256 * u + 256)
            b.op("sync", lambda E, cols=cols: E.dma_start(
                o2[:, :].rearrange("p (k n) -> p k n", k=KT),
                dram["o"].ap()[:, cols].rearrange("(k p) n -> p k n", p=128)),
                reads=[rD["o"]], writes=[r_o2], dma=True)
            atf, r_atf = attf[u % 2]
            atb, r_atb = attb[u % 2]
            for dt in range(KT):
                if mid is not None and dt in mid:
                    mid[dt]()
                pw, rpw = bW[dt % 2]
                def mmo(E, dt=dt, pw=pw):
                    for k in range(KT):
                        ins = E.matmul(
                            pw[:, :],
                            wo[:, k * 2048 + dt * 128:k * 2048 + dt * 128 + 128],
                            o2[:, k * 256:(k + 1) * 256],
                            start=(k == 0), stop=(k == KT - 1))
                    return ins
                b.op("tensor", mmo, reads=[r_wo, r_o2], writes=[rpw])
                b.op("scalar", lambda E, pw=pw, atf=atf, dt=dt: E.copy(
                    atf[:, dt * 256:(dt + 1) * 256], pw[:, :]),
                    reads=[rpw], writes=[r_atf])
                b.op("vector", lambda E, pw=pw, atb=atb, dt=dt: E.tensor_copy(
                    atb[:, dt * 256:(dt + 1) * 256], pw[:, :]),
                    reads=[rpw], writes=[r_atb])

        def b2_ln_a(u):
            atb, r_atb = attb[u % 2]
            b.op("sync", lambda E, u=u: E.dma_start(
                xqf[:, :].rearrange("p (k n) -> p k n", k=KT),
                DI["xTqc"].ap()[u // 2]
                .rearrange("p (k n) -> p k n", k=KT)[:, :, (u % 2) * 256:(u % 2) * 256 + 256]),
                writes=[r_xqf], dma=True)
            ln_stats(G, atb, r_atb, 256, bS, rbS, bQ, rbQ)

        def b2_ln_b(u):
            cols = slice(256 * u, 256 * u + 256)
            atf, r_atf = attf[u % 2]
            rb, r_rb, mb, r_mb = ln_bcast(G, 256, bS, rbS, bQ, rbQ, f32=True)
            b.op("vector", lambda E, atf=atf, rb=rb: E.tensor_tensor(
                x2f[:, :].rearrange("p (k n) -> p k n", k=KT),
                atf[:, :].rearrange("p (k n) -> p k n", k=KT),
                rb[:, None, 0:256].broadcast_to([128, KT, 256]), op=ALU.mult),
                reads=[r_atf, r_rb], writes=[r_x2f])
            b.op("vector", lambda E, mb=mb: E.tensor_tensor(
                x2f[:, :].rearrange("p (k n) -> p k n", k=KT),
                x2f[:, :].rearrange("p (k n) -> p k n", k=KT),
                mb[:, None, 0:256].broadcast_to([128, KT, 256]), op=ALU.subtract),
                reads=[r_x2f, r_mb], writes=[r_x2f])
            b.op("vector", lambda E: E.tensor_tensor(
                x2f[:, :], x2f[:, :], xqf[:, :], op=ALU.add),
                reads=[r_x2f, r_xqf], writes=[r_x2f])
            b.op("scalar", lambda E, cols=cols: E.dma_start(
                dram["x2"].ap()[:, cols].rearrange("(k p) n -> p k n", p=128),
                x2f[:, :].rearrange("p (k n) -> p k n", k=KT)),
                reads=[r_x2f], writes=[rD["x2"]], dma=True)

        def b2_ln_c(u):
            b.op("scalar", lambda E: E.copy(x2b[:, :], x2f[:, :]),
                 reads=[r_x2f], writes=[r_x2b])
            ln_stats(G, x2b, r_x2b, 256, bS, rbS, bQ, rbQ)

        def b2_ln_d(u):
            cols = slice(256 * u, 256 * u + 256)
            ln_bcast(G, 256, bS, rbS, bQ, rbQ, f32=False)
            ln_apply_bf16(G, x2b, r_x2b, t2o, r_t2o, 0, 256)
            b.op("scalar", lambda E, cols=cols: E.dma_start(
                dram["t2"].ap()[:, cols].rearrange("(k p) n -> p k n", p=128),
                t2o[:, :].rearrange("p (k n) -> p k n", k=KT)),
                reads=[r_t2o], writes=[rD["t2"]], dma=True)

        def mids(u):
            return {2: lambda: b2_ln_a(u - 1), 6: lambda: b2_ln_b(u - 1),
                    9: lambda: b2_ln_c(u - 1), 13: lambda: b2_ln_d(u - 1)}

        for u in range(9):
            if u < 8:
                b2_wo(u, mid=mids(u) if u >= 1 else None)
            else:
                b2_ln_a(7); b2_ln_b(7); b2_ln_c(7); b2_ln_d(7)



def phase_C(G):
    nc, b, DI = G["nc"], G["b"], G["DI"]
    dram, rD = G["dram"], G["rD"]
    cst, r_cst = G["cst"], G["r_cst"]
    ident, r_ident = G["ident"], G["r_ident"]
    ones_c, r_ones_c = G["ones_c"], G["r_ones_c"]
    ones_r, r_ones_r = G["ones_r"], G["r_ones_r"]

    with ExitStack() as pc:
        def psb(name, shape, dt):
            t = pc.enter_context(nc.sbuf_tensor(name, shape, dt))
            return t, b.res(name)

        def pps(name, shape, dt=F32):
            t = pc.enter_context(nc.psum_tensor(name, shape, dt))
            return t, b.res(name, xread=True)

        bH1 = [pps("pc_h1a", [128, 512]), pps("pc_h1b", [128, 512])]
        bH3 = [pps("pc_h3a", [128, 512]), pps("pc_h3b", [128, 512])]
        bY = [pps("pc_ya", [128, 512]), pps("pc_yb", [128, 512])]
        bS, rbS = pps("pc_s", [128, 512])
        bQ, rbQ = pps("pc_q", [128, 512])

        t2c, r_t2c = psb("t2c", [128, KT * 1024], BF16)
        ysb, r_ysb = psb("ysb", [128, KT * 1024], F32)
        hsb, r_hsb = psb("hsb", [128, IT * 1024], BF16)
        w1b = [psb(f"w1b{i}", [128, KT * 128], BF16) for i in range(3)]
        w3b = [psb(f"w3b{i}", [128, KT * 128], BF16) for i in range(3)]
        w2b = [psb(f"w2b{i}", [128, IT * 256], BF16) for i in range(2)]
        silu = [psb("silu0", [128, 512], BF16), psb("silu1", [128, 512], BF16)]
        gw, r_gw = psb("gw_sb", [128, KT * 16], BF16)
        glog, r_glog = psb("glog", [128, 16], F32)
        gsc, r_gsc = psb("gsc", [128, 24], F32)
        cbf, r_cbf = psb("cbf", [128, 16], BF16)
        crs, r_crs = psb("crs", [1, 1024], BF16)
        crl, r_crl = psb("crl", [1, 1024], BF16)
        cbc, r_cbc = psb("cbc", [128, 1024], BF16)
        xo = [psb("xo0", [128, 512], F32), psb("xo1", [128, 512], F32)]
        oo = [psb("oo0", [128, 512], F32), psb("oo1", [128, 512], F32)]
        ybf = [psb("ybf0", [128, 512], BF16), psb("ybf1", [128, 512], BF16)]

        b.op("gpsimd", lambda E: E.dma_start(
            gw[:, :], DI["gwS"].ap()), writes=[r_gw], dma=True)

        def load_t2c(cc):
            ccols = slice(1024 * cc, 1024 * cc + 1024)
            b.op("sync", lambda E, ccols=ccols: E.dma_start(
                t2c[:, :].rearrange("p (k n) -> p k n", k=KT),
                dram["t2"].ap()[:, ccols].rearrange("(k p) n -> p k n", p=128)),
                reads=[rD["t2"]], writes=[r_t2c], dma=True)

        load_t2c(0)
        for cc in range(2):

            for tt in range(8):
                def mmg(E, tt=tt):
                    for k in range(KT):
                        ins = E.matmul(
                            bS[:, 0:16],
                            t2c[:, k * 1024 + tt * 128:k * 1024 + tt * 128 + 128],
                            gw[:, k * 16:(k + 1) * 16],
                            start=(k == 0), stop=(k == KT - 1))
                    return ins
                b.op("tensor", mmg, reads=[r_t2c, r_gw], writes=[rbS])
                b.op("vector", lambda E: E.tensor_copy(glog[:, :], bS[:, 0:16]),
                     reads=[rbS], writes=[r_glog])
                b.op("vector", lambda E: E.reduce_max(
                    gsc[:, 0:1], glog[:, :], axis=AX, negate=True),
                    reads=[r_glog], writes=[r_gsc])
                b.op("scalar", lambda E: E.activation(
                    glog[:, :], glog[:, :], AF.Exp, bias=gsc[:, 0:1], scale=1.0,
                    accum_out=gsc[:, 1:2]),
                    reads=[r_glog, r_gsc], writes=[r_glog, r_gsc])
                b.op("vector", lambda E: E.reciprocal(gsc[:, 2:3], gsc[:, 1:2]),
                     reads=[r_gsc], writes=[r_gsc])
                b.op("vector", lambda E: E.tensor_scalar(
                    out=glog[:, :], in0=glog[:, :], scalar1=gsc[:, 2:3],
                    scalar2=None, op0=ALU.mult),
                    reads=[r_glog, r_gsc], writes=[r_glog])
                b.op("vector", lambda E: E.reduce_max(
                    gsc[:, 3:4], glog[:, :], axis=AX),
                    reads=[r_glog], writes=[r_gsc])
                b.op("vector", lambda E: E.tensor_scalar(
                    out=gsc[:, 4:20], in0=glog[:, :], scalar1=gsc[:, 3:4],
                    scalar2=None, op0=ALU.is_ge),
                    reads=[r_glog, r_gsc], writes=[r_gsc])
                b.op("vector", lambda E: E.tensor_scalar(
                    out=gsc[:, 4:20], in0=gsc[:, 4:20], scalar1=-2.0,
                    scalar2=None, op0=ALU.mult),
                    reads=[r_gsc], writes=[r_gsc])
                b.op("vector", lambda E: E.tensor_tensor(
                    gsc[:, 4:20], gsc[:, 4:20], glog[:, :], op=ALU.add),
                    reads=[r_gsc, r_glog], writes=[r_gsc])
                b.op("vector", lambda E: E.reduce_max(
                    gsc[:, 20:21], gsc[:, 4:20], axis=AX),
                    reads=[r_gsc], writes=[r_gsc])
                b.op("vector", lambda E: E.tensor_scalar(
                    out=gsc[:, 4:20], in0=glog[:, :], scalar1=gsc[:, 20:21],
                    scalar2=None, op0=ALU.is_ge),
                    reads=[r_glog, r_gsc], writes=[r_gsc])
                b.op("vector", lambda E: E.tensor_tensor(
                    glog[:, :], glog[:, :], gsc[:, 4:20], op=ALU.mult),
                    reads=[r_glog, r_gsc], writes=[r_glog])
                b.op("vector", lambda E: E.tensor_copy(cbf[:, :], glog[:, :]),
                     reads=[r_glog], writes=[r_cbf])
                for eh in range(2):
                    def mmtr(E, eh=eh):
                        for j in range(8):
                            ins = E.transpose(
                                bQ.ap().bitcast(BF16)[0:1, j * 128:(j + 1) * 128],
                                cbf[:, eh * 8 + j:eh * 8 + j + 1], ident[:, :])
                        return ins
                    b.op("tensor", mmtr, reads=[r_cbf, r_ident], writes=[rbQ])
                    def evtr(E, eh=eh):
                        for j in range(8):
                            ins = E.tensor_copy(
                                crs[0:1, j * 128:(j + 1) * 128],
                                bQ.ap().bitcast(BF16)[0:1, j * 128:(j + 1) * 128])
                        return ins
                    b.op("vector", evtr, reads=[rbQ], writes=[r_crs])
                    b.op("scalar", lambda E, tt=tt, eh=eh: E.dma_start(
                        dram["cr"].ap()[eh * 8:(eh + 1) * 8,
                                        tt * 128:(tt + 1) * 128].unsqueeze(0),
                        crs[0:1, :].rearrange("o (j n) -> o j n", j=8)),
                        reads=[r_crs], writes=[rD["cr"]], dma=True)

            for e in range(NEXP):
                hooks = G.pop("c_hooks", None) if e == 0 else None
                if e < NE:
                    src1 = DI["w1S"].ap()[e]
                    src3 = DI["w3S"].ap()[e]
                    src2 = DI["w2S"].ap()[e]
                else:
                    sh = e - NE
                    src1 = DI["sw1S"].ap()[sh]
                    src3 = DI["sw3S"].ap()[sh]
                    src2 = DI["sw2S"].ap()[sh]

                if e < NE:
                    b.op("sync", lambda E, e=e: E.dma_start(
                        crl[0:1, :], dram["cr"].ap()[e:e + 1, :]),
                        reads=[rD["cr"]], writes=[r_crl], dma=True)
                    def mmcb(E):
                        E.matmul(bS[:, 0:512], ones_r[0:1, :],
                                 crl[0:1, 0:512], start=True, stop=True)
                        return E.matmul(bQ[:, 0:512], ones_r[0:1, :],
                                        crl[0:1, 512:1024], start=True, stop=True)
                    b.op("tensor", mmcb, reads=[r_ones_r, r_crl],
                         writes=[rbS, rbQ])
                    b.op("vector", lambda E: E.tensor_copy(
                        cbc[:, 0:512], bS[:, 0:512]), reads=[rbS], writes=[r_cbc])
                    b.op("vector", lambda E: E.tensor_copy(
                        cbc[:, 512:1024], bQ[:, 0:512]), reads=[rbQ], writes=[r_cbc])

                for it in range(IT):
                    if hooks and it in hooks:
                        hooks.pop(it)()
                    gi = e * IT + it
                    w1t, r_w1t = w1b[gi % 3]
                    w3t, r_w3t = w3b[gi % 3]
                    b.op("gpsimd", lambda E, w1t=w1t, src1=src1, it=it: E.dma_start(
                        w1t[:, :], src1[it]), writes=[r_w1t], dma=True)
                    b.op("gpsimd", lambda E, w3t=w3t, src3=src3, it=it: E.dma_start(
                        w3t[:, :], src3[it]), writes=[r_w3t], dma=True)
                    for sub in range(2):
                        p1, rp1 = bH1[(2 * it + sub) % 2]
                        p3, rp3 = bH3[(2 * it + sub) % 2]
                        def mmh(E, w1t=w1t, w3t=w3t, sub=sub, p1=p1, p3=p3):
                            for k in range(KT):
                                E.matmul(p1[:, :], w1t[:, k * 128:(k + 1) * 128],
                                         t2c[:, k * 1024 + sub * 512:
                                             k * 1024 + sub * 512 + 512],
                                         start=(k == 0), stop=(k == KT - 1))
                            for k in range(KT):
                                ins = E.matmul(
                                    p3[:, :], w3t[:, k * 128:(k + 1) * 128],
                                    t2c[:, k * 1024 + sub * 512:
                                        k * 1024 + sub * 512 + 512],
                                    start=(k == 0), stop=(k == KT - 1))
                            return ins
                        b.op("tensor", mmh, reads=[r_w1t, r_w3t, r_t2c],
                             writes=[rp1, rp3])
                        sl, r_sl = silu[sub]
                        b.op("scalar", lambda E, p1=p1, sl=sl: E.activation(
                            sl[:, :], p1[:, :], AF.Silu),
                            reads=[rp1], writes=[r_sl])
                        hoff = it * 1024 + sub * 512
                        if e < NE:
                            b.op("vector", lambda E, sl=sl, p3=p3:
                                 E.tensor_tensor(sl[:, :], sl[:, :], p3[:, :],
                                                 op=ALU.mult),
                                 reads=[r_sl, rp3], writes=[r_sl])
                            b.op("vector", lambda E, sl=sl, hoff=hoff, sub=sub:
                                 E.tensor_tensor(
                                     hsb[:, hoff:hoff + 512], sl[:, :],
                                     cbc[:, sub * 512:sub * 512 + 512],
                                     op=ALU.mult),
                                 reads=[r_sl, r_cbc], writes=[r_hsb])
                        else:
                            b.op("vector", lambda E, sl=sl, p3=p3, hoff=hoff:
                                 E.tensor_tensor(hsb[:, hoff:hoff + 512],
                                                 sl[:, :], p3[:, :], op=ALU.mult),
                                 reads=[r_sl, rp3], writes=[r_hsb])
                if e == NEXP - 1 and cc == 0:
                    load_t2c(1)
                for q8 in range(8):
                    w2t, r_w2t = w2b[q8 % 2]
                    b.op("gpsimd", lambda E, w2t=w2t, src2=src2, q8=q8: E.dma_start(
                        w2t[:, :], src2[q8]), writes=[r_w2t], dma=True)
                    for dt in range(2):
                        dtg = q8 * 2 + dt
                        for sub in range(2):
                            py, rpy = bY[(2 * dt + sub) % 2]
                            def mmy(E, dt=dt, sub=sub, py=py, w2t=w2t):
                                for i in range(IT):
                                    ins = E.matmul(
                                        py[:, :],
                                        w2t[:, i * 256 + dt * 128:
                                            i * 256 + dt * 128 + 128],
                                        hsb[:, i * 1024 + sub * 512:
                                            i * 1024 + sub * 512 + 512],
                                        start=(i == 0), stop=(i == IT - 1))
                                return ins
                            b.op("tensor", mmy, reads=[r_w2t, r_hsb], writes=[rpy])
                            yoff = dtg * 1024 + sub * 512
                            if e == 0:
                                b.op("vector", lambda E, py=py, yoff=yoff:
                                     E.tensor_copy(ysb[:, yoff:yoff + 512],
                                                   py[:, :]),
                                     reads=[rpy], writes=[r_ysb])
                            else:
                                b.op("vector", lambda E, py=py, yoff=yoff:
                                     E.tensor_tensor(ysb[:, yoff:yoff + 512],
                                                     ysb[:, yoff:yoff + 512],
                                                     py[:, :], op=ALU.add),
                                     reads=[rpy, r_ysb], writes=[r_ysb])

            def ln4_a(sub, cc=cc):
                srow, r_srow = G["srow"], G["r_srow"]
                cstl, r_cstl = G["cst"], G["r_cst"]
                for k in range(KT):
                    yb, r_yb = ybf[k % 2]
                    yoff = k * 1024 + sub * 512
                    b.op("scalar", lambda E, yb=yb, yoff=yoff: E.copy(
                        yb[:, :], ysb[:, yoff:yoff + 512]),
                        reads=[r_ysb], writes=[r_yb])
                    b.op("tensor", lambda E, yb=yb, k=k: E.matmul(
                        bS[0:1, 0:512], ones_c[:, :], yb[:, :],
                        start=(k == 0), stop=(k == KT - 1)),
                        reads=[r_yb, r_ones_c], writes=[rbS])
                    b.op("vector", lambda E, yb=yb: E.tensor_tensor(
                        yb[:, :], yb[:, :], yb[:, :], op=ALU.mult),
                        reads=[r_yb], writes=[r_yb])
                    b.op("tensor", lambda E, yb=yb, k=k: E.matmul(
                        bQ[0:1, 0:512], ones_c[:, :], yb[:, :],
                        start=(k == 0), stop=(k == KT - 1)),
                        reads=[r_yb, r_ones_c], writes=[rbQ])
                b.op("vector", lambda E: E.tensor_scalar(
                    out=srow[0:1, 1024:1024 + 512], in0=bS[0:1, 0:512],
                    scalar1=1.0 / D, scalar2=None, op0=ALU.mult),
                    reads=[rbS], writes=[r_srow])
                b.op("vector", lambda E: E.tensor_scalar(
                    out=srow[0:1, 0:512], in0=bQ[0:1, 0:512],
                    scalar1=1.0 / D, scalar2=None, op0=ALU.mult),
                    reads=[rbQ], writes=[r_srow])
                b.op("vector", lambda E: E.tensor_tensor(
                    srow[0:1, 512:1024], srow[0:1, 1024:1536],
                    srow[0:1, 1024:1536], op=ALU.mult),
                    reads=[r_srow], writes=[r_srow])
                b.op("vector", lambda E: E.tensor_tensor(
                    srow[0:1, 0:512], srow[0:1, 0:512], srow[0:1, 512:1024],
                    op=ALU.subtract), reads=[r_srow], writes=[r_srow])
                b.op("scalar", lambda E: E.activation(
                    srow[0:1, 512:1024], srow[0:1, 0:512], AF.Ln,
                    bias=cstl[0:1, 1:2], scale=1.0),
                    reads=[r_srow, r_cstl], writes=[r_srow])
                b.op("scalar", lambda E: E.activation(
                    srow[0:1, 1536:2048], srow[0:1, 512:1024], AF.Exp,
                    bias=cstl[0:1, 2:3], scale=-0.5),
                    reads=[r_srow, r_cstl], writes=[r_srow])
                b.op("vector", lambda E: E.tensor_tensor(
                    srow[0:1, 2048:2560], srow[0:1, 1024:1536],
                    srow[0:1, 1536:2048], op=ALU.mult),
                    reads=[r_srow], writes=[r_srow])

            def ln4_b(sub, cc=cc):
                scols = slice(1024 * cc + 512 * sub, 1024 * cc + 512 * sub + 512)
                rb, r_rb, mb, r_mb = ln_bcast(G, 512, bS, rbS, bQ, rbQ, f32=True)
                for dt in range(KT):
                    xt, r_xt = xo[dt % 2]
                    ot, r_ot = oo[dt % 2]
                    b.op("sync", lambda E, xt=xt, dt=dt, scols=scols: E.dma_start(
                        xt[:, :], dram["x2"].ap()[128 * dt:128 * dt + 128, scols]),
                        reads=[rD["x2"]], writes=[r_xt], dma=True)
                    yoff = dt * 1024 + sub * 512
                    b.op("vector", lambda E, ot=ot, yoff=yoff, rb=rb:
                         E.tensor_tensor(ot[:, :], ysb[:, yoff:yoff + 512],
                                         rb[:, 0:512], op=ALU.mult),
                         reads=[r_ysb, r_rb], writes=[r_ot])
                    b.op("vector", lambda E, ot=ot, mb=mb: E.tensor_tensor(
                        ot[:, :], ot[:, :], mb[:, 0:512], op=ALU.subtract),
                        reads=[r_ot, r_mb], writes=[r_ot])
                    b.op("vector", lambda E, ot=ot, xt=xt: E.tensor_tensor(
                        ot[:, :], ot[:, :], xt[:, :], op=ALU.add),
                        reads=[r_ot, r_xt], writes=[r_ot])
                    b.op("scalar", lambda E, ot=ot, dt=dt, scols=scols: E.dma_start(
                        dram["out"].ap()[128 * dt:128 * dt + 128, scols], ot[:, :]),
                        reads=[r_ot], writes=[rD["out"]], dma=True)

            if cc == 0:
                G["c_hooks"] = {1: lambda: ln4_a(0), 3: lambda: ln4_b(0),
                                6: lambda: ln4_a(1), 8: lambda: ln4_b(1)}
            else:
                for sub in range(2):
                    ln4_a(sub)
                    ln4_b(sub)



def _host_prep(inputs):
    import ml_dtypes
    x = np.asarray(inputs["L0_attn_x"], dtype=np.float32)
    shared = {}

    def tile_w1like(w):
        E_ = w.shape[0]
        r = w.reshape(E_, IT, 128, KT, 128)
        r = r.transpose(0, 1, 4, 3, 2)
        return np.ascontiguousarray(r.reshape(E_, IT, 128, KT * 128))

    def tile_w2like(w):
        E_ = w.shape[0]
        r = w.reshape(E_, 8, 256, IT, 128)
        r = r.transpose(0, 1, 4, 3, 2)
        return np.ascontiguousarray(r.reshape(E_, 8, 128, IT * 256))

    wkT = np.asarray(inputs["L0_attn_Wk"], dtype=np.float32).T
    wkS = wkT.reshape(KT, 128, 16, 128).transpose(2, 1, 0, 3).reshape(16, 128, KT * 128)
    shared["wkS4"] = np.ascontiguousarray(
        wkS.reshape(4, 4, 128, KT * 128).transpose(0, 2, 1, 3).reshape(4, 128, 4 * KT * 128))
    wvT = np.asarray(inputs["L0_attn_Wv"], dtype=np.float32).T
    shared["wvS"] = np.ascontiguousarray(
        wvT.reshape(KT, 128, 4, 512).transpose(2, 1, 0, 3).reshape(4, 128, KT * 512))
    shared["woT"] = np.ascontiguousarray(np.asarray(inputs["L0_attn_Wo"], dtype=np.float32).T)
    wqT = np.asarray(inputs["L0_attn_Wq"], dtype=np.float32).T
    r = wqT.reshape(KT, 128, 8, 256).transpose(2, 1, 0, 3)
    shared["wqS"] = np.ascontiguousarray(r.reshape(8, 128, KT * 256))
    gwT = np.asarray(inputs["L0_ffn_gate_w"], dtype=np.float32).T
    shared["gwS"] = np.ascontiguousarray(
        gwT.reshape(KT, 128, 16).transpose(1, 0, 2).reshape(128, KT * 16))
    shared["w1S"] = tile_w1like(np.asarray(inputs["L0_ffn_w1"], dtype=np.float32))
    shared["w3S"] = tile_w1like(np.asarray(inputs["L0_ffn_w3"], dtype=np.float32))
    shared["w2S"] = tile_w2like(np.asarray(inputs["L0_ffn_w2"], dtype=np.float32))
    sw1 = np.asarray(inputs["L0_ffn_sw1"], dtype=np.float32).reshape(2, INTER, D)
    sw3 = np.asarray(inputs["L0_ffn_sw3"], dtype=np.float32).reshape(2, INTER, D)
    shared["sw1S"] = tile_w1like(sw1)
    shared["sw3S"] = tile_w1like(sw3)
    sw2 = np.asarray(inputs["L0_ffn_sw2"], dtype=np.float32)
    sw2e = np.stack([sw2[:, 0:INTER], sw2[:, INTER:2 * INTER]], axis=0)
    shared["sw2S"] = tile_w2like(sw2e)

    half = 64
    pos = np.arange(T, dtype=np.float32)
    inv_freq = 1.0 / (10000.0 ** (np.arange(half, dtype=np.float32) / half))
    ang = inv_freq[:, None] * pos[None, :]
    sin, cos = np.sin(ang), np.cos(ang)
    Ck = np.ascontiguousarray(np.concatenate([cos, cos], axis=0).astype(np.float32))
    Sk = np.ascontiguousarray(np.concatenate([-sin, sin], axis=0).astype(np.float32))
    shared["ropeCk"] = Ck
    shared["ropeSk"] = Sk
    shared["identity"] = np.eye(128, dtype=ml_dtypes.bfloat16)

    per_core = []
    for c in range(8):
        bidx, o = c // 2, c % 2
        idx = np.concatenate([np.arange(512 * (2 * qc + o), 512 * (2 * qc + o) + 512)
                              for qc in range(4)])
        m = dict(shared)
        xb = x[bidx].reshape(8, 512, KT, 128)
        m["xTc"] = np.ascontiguousarray(
            xb.transpose(0, 3, 2, 1).reshape(8, 128, KT * 512))
        xq = x[bidx][idx].reshape(4, 512, KT, 128)
        m["xTqc"] = np.ascontiguousarray(
            xq.transpose(0, 3, 2, 1).reshape(4, 128, KT * 512))
        m["ropeCq"] = np.ascontiguousarray(Ck[:, idx])
        m["ropeSq"] = np.ascontiguousarray(Sk[:, idx])
        j = np.arange(8)[:, None, None]
        p = np.arange(128)[None, :, None]
        n = np.arange(512)[None, None, :]
        mask = np.where(n >= p + 128 * j - 512 * o, 0.0, -1e9)
        m["maskT"] = np.ascontiguousarray(mask.astype(ml_dtypes.bfloat16))
        per_core.append((m, idx))
    return per_core


def kernel(**inputs):
    per_core = _host_prep(inputs)
    nc = build()
    in_maps = [m for (m, _) in per_core]
    res = run_bass_kernel_spmd(nc, in_maps, core_ids=list(range(8)))
    out = np.empty((NB, T, D), dtype=np.float32)
    for c in range(8):
        bidx = c // 2
        _, idx = per_core[c]
        out[bidx, idx, :] = np.asarray(res.results[c]["outT"], dtype=np.float32).T
    return out


# version 08 (final)
# speedup vs baseline: 1.1410x; 1.0367x over previous
import numpy as np
from contextlib import ExitStack

import concourse.bass as bass
import concourse.mybir as mybir
from concourse.bass_utils import run_bass_kernel_spmd

F32 = mybir.dt.float32
BF16 = mybir.dt.bfloat16
AF = mybir.ActivationFunctionType
ALU = mybir.AluOpType
AX = mybir.AxisListType.X

D = 2048
T = 4096
NB = 4
NH = 16
DH = 128
KT = 16
TL = 2048
INTER = 1408
IT = 11
NE = 16
NEXP = 18
EPS = 1e-5
EXP_BIAS = -15.0
ISCALE = float(1.0 / np.sqrt(DH))

DEBUG_SCRATCH = False
PHASES = "ABC"
A_CHUNKS = 8
A2_ON = True
A_STAGE = 9
A_ROPE = 4
PREFETCH_C = True

ENGS = ("tensor", "scalar", "vector", "gpsimd", "sync")


class Res:

    __slots__ = ("name", "writer", "readers", "dsem", "xread")

    def __init__(self, name, xread=False):
        self.name = name
        self.writer = None
        self.readers = {}
        self.dsem = None
        self.xread = xread


class B:

    def __init__(self, nc, ctx):
        self.nc = nc
        self.ctx = ctx
        self.streams = {e: [] for e in ENGS}
        self.sem = {e: self._mk_sem("sm_" + e) for e in ENGS}
        self.semv = {s: 0 for s in self.sem.values()}
        self.hw = {e: {} for e in ENGS}
        self._ndsem = 0

    def _mk_sem(self, name):
        return self.ctx.enter_context(self.nc.semaphore(name))

    def res(self, name, xread=False):
        return Res(name, xread)

    def _wait(self, eng, s, v):
        if v <= self.hw[eng].get(s, 0):
            return
        self.hw[eng][s] = v
        self.streams[eng].append(lambda E, s=s, v=v: E.wait_ge(s, v))

    def op(self, eng, fn, reads=(), writes=(), dma=False):
        assert eng in ENGS
        for r in reads:
            if r.xread:
                for s2, (v2, reng2) in r.readers.items():
                    if reng2 != eng:
                        self._wait(eng, s2, v2)
            w = r.writer
            if w is None:
                continue
            s, v, weng = w
            if weng == eng == "tensor":
                continue
            if weng == eng and not dma:
                self._wait(eng, s, v)
            else:
                self._wait(eng, s, v)
        for wres in writes:
            for s, (v, reng) in wres.readers.items():
                if reng == eng == "tensor":
                    continue
                self._wait(eng, s, v)
            w = wres.writer
            if w is not None:
                s, v, weng = w
                if weng == eng == "tensor":
                    pass
                else:
                    self._wait(eng, s, v)
        if dma:
            assert len(writes) == 1, "each DMA must write exactly one resource"
            wres = writes[0]
            if wres.dsem is None:
                wres.dsem = self._mk_sem(f"sd{self._ndsem}_{wres.name}")
                self._ndsem += 1
                self.semv[wres.dsem] = 0
            s = wres.dsem
            self.semv[s] += 16
            v = self.semv[s]
            def run(E, fn=fn, s=s):
                ins = fn(E)
                ins.then_inc(s, 16)
            self.streams[eng].append(run)
        else:
            s = self.sem[eng]
            self.semv[s] += 1
            v = self.semv[s]
            def run(E, fn=fn, s=s):
                ins = fn(E)
                ins.then_inc(s, 1)
            self.streams[eng].append(run)
        for wres in writes:
            wres.writer = (s, v, eng)
            wres.readers = {}
        for r in reads:
            cur = r.readers.get(s)
            if cur is None or cur[0] < v:
                r.readers[s] = (v, eng)
        return (s, v)

    def barrier(self):
        marks = [(s, v) for s, v in self.semv.items() if v > 0]
        for e in ENGS:
            for s, v in marks:
                self._wait(e, s, v)

    def emit(self):
        with self.nc.Block() as block:
            @block.tensor
            def _(E):
                for fn in self.streams["tensor"]:
                    fn(E)

            @block.scalar
            def _(E):
                for fn in self.streams["scalar"]:
                    fn(E)

            @block.vector
            def _(E):
                for fn in self.streams["vector"]:
                    fn(E)

            @block.gpsimd
            def _(E):
                for fn in self.streams["gpsimd"]:
                    fn(E)

            @block.sync
            def _(E):
                for fn in self.streams["sync"]:
                    fn(E)



def build():
    nc = bass.Bass("TRN2", target_bir_lowering=False)

    DI = dict()

    def din(name, shape, dt=F32):
        DI[name] = nc.dram_tensor(name, shape, dt, kind="ExternalInput")
        return DI[name]

    din("xTc", [8, 128, KT * 512])
    din("xTqc", [4, 128, KT * 512])
    din("wqS", [8, 128, KT * 256])
    din("wkS4", [4, 128, 4 * KT * 128]); din("wvS", [4, 128, KT * 512]); din("woT", [D, D])
    din("gwS", [128, KT * 16])
    din("w1S", [NE, IT, 128, KT * 128])
    din("w3S", [NE, IT, 128, KT * 128])
    din("w2S", [NE, 8, 128, IT * 256])
    din("sw1S", [2, IT, 128, KT * 128])
    din("sw3S", [2, IT, 128, KT * 128])
    din("sw2S", [2, 8, 128, IT * 256])
    din("ropeCk", [128, T]); din("ropeSk", [128, T])
    din("ropeCq", [128, TL]); din("ropeSq", [128, TL])
    din("maskT", [8, 128, 512], BF16)
    din("identity", [128, 128], BF16)

    skind = "ExternalOutput" if DEBUG_SCRATCH else "Internal"
    d_t1 = nc.dram_tensor("t1_hbm", [D, T], BF16, kind=skind)
    d_k = nc.dram_tensor("k_hbm", [D, T], BF16, kind=skind)
    d_v = nc.dram_tensor("v_hbm", [T // 128, 128, D], BF16, kind=skind)
    d_t1q = nc.dram_tensor("t1q_hbm", [D, TL], BF16, kind=skind)
    d_o = nc.dram_tensor("o_hbm", [D, TL], BF16, kind=skind)
    d_x2 = nc.dram_tensor("x2_hbm", [D, TL], F32, kind=skind)
    d_t2 = nc.dram_tensor("t2_hbm", [D, TL], BF16, kind=skind)
    d_cr = nc.dram_tensor("crows_hbm", [16, 1024], BF16, kind=skind)
    d_out = nc.dram_tensor("outT", [D, TL], F32, kind="ExternalOutput")

    with ExitStack() as ctx:
        b = B(nc, ctx)

        def sb(name, shape, dt):
            t = ctx.enter_context(nc.sbuf_tensor(name, shape, dt))
            return t, b.res(name)

        ones_c, r_ones_c = sb("ones_c", [128, 1], BF16)
        ones_r, r_ones_r = sb("ones_r", [1, 128], BF16)
        ones_rf, r_ones_rf = sb("ones_rf", [1, 128], F32)
        ident, r_ident = sb("identS", [128, 128], BF16)
        cst, r_cst = sb("cst", [128, 4], F32)
        srow, r_srow = sb("srow", [1, 2560], F32)
        ln_sq, r_ln_sq = sb("ln_sq", [128, KT * 512], BF16)
        brow_g, r_brow_g = sb("brow_g", [1, 1024], BF16)
        ln_rb, r_ln_rb = sb("ln_rb", [128, 512], F32)
        ln_mb, r_ln_mb = sb("ln_mb", [128, 512], F32)
        ln_rb16, r_ln_rb16 = sb("ln_rb16", [128, 512], BF16)
        ln_mb16, r_ln_mb16 = sb("ln_mb16", [128, 512], BF16)

        rD = {n: b.res("dram_" + n) for n in
              ("t1", "k", "v", "t1q", "o", "x2", "t2", "cr", "out")}
        dram = dict(t1=d_t1, k=d_k, v=d_v, t1q=d_t1q, o=d_o, x2=d_x2,
                    t2=d_t2, cr=d_cr, out=d_out)

        b.op("gpsimd", lambda E: E.memset(ones_c[:, :], 1.0), writes=[r_ones_c])
        b.op("gpsimd", lambda E: E.memset(ones_r[:, :], 1.0), writes=[r_ones_r])
        b.op("gpsimd", lambda E: E.memset(ones_rf[:, :], 1.0), writes=[r_ones_rf])
        b.op("vector", lambda E: E.memset(cst[:, 0:1], EXP_BIAS), writes=[r_cst])
        b.op("vector", lambda E: E.memset(cst[:, 1:2], EPS), writes=[r_cst])
        b.op("vector", lambda E: E.memset(cst[:, 2:3], 0.0), writes=[r_cst])
        b.op("sync", lambda E: E.dma_start(ident[:, :], DI["identity"].ap()),
             writes=[r_ident], dma=True)

        G = dict(nc=nc, b=b, ctx=ctx, DI=DI, dram=dram, rD=rD, sb=sb,
                 ones_c=ones_c, r_ones_c=r_ones_c, ones_r=ones_r,
                 r_ones_r=r_ones_r, ones_rf=ones_rf, r_ones_rf=r_ones_rf,
                 ident=ident, r_ident=r_ident, cst=cst, r_cst=r_cst,
                 srow=srow, r_srow=r_srow,
                 ln_sq=ln_sq, r_ln_sq=r_ln_sq, brow_g=brow_g, r_brow_g=r_brow_g,
                 ln_rb=ln_rb, r_ln_rb=r_ln_rb, ln_mb=ln_mb, r_ln_mb=r_ln_mb,
                 ln_rb16=ln_rb16, r_ln_rb16=r_ln_rb16,
                 ln_mb16=ln_mb16, r_ln_mb16=r_ln_mb16)

        if "A" in PHASES:
            phase_A(G)
        b.barrier()
        if "B" in PHASES:
            phase_B(G)
        b.barrier()
        if "C" in PHASES:
            phase_C(G)

        for s, v in list(b.semv.items()):
            if v > 0:
                b._wait("sync", s, v)

        b.emit()
    return nc



def ln_stats(G, xb, r_xb, ncols, bS, rbS, bQ, rbQ, nkt=KT):
    b = G["b"]
    srow, r_srow, cst, r_cst = G["srow"], G["r_srow"], G["cst"], G["r_cst"]
    ones_c, r_ones_c = G["ones_c"], G["r_ones_c"]
    ln_sq, r_ln_sq = G["ln_sq"], G["r_ln_sq"]

    b.op("vector", lambda E: E.tensor_tensor(
        ln_sq[:, 0:nkt * ncols], xb[:, 0:nkt * ncols], xb[:, 0:nkt * ncols],
        op=ALU.mult), reads=[r_xb], writes=[r_ln_sq])

    def mm_s(E):
        for k in range(nkt):
            ins = E.matmul(bS[0:1, 0:ncols], ones_c[:, :],
                           xb[:, k * ncols:(k + 1) * ncols],
                           start=(k == 0), stop=(k == nkt - 1))
        return ins
    b.op("tensor", mm_s, reads=[r_ones_c, r_xb], writes=[rbS])

    def mm_q(E):
        for k in range(nkt):
            ins = E.matmul(bQ[0:1, 0:ncols], ones_c[:, :],
                           ln_sq[:, k * ncols:(k + 1) * ncols],
                           start=(k == 0), stop=(k == nkt - 1))
        return ins
    b.op("tensor", mm_q, reads=[r_ones_c, r_ln_sq], writes=[rbQ])

    b.op("vector", lambda E: E.tensor_scalar(
        out=srow[0:1, 1024:1024 + ncols], in0=bS[0:1, 0:ncols],
        scalar1=1.0 / D, scalar2=None, op0=ALU.mult),
        reads=[rbS], writes=[r_srow])
    b.op("vector", lambda E: E.tensor_scalar(
        out=srow[0:1, 0:ncols], in0=bQ[0:1, 0:ncols],
        scalar1=1.0 / D, scalar2=None, op0=ALU.mult),
        reads=[rbQ], writes=[r_srow])
    b.op("vector", lambda E: E.tensor_tensor(
        srow[0:1, 512:512 + ncols], srow[0:1, 1024:1024 + ncols],
        srow[0:1, 1024:1024 + ncols], op=ALU.mult),
        reads=[r_srow], writes=[r_srow])
    b.op("vector", lambda E: E.tensor_tensor(
        srow[0:1, 0:ncols], srow[0:1, 0:ncols], srow[0:1, 512:512 + ncols],
        op=ALU.subtract), reads=[r_srow], writes=[r_srow])
    b.op("scalar", lambda E: E.activation(
        srow[0:1, 512:512 + ncols], srow[0:1, 0:ncols], AF.Ln,
        bias=cst[0:1, 1:2], scale=1.0), reads=[r_srow, r_cst], writes=[r_srow])
    b.op("scalar", lambda E: E.activation(
        srow[0:1, 1536:1536 + ncols], srow[0:1, 512:512 + ncols], AF.Exp,
        bias=cst[0:1, 2:3], scale=-0.5), reads=[r_srow, r_cst], writes=[r_srow])
    b.op("vector", lambda E: E.tensor_tensor(
        srow[0:1, 2048:2048 + ncols], srow[0:1, 1024:1024 + ncols],
        srow[0:1, 1536:1536 + ncols], op=ALU.mult),
        reads=[r_srow], writes=[r_srow])


def ln_bcast(G, ncols, bS, rbS, bQ, rbQ, f32=False):
    b = G["b"]
    srow, r_srow = G["srow"], G["r_srow"]
    if f32:
        ones, r_ones = G["ones_rf"], G["r_ones_rf"]
        rb, r_rb, mb, r_mb = G["ln_rb"], G["r_ln_rb"], G["ln_mb"], G["r_ln_mb"]
        def mmb(E):
            E.matmul(bS[:, 0:ncols], ones[0:1, :], srow[0:1, 1536:1536 + ncols],
                     start=True, stop=True)
            return E.matmul(bQ[:, 0:ncols], ones[0:1, :],
                            srow[0:1, 2048:2048 + ncols], start=True, stop=True)
        b.op("tensor", mmb, reads=[r_ones, r_srow], writes=[rbS, rbQ])
    else:
        ones, r_ones = G["ones_r"], G["r_ones_r"]
        rb, r_rb, mb, r_mb = (G["ln_rb16"], G["r_ln_rb16"],
                              G["ln_mb16"], G["r_ln_mb16"])
        brow, r_brow = G["brow_g"], G["r_brow_g"]
        b.op("scalar", lambda E: E.copy(
            brow[0:1, 0:ncols], srow[0:1, 1536:1536 + ncols]),
            reads=[r_srow], writes=[r_brow])
        b.op("scalar", lambda E: E.copy(
            brow[0:1, 512:512 + ncols], srow[0:1, 2048:2048 + ncols]),
            reads=[r_srow], writes=[r_brow])
        def mmb(E):
            E.matmul(bS[:, 0:ncols], ones[0:1, :], brow[0:1, 0:ncols],
                     start=True, stop=True)
            return E.matmul(bQ[:, 0:ncols], ones[0:1, :],
                            brow[0:1, 512:512 + ncols], start=True, stop=True)
        b.op("tensor", mmb, reads=[r_ones, r_brow], writes=[rbS, rbQ])
    b.op("vector", lambda E: E.tensor_copy(rb[:, 0:ncols], bS[:, 0:ncols]),
         reads=[rbS], writes=[r_rb])
    b.op("vector", lambda E: E.tensor_copy(mb[:, 0:ncols], bQ[:, 0:ncols]),
         reads=[rbQ], writes=[r_mb])
    return rb, r_rb, mb, r_mb


def ln_apply_bf16(G, xb, r_xb, out, r_out, out_off, ncols, nkt=KT):
    b = G["b"]
    ln_tmp, r_ln_tmp = G["ln_sq"], G["r_ln_sq"]
    rb, r_rb = G["ln_rb16"], G["r_ln_rb16"]
    mb, r_mb = G["ln_mb16"], G["r_ln_mb16"]
    b.op("vector", lambda E: E.tensor_tensor(
        ln_tmp[:, 0:nkt * ncols].rearrange("p (k n) -> p k n", k=nkt),
        xb[:, 0:nkt * ncols].rearrange("p (k n) -> p k n", k=nkt),
        rb[:, None, 0:ncols].broadcast_to([128, nkt, ncols]), op=ALU.mult),
        reads=[r_xb, r_rb], writes=[r_ln_tmp])
    b.op("vector", lambda E: E.tensor_tensor(
        out[:, out_off:out_off + nkt * ncols].rearrange("p (k n) -> p k n", k=nkt),
        ln_tmp[:, 0:nkt * ncols].rearrange("p (k n) -> p k n", k=nkt),
        mb[:, None, 0:ncols].broadcast_to([128, nkt, ncols]), op=ALU.subtract),
        reads=[r_ln_tmp, r_mb], writes=[r_out])



def phase_A(G):
    nc, b, ctx, DI = G["nc"], G["b"], G["ctx"], G["DI"]
    dram, rD = G["dram"], G["rD"]
    sb = G["sb"]

    with ExitStack() as pc:
        def psb(name, shape, dt):
            t = pc.enter_context(nc.sbuf_tensor(name, shape, dt))
            return t, b.res(name)

        def pps(name, shape):
            t = pc.enter_context(nc.psum_tensor(name, shape, F32))
            return t, b.res(name, xread=True)

        bS, rbS = pps("pa_s", [128, 512])
        bQ, rbQ = pps("pa_q", [128, 512])
        bK = [pps(f"pa_k{i}", [128, 512]) for i in range(4)]

        xa = [psb("xa0", [128, KT * 512], BF16), psb("xa1", [128, KT * 512], BF16)]
        t1a = [psb("t1a0", [128, KT * 512], BF16), psb("t1a1", [128, KT * 512], BF16)]
        kout = [psb("ko0", [128, KT * 512], BF16)]
        wslab = [psb(f"wslab{i}", [128, 4 * KT * 128], BF16) for i in range(4)]
        for t in range(4):
            wst, r_wst = wslab[t]
            b.op("gpsimd", lambda E, wst=wst, t=t: E.dma_start(
                wst[:, :], DI["wkS4"].ap()[t]), writes=[r_wst], dma=True)
        ropeC = [psb("ropeC0", [128, 512], F32), psb("ropeC1", [128, 512], F32)]
        ropeS = [psb("ropeS0", [128, 512], F32), psb("ropeS1", [128, 512], F32)]
        krot = [psb("krot0", [128, 512], F32), psb("krot1", [128, 512], F32)]
        ta = [psb("ta0", [128, 512], F32), psb("ta1", [128, 512], F32)]
        tb = [psb("tb0", [128, 512], F32), psb("tb1", [128, 512], F32)]


        def a1_ln_pre(ch):
            cols = slice(512 * ch, 512 * ch + 512)
            xat, r_xat = xa[ch % 2]
            rC, r_rC = ropeC[ch % 2]
            rS, r_rS = ropeS[ch % 2]
            b.op("sync", lambda E, rC=rC, cols=cols: E.dma_start(
                rC[:, :], DI["ropeCk"].ap()[:, cols]), writes=[r_rC], dma=True)
            b.op("sync", lambda E, rS=rS, cols=cols: E.dma_start(
                rS[:, :], DI["ropeSk"].ap()[:, cols]), writes=[r_rS], dma=True)
            b.op("gpsimd", lambda E, xat=xat, ch=ch: E.dma_start(
                xat[:, :], DI["xTc"].ap()[ch]), writes=[r_xat], dma=True)
            ln_stats(G, xat, r_xat, 512, bS, rbS, bQ, rbQ)

        def a1_ln_post(ch):
            cols = slice(512 * ch, 512 * ch + 512)
            xat, r_xat = xa[ch % 2]
            t1t, r_t1t = t1a[ch % 2]
            ln_bcast(G, 512, bS, rbS, bQ, rbQ, f32=False)
            ln_apply_bf16(G, xat, r_xat, t1t, r_t1t, 0, 512)
            b.op("scalar", lambda E, t1t=t1t, cols=cols: E.dma_start(
                dram["t1"].ap()[:, cols].rearrange("(k p) n -> p k n", p=128),
                t1t[:, :].rearrange("p (k n) -> p k n", k=KT)),
                reads=[r_t1t], writes=[rD["t1"]], dma=True)

        def a1_k(ch, mid=None):
            cols = slice(512 * ch, 512 * ch + 512)
            t1t, r_t1t = t1a[ch % 2]
            kot, r_kot = kout[0]
            rC, r_rC = ropeC[ch % 2]
            rS, r_rS = ropeS[ch % 2]
            for h in range(KT):
                if h == 4 and mid is not None:
                    mid()
                pk, rpk = bK[h % 4]
                wkt, r_wkt = wslab[h // 4]
                hoff = (h % 4) * (KT * 128)
                def mmk(E, h=h, t1t=t1t, pk=pk, wkt=wkt, hoff=hoff):
                    for k in range(KT):
                        ins = E.matmul(pk[:, :],
                                       wkt[:, hoff + k * 128:hoff + (k + 1) * 128],
                                       t1t[:, k * 512:(k + 1) * 512],
                                       start=(k == 0), stop=(k == KT - 1))
                    return ins
                b.op("tensor", mmk, reads=[r_wkt, r_t1t], writes=[rpk])
                kr, r_kr = krot[h % 2]
                tat, r_tat = ta[h % 2]
                tbt, r_tbt = tb[h % 2]
                def rot(E, pk=pk, kr=kr):
                    E.copy(kr[0:64, :], pk[64:128, 0:512])
                    return E.copy(kr[64:128, :], pk[0:64, 0:512])
                b.op("scalar", rot, reads=[rpk], writes=[r_kr])
                b.op("vector", lambda E, pk=pk, tat=tat, rC=rC: E.tensor_tensor(
                    tat[:, :], rC[:, :], pk[:, 0:512], op=ALU.mult),
                    reads=[rpk, r_rC], writes=[r_tat])
                b.op("gpsimd", lambda E, kr=kr, tbt=tbt, rS=rS: E.tensor_tensor(
                    tbt[:, :], kr[:, :], rS[:, :], op=ALU.mult),
                    reads=[r_kr, r_rS], writes=[r_tbt])
                b.op("vector", lambda E, tat=tat, tbt=tbt, kot=kot, h=h:
                     E.tensor_tensor(kot[:, h * 512:(h + 1) * 512], tat[:, :],
                                     tbt[:, :], op=ALU.add),
                     reads=[r_tat, r_tbt], writes=[r_kot])
            b.op("scalar", lambda E, kot=kot, cols=cols: E.dma_start(
                dram["k"].ap()[:, cols].rearrange("(k p) n -> p k n", p=128),
                kot[:, :].rearrange("p (k n) -> p k n", k=KT)),
                reads=[r_kot], writes=[rD["k"]], dma=True)

        a1_ln_pre(0)
        a1_ln_post(0)
        for ch in range(1, A_CHUNKS + 1):
            if ch < A_CHUNKS:
                a1_ln_pre(ch)
                a1_k(ch - 1, mid=lambda ch=ch: a1_ln_post(ch))
            else:
                a1_k(ch - 1)

        def b0_chunk(qc):
            cols = slice(512 * qc, 512 * qc + 512)
            xqt, r_xqt = xa[qc % 2]
            tqt, r_tqt = kout[0]
            b.op("gpsimd", lambda E, xqt=xqt, qc=qc: E.dma_start(
                xqt[:, :], DI["xTqc"].ap()[qc]), writes=[r_xqt], dma=True)
            ln_stats(G, xqt, r_xqt, 512, bS, rbS, bQ, rbQ)
            ln_bcast(G, 512, bS, rbS, bQ, rbQ, f32=False)
            ln_apply_bf16(G, xqt, r_xqt, tqt, r_tqt, 0, 512)
            b.op("scalar", lambda E, tqt=tqt, cols=cols: E.dma_start(
                dram["t1q"].ap()[:, cols].rearrange("(k p) n -> p k n", p=128),
                tqt[:, :].rearrange("p (k n) -> p k n", k=KT)),
                reads=[r_tqt], writes=[rD["t1q"]], dma=True)

        vout = [psb(f"vo{i}", [128, 2048], BF16) for i in range(2)]
        for dc in range(4):
            wvt, r_wvt = wslab[dc]
            b.op("gpsimd", lambda E, wvt=wvt, dc=dc: E.dma_start(
                wvt[:, :], DI["wvS"].ap()[dc]), writes=[r_wvt], dma=True)
        for ch in range(A_CHUNKS):
            cols = slice(512 * ch, 512 * ch + 512)
            t1t, r_t1t = t1a[ch % 2]
            b.op("sync", lambda E, t1t=t1t, cols=cols: E.dma_start(
                t1t[:, :].rearrange("p (k n) -> p k n", k=KT),
                dram["t1"].ap()[:, cols].rearrange("(k p) n -> p k n", p=128)),
                reads=[rD["t1"]], writes=[r_t1t], dma=True)
            for tt in range(4):
                vo, r_vo = vout[tt % 2]
                for dc in range(4):
                    wvt, r_wvt = wslab[dc]
                    pk, rpk = bK[(tt * 4 + dc) % 4]
                    def mmv(E, t1t=t1t, tt=tt, dc=dc, pk=pk, wvt=wvt):
                        for k in range(KT):
                            ins = E.matmul(
                                pk[:, :],
                                t1t[:, k * 512 + tt * 128:k * 512 + tt * 128 + 128],
                                wvt[:, k * 512:(k + 1) * 512],
                                start=(k == 0), stop=(k == KT - 1))
                        return ins
                    b.op("tensor", mmv, reads=[r_wvt, r_t1t], writes=[rpk])
                    b.op("vector", lambda E, pk=pk, vo=vo, dc=dc: E.tensor_copy(
                        vo[:, dc * 512:(dc + 1) * 512], pk[:, :]),
                        reads=[rpk], writes=[r_vo])
                b.op("scalar", lambda E, vo=vo, ch=ch, tt=tt: E.dma_start(
                    dram["v"].ap()[ch * 4 + tt], vo[:, :]),
                    reads=[r_vo], writes=[rD["v"]], dma=True)
            if ch % 2 == 1 and ch // 2 < 4:
                b0_chunk(ch // 2)


def phase_B(G):
    nc, b, DI = G["nc"], G["b"], G["DI"]
    dram, rD = G["dram"], G["rD"]


    with ExitStack() as pc:
        def psb(name, shape, dt):
            t = pc.enter_context(nc.sbuf_tensor(name, shape, dt))
            return t, b.res(name)

        def pps(name, shape, dt=F32):
            t = pc.enter_context(nc.psum_tensor(name, shape, dt))
            return t, b.res(name, xread=True)

        pgA, rpgA = pps("pb1_ga", [128, 1536])
        pgB, rpgB = pps("pb1_gb", [128, 1536])
        pSum, rpSum = pps("pb1_sum", [128, 512])
        pO, rpO = pps("pb1_o", [128, 512])

        cst, r_cst = G["cst"], G["r_cst"]
        srow, r_srow = G["srow"], G["r_srow"]
        ident, r_ident = G["ident"], G["r_ident"]
        ones_c, r_ones_c = G["ones_c"], G["r_ones_c"]
        ones_rf, r_ones_rf = G["ones_rf"], G["r_ones_rf"]

        mask, r_mask = psb("mask_sb", [128, 8 * 512], BF16)
        ropeCq, r_ropeCq = psb("ropeCq_sb", [128, TL], F32)
        ropeSq, r_ropeSq = psb("ropeSq_sb", [128, TL], F32)
        b.op("sync", lambda E: E.dma_start(
            mask[:, :].rearrange("p (j n) -> p j n", j=8),
            DI["maskT"].ap().rearrange("j p n -> p j n")),
            writes=[r_mask], dma=True)
        b.op("sync", lambda E: E.dma_start(ropeCq[:, :], DI["ropeCq"].ap()),
             writes=[r_ropeCq], dma=True)
        b.op("sync", lambda E: E.dma_start(ropeSq[:, :], DI["ropeSq"].ap()),
             writes=[r_ropeSq], dma=True)

        wq = [psb("wq0", [128, KT * 256], BF16), psb("wq1", [128, KT * 256], BF16)]
        K2a = [psb("K2a0", [128, 2 * 1024], BF16), psb("K2a1", [128, 2 * 1024], BF16)]
        K2b = [psb("K2b0", [128, 2 * 3072], BF16), psb("K2b1", [128, 2 * 3072], BF16)]
        V2a = [psb("V2a0", [128, 2 * 1024], BF16), psb("V2a1", [128, 2 * 1024], BF16)]
        V2b = [psb("V2b0", [128, 2 * 3072], BF16), psb("V2b1", [128, 2 * 3072], BF16)]

        def kv_slice(slot, which, hh, kt):
            if kt < 8:
                t_, r_ = (K2a if which == "k" else V2a)[slot]
                return t_, r_, hh * 1024 + kt * 128
            t_, r_ = (K2b if which == "k" else V2b)[slot]
            return t_, r_, hh * 3072 + (kt - 8) * 128
        t1qc = [psb("t1qc0", [128, KT * 512], BF16),
                psb("t1qc1", [128, KT * 512], BF16)]
        qsb = [psb("qsb0", [128, 512], BF16), psb("qsb1", [128, 512], BF16)]
        qrot = psb("qrot", [128, 512], F32)
        qta = psb("qta", [128, 512], F32)
        qtb = psb("qtb", [128, 512], F32)
        PT = [psb(f"pt{i}", [128, 1536], BF16) for i in range(3)]
        rrow = psb("rrow", [1, 512], F32)
        rbo = psb("rbo", [128, 512], F32)
        Opair = psb("Opair", [128, 2 * TL], BF16)

        def load_pair(hp, slot):
            wqt, r_wqt = wq[slot]
            b.op("gpsimd", lambda E, wqt=wqt, hp=hp: E.dma_start(
                wqt[:, :], DI["wqS"].ap()[hp]), writes=[r_wqt], dma=True)
            for hh in range(2):
                h = 2 * hp + hh
                ka, r_ka = K2a[slot]
                kb, r_kb = K2b[slot]
                va, r_va = V2a[slot]
                vb, r_vb = V2b[slot]
                b.op("sync", lambda E, ka=ka, h=h, hh=hh: E.dma_start(
                    ka[:, hh * 1024:(hh + 1) * 1024],
                    dram["k"].ap()[128 * h:128 * h + 128, 0:1024]),
                    reads=[rD["k"]], writes=[r_ka], dma=True)
                b.op("sync", lambda E, va=va, h=h, hh=hh: E.dma_start(
                    va[:, hh * 1024:(hh + 1) * 1024].rearrange("p (kt d) -> p kt d", kt=8),
                    dram["v"].ap()[0:8, :, 128 * h:128 * h + 128]
                    .rearrange("kt p d -> p kt d"),),
                    reads=[rD["v"]], writes=[r_va], dma=True)
            for hh in range(2):
                h = 2 * hp + hh
                kb, r_kb = K2b[slot]
                vb, r_vb = V2b[slot]
                b.op("sync", lambda E, kb=kb, h=h, hh=hh: E.dma_start(
                    kb[:, hh * 3072:(hh + 1) * 3072],
                    dram["k"].ap()[128 * h:128 * h + 128, 1024:4096]),
                    reads=[rD["k"]], writes=[r_kb], dma=True)
                b.op("sync", lambda E, vb=vb, h=h, hh=hh: E.dma_start(
                    vb[:, hh * 3072:(hh + 1) * 3072].rearrange("p (kt d) -> p kt d", kt=24),
                    dram["v"].ap()[8:32, :, 128 * h:128 * h + 128]
                    .rearrange("kt p d -> p kt d"),),
                    reads=[rD["v"]], writes=[r_vb], dma=True)

        load_pair(0, 0)
        for hp in range(8):
            slot = hp % 2
            if hp + 1 < 8:
                load_pair(hp + 1, (hp + 1) % 2)
            wqt, r_wqt = wq[slot]
            for qc in range(4):
                qcols = slice(512 * qc, 512 * qc + 512)
                tq, r_tq = t1qc[qc % 2]
                b.op("gpsimd", lambda E, tq=tq, qcols=qcols: E.dma_start(
                    tq[:, :].rearrange("p (k n) -> p k n", k=KT),
                    dram["t1q"].ap()[:, qcols].rearrange("(k p) n -> p k n", p=128)),
                    reads=[rD["t1q"]], writes=[r_tq], dma=True)
                nkt = 8 * qc + 8
                for hh in range(2):
                    def mmq(E, wqt=wqt, tq=tq, hh=hh):
                        for k in range(KT):
                            ins = E.matmul(
                                pgA[:, 0:512],
                                wqt[:, k * 256 + hh * 128:k * 256 + hh * 128 + 128],
                                tq[:, k * 512:(k + 1) * 512],
                                start=(k == 0), stop=(k == KT - 1))
                        return ins
                    b.op("tensor", mmq, reads=[r_wqt, r_tq], writes=[rpgA])
                    qr, r_qr = qrot
                    qa, r_qa = qta
                    qb, r_qb = qtb
                    def qrots(E, qr=qr):
                        E.copy(qr[0:64, :], pgA[64:128, 0:512])
                        return E.copy(qr[64:128, :], pgA[0:64, 0:512])
                    b.op("scalar", qrots, reads=[rpgA], writes=[r_qr])
                    b.op("vector", lambda E, qa=qa, qcols=qcols: E.tensor_tensor(
                        qa[:, :], ropeCq[:, qcols], pgA[:, 0:512], op=ALU.mult),
                        reads=[rpgA, r_ropeCq], writes=[r_qa])
                    b.op("gpsimd", lambda E, qr=qr, qb=qb, qcols=qcols:
                         E.tensor_tensor(qb[:, :], qr[:, :], ropeSq[:, qcols],
                                         op=ALU.mult),
                         reads=[r_qr, r_ropeSq], writes=[r_qb])
                    qs, r_qs = qsb[hh]
                    b.op("vector", lambda E, qa=qa, qb=qb, qs=qs: E.tensor_tensor(
                        qs[:, :], qa[:, :], qb[:, :], op=ALU.add),
                        reads=[r_qa, r_qb], writes=[r_qs])
                for hh in range(2):
                    qs, r_qs = qsb[hh]
                    groups = [list(range(g, min(g + 3, nkt)))
                              for g in range(0, nkt, 3)]
                    pend = None

                    def sums_av(tiles, slot_idx, first, last):
                        pts, r_pts = PT[slot_idx]
                        vtensors = [kv_slice(slot, "v", hh, kt) for kt in tiles]
                        def mm2(E, tiles=tiles, pts=pts, first=first, last=last,
                                vtensors=vtensors):
                            for i, kt in enumerate(tiles):
                                E.matmul(pSum[0:1, :], ones_c[:, :],
                                         pts[:, i * 512:(i + 1) * 512],
                                         start=(first and i == 0),
                                         stop=(last and i == len(tiles) - 1))
                                v_t, _, voff = vtensors[i]
                                ins = E.matmul(
                                    pO[:, :], v_t[:, voff:voff + 128],
                                    pts[:, i * 512:(i + 1) * 512],
                                    start=(first and i == 0),
                                    stop=(last and i == len(tiles) - 1))
                            return ins
                        b.op("tensor", mm2,
                             reads=[r_ones_c, PT[slot_idx][1]] + [vv[1] for vv in vtensors],
                             writes=[rpSum, rpO])

                    for gi, tiles in enumerate(groups):
                        grp, rgrp = (pgA, rpgA) if gi % 2 == 0 else (pgB, rpgB)
                        wid = len(tiles) * 512
                        ktensors = [kv_slice(slot, "k", hh, kt) for kt in tiles]
                        def mms(E, tiles=tiles, grp=grp, qs=qs, qc=qc,
                                ktensors=ktensors):
                            for i, kt in enumerate(tiles):
                                sl = grp[:, i * 512:(i + 1) * 512]
                                kt_t, _, koff = ktensors[i]
                                if kt >= 8 * qc:
                                    E.matmul(sl, kt_t[:, koff:koff + 128],
                                             qs[:, :], start=True, stop=False)
                                    ins = E.matmul(
                                        sl, ident[:, :],
                                        mask[:, (kt - 8 * qc) * 512:
                                             (kt - 8 * qc) * 512 + 512],
                                        start=False, stop=True)
                                else:
                                    ins = E.matmul(
                                        sl, kt_t[:, koff:koff + 128],
                                        qs[:, :], start=True, stop=True)
                            return ins
                        b.op("tensor", mms,
                             reads=[r_qs, r_ident, r_mask] + [kk[1] for kk in ktensors],
                             writes=[rgrp])
                        if gi == 0 and G.get("b1_pending") is not None:
                            G["b1_pending"]()
                            G["b1_pending"] = None
                        pts, r_pts = PT[gi % 3]
                        b.op("scalar", lambda E, grp=grp, pts=pts, wid=wid:
                             E.activation(pts[:, 0:wid], grp[:, 0:wid], AF.Exp,
                                          bias=cst[:, 0:1], scale=ISCALE),
                             reads=[rgrp, r_cst], writes=[r_pts])
                        if pend is not None:
                            sums_av(*pend)
                        pend = (tiles, gi % 3,
                                gi == 0, gi == len(groups) - 1)
                    if pend is not None:
                        sums_av(*pend)

                    def mknorm(hh=hh, qc=qc):
                        def donorm():
                            rr, r_rr = rrow
                            b.op("vector", lambda E, rr=rr: E.reciprocal(
                                rr[0:1, :], pSum[0:1, :]),
                                reads=[rpSum], writes=[r_rr])
                            def mmbc(E, rr=rr):
                                return E.matmul(pSum[:, :], ones_rf[0:1, :],
                                                rr[0:1, :], start=True, stop=True)
                            b.op("tensor", mmbc, reads=[r_ones_rf, r_rr],
                                 writes=[rpSum])
                            rb, r_rb = rbo
                            b.op("vector", lambda E, rb=rb: E.tensor_copy(
                                rb[:, :], pSum[:, :]), reads=[rpSum], writes=[r_rb])
                            Op, r_Op = Opair
                            b.op("vector", lambda E, rb=rb, Op=Op, hh=hh, qc=qc:
                                 E.tensor_tensor(
                                     Op[:, hh * TL + qc * 512:hh * TL + qc * 512 + 512],
                                     rb[:, :], pO[:, :], op=ALU.mult),
                                 reads=[rpO, r_rb], writes=[r_Op])
                        return donorm
                    G["b1_pending"] = mknorm()
            if G.get("b1_pending") is not None:
                G["b1_pending"]()
                G["b1_pending"] = None
            Op, r_Op = Opair
            for hh in range(2):
                h = 2 * hp + hh
                b.op("scalar", lambda E, Op=Op, h=h, hh=hh: E.dma_start(
                    dram["o"].ap()[128 * h:128 * h + 128, :],
                    Op[:, hh * TL:(hh + 1) * TL]),
                    reads=[r_Op], writes=[rD["o"]], dma=True)

    b.barrier()

    with ExitStack() as pc:
        def psb(name, shape, dt):
            t = pc.enter_context(nc.sbuf_tensor(name, shape, dt))
            return t, b.res(name)

        def pps(name, shape, dt=F32):
            t = pc.enter_context(nc.psum_tensor(name, shape, dt))
            return t, b.res(name, xread=True)

        bS, rbS = pps("pb2_s", [128, 512])
        bQ, rbQ = pps("pb2_q", [128, 512])
        bW = [pps(f"pb2_w{i}", [128, 256]) for i in range(4)]

        wo, r_wo = psb("wo_sb", [128, KT * 2048], BF16)
        b.op("gpsimd", lambda E: E.dma_start(
            wo[:, :].rearrange("p (k e) -> p k e", k=KT),
            DI["woT"].ap().rearrange("(k p) e -> p k e", p=128)),
            writes=[r_wo], dma=True)
        o2, r_o2 = psb("o2_0", [128, KT * 256], BF16)
        xqf, r_xqf = psb("xqf0", [128, KT * 256], F32)
        attf = [psb("attf0", [128, KT * 256], F32), psb("attf1", [128, KT * 256], F32)]
        attb = [psb("attb0", [128, KT * 256], BF16), psb("attb1", [128, KT * 256], BF16)]
        x2f, r_x2f = psb("x2f", [128, KT * 256], F32)
        x2b, r_x2b = psb("x2b16", [128, KT * 256], BF16)
        t2o, r_t2o = psb("t2o", [128, KT * 256], BF16)

        def b2_wo(u, mid=None):
            cols = slice(256 * u, 256 * u + 256)
            b.op("sync", lambda E, cols=cols: E.dma_start(
                o2[:, :].rearrange("p (k n) -> p k n", k=KT),
                dram["o"].ap()[:, cols].rearrange("(k p) n -> p k n", p=128)),
                reads=[rD["o"]], writes=[r_o2], dma=True)
            atf, r_atf = attf[u % 2]
            atb, r_atb = attb[u % 2]
            for dt in range(KT):
                if mid is not None and dt in mid:
                    mid[dt]()
                pw, rpw = bW[dt % 4]
                def mmo(E, dt=dt, pw=pw):
                    for k in range(KT):
                        ins = E.matmul(
                            pw[:, :],
                            wo[:, k * 2048 + dt * 128:k * 2048 + dt * 128 + 128],
                            o2[:, k * 256:(k + 1) * 256],
                            start=(k == 0), stop=(k == KT - 1))
                    return ins
                b.op("tensor", mmo, reads=[r_wo, r_o2], writes=[rpw])
                b.op("scalar", lambda E, pw=pw, atf=atf, dt=dt: E.copy(
                    atf[:, dt * 256:(dt + 1) * 256], pw[:, :]),
                    reads=[rpw], writes=[r_atf])
                b.op("vector", lambda E, pw=pw, atb=atb, dt=dt: E.tensor_copy(
                    atb[:, dt * 256:(dt + 1) * 256], pw[:, :]),
                    reads=[rpw], writes=[r_atb])

        def b2_ln_a(u):
            atb, r_atb = attb[u % 2]
            b.op("sync", lambda E, u=u: E.dma_start(
                xqf[:, :].rearrange("p (k n) -> p k n", k=KT),
                DI["xTqc"].ap()[u // 2]
                .rearrange("p (k n) -> p k n", k=KT)[:, :, (u % 2) * 256:(u % 2) * 256 + 256]),
                writes=[r_xqf], dma=True)
            ln_stats(G, atb, r_atb, 256, bS, rbS, bQ, rbQ)

        def b2_ln_b(u):
            cols = slice(256 * u, 256 * u + 256)
            atf, r_atf = attf[u % 2]
            rb, r_rb, mb, r_mb = ln_bcast(G, 256, bS, rbS, bQ, rbQ, f32=True)
            b.op("vector", lambda E, atf=atf, rb=rb: E.tensor_tensor(
                x2f[:, :].rearrange("p (k n) -> p k n", k=KT),
                atf[:, :].rearrange("p (k n) -> p k n", k=KT),
                rb[:, None, 0:256].broadcast_to([128, KT, 256]), op=ALU.mult),
                reads=[r_atf, r_rb], writes=[r_x2f])
            b.op("vector", lambda E, mb=mb: E.tensor_tensor(
                x2f[:, :].rearrange("p (k n) -> p k n", k=KT),
                x2f[:, :].rearrange("p (k n) -> p k n", k=KT),
                mb[:, None, 0:256].broadcast_to([128, KT, 256]), op=ALU.subtract),
                reads=[r_x2f, r_mb], writes=[r_x2f])
            b.op("vector", lambda E: E.tensor_tensor(
                x2f[:, :], x2f[:, :], xqf[:, :], op=ALU.add),
                reads=[r_x2f, r_xqf], writes=[r_x2f])
            b.op("scalar", lambda E, cols=cols: E.dma_start(
                dram["x2"].ap()[:, cols].rearrange("(k p) n -> p k n", p=128),
                x2f[:, :].rearrange("p (k n) -> p k n", k=KT)),
                reads=[r_x2f], writes=[rD["x2"]], dma=True)

        def b2_ln_c(u):
            b.op("scalar", lambda E: E.copy(x2b[:, :], x2f[:, :]),
                 reads=[r_x2f], writes=[r_x2b])
            ln_stats(G, x2b, r_x2b, 256, bS, rbS, bQ, rbQ)

        def b2_ln_d(u):
            cols = slice(256 * u, 256 * u + 256)
            ln_bcast(G, 256, bS, rbS, bQ, rbQ, f32=False)
            ln_apply_bf16(G, x2b, r_x2b, t2o, r_t2o, 0, 256)
            b.op("scalar", lambda E, cols=cols: E.dma_start(
                dram["t2"].ap()[:, cols].rearrange("(k p) n -> p k n", p=128),
                t2o[:, :].rearrange("p (k n) -> p k n", k=KT)),
                reads=[r_t2o], writes=[rD["t2"]], dma=True)

        def mids(u):
            return {2: lambda: b2_ln_a(u - 1), 6: lambda: b2_ln_b(u - 1),
                    9: lambda: b2_ln_c(u - 1), 13: lambda: b2_ln_d(u - 1)}

        for u in range(9):
            if u < 8:
                b2_wo(u, mid=mids(u) if u >= 1 else None)
            else:
                b2_ln_a(7); b2_ln_b(7); b2_ln_c(7); b2_ln_d(7)



def phase_C(G):
    nc, b, DI = G["nc"], G["b"], G["DI"]
    dram, rD = G["dram"], G["rD"]
    cst, r_cst = G["cst"], G["r_cst"]
    ident, r_ident = G["ident"], G["r_ident"]
    ones_c, r_ones_c = G["ones_c"], G["r_ones_c"]
    ones_r, r_ones_r = G["ones_r"], G["r_ones_r"]

    with ExitStack() as pc:
        def psb(name, shape, dt):
            t = pc.enter_context(nc.sbuf_tensor(name, shape, dt))
            return t, b.res(name)

        def pps(name, shape, dt=F32):
            t = pc.enter_context(nc.psum_tensor(name, shape, dt))
            return t, b.res(name, xread=True)

        bH1 = [pps("pc_h1a", [128, 512]), pps("pc_h1b", [128, 512])]
        bH3 = [pps("pc_h3a", [128, 512]), pps("pc_h3b", [128, 512])]
        bY = [pps("pc_ya", [128, 512]), pps("pc_yb", [128, 512])]
        bS, rbS = pps("pc_s", [128, 512])
        bQ, rbQ = pps("pc_q", [128, 512])

        t2c, r_t2c = psb("t2c", [128, KT * 1024], BF16)
        ysb, r_ysb = psb("ysb", [128, KT * 1024], F32)
        hsb, r_hsb = psb("hsb", [128, IT * 1024], BF16)
        w1b = [psb(f"w1b{i}", [128, KT * 128], BF16) for i in range(3)]
        w3b = [psb(f"w3b{i}", [128, KT * 128], BF16) for i in range(3)]
        w2b = [psb(f"w2b{i}", [128, IT * 256], BF16) for i in range(2)]
        silu = [psb("silu0", [128, 512], BF16), psb("silu1", [128, 512], BF16)]
        gw, r_gw = psb("gw_sb", [128, KT * 16], BF16)
        glog, r_glog = psb("glog", [128, 16], F32)
        gsc, r_gsc = psb("gsc", [128, 24], F32)
        cbf, r_cbf = psb("cbf", [128, 16], BF16)
        crs, r_crs = psb("crs", [1, 1024], BF16)
        crl, r_crl = psb("crl", [1, 1024], BF16)
        cbc, r_cbc = psb("cbc", [128, 1024], BF16)
        xo = [psb("xo0", [128, 512], F32), psb("xo1", [128, 512], F32)]
        oo = [psb("oo0", [128, 512], F32), psb("oo1", [128, 512], F32)]
        ybf = [psb("ybf0", [128, 512], BF16), psb("ybf1", [128, 512], BF16)]

        b.op("gpsimd", lambda E: E.dma_start(
            gw[:, :], DI["gwS"].ap()), writes=[r_gw], dma=True)

        def load_t2c(cc):
            ccols = slice(1024 * cc, 1024 * cc + 1024)
            b.op("sync", lambda E, ccols=ccols: E.dma_start(
                t2c[:, :].rearrange("p (k n) -> p k n", k=KT),
                dram["t2"].ap()[:, ccols].rearrange("(k p) n -> p k n", p=128)),
                reads=[rD["t2"]], writes=[r_t2c], dma=True)

        load_t2c(0)
        for cc in range(2):

            for tt in range(8):
                def mmg(E, tt=tt):
                    for k in range(KT):
                        ins = E.matmul(
                            bS[:, 0:16],
                            t2c[:, k * 1024 + tt * 128:k * 1024 + tt * 128 + 128],
                            gw[:, k * 16:(k + 1) * 16],
                            start=(k == 0), stop=(k == KT - 1))
                    return ins
                b.op("tensor", mmg, reads=[r_t2c, r_gw], writes=[rbS])
                b.op("vector", lambda E: E.tensor_copy(glog[:, :], bS[:, 0:16]),
                     reads=[rbS], writes=[r_glog])
                b.op("vector", lambda E: E.reduce_max(
                    gsc[:, 0:1], glog[:, :], axis=AX, negate=True),
                    reads=[r_glog], writes=[r_gsc])
                b.op("scalar", lambda E: E.activation(
                    glog[:, :], glog[:, :], AF.Exp, bias=gsc[:, 0:1], scale=1.0,
                    accum_out=gsc[:, 1:2]),
                    reads=[r_glog, r_gsc], writes=[r_glog, r_gsc])
                b.op("vector", lambda E: E.reciprocal(gsc[:, 2:3], gsc[:, 1:2]),
                     reads=[r_gsc], writes=[r_gsc])
                b.op("vector", lambda E: E.tensor_scalar(
                    out=glog[:, :], in0=glog[:, :], scalar1=gsc[:, 2:3],
                    scalar2=None, op0=ALU.mult),
                    reads=[r_glog, r_gsc], writes=[r_glog])
                b.op("vector", lambda E: E.reduce_max(
                    gsc[:, 3:4], glog[:, :], axis=AX),
                    reads=[r_glog], writes=[r_gsc])
                b.op("vector", lambda E: E.tensor_scalar(
                    out=gsc[:, 4:20], in0=glog[:, :], scalar1=gsc[:, 3:4],
                    scalar2=None, op0=ALU.is_ge),
                    reads=[r_glog, r_gsc], writes=[r_gsc])
                b.op("vector", lambda E: E.tensor_scalar(
                    out=gsc[:, 4:20], in0=gsc[:, 4:20], scalar1=-2.0,
                    scalar2=None, op0=ALU.mult),
                    reads=[r_gsc], writes=[r_gsc])
                b.op("vector", lambda E: E.tensor_tensor(
                    gsc[:, 4:20], gsc[:, 4:20], glog[:, :], op=ALU.add),
                    reads=[r_gsc, r_glog], writes=[r_gsc])
                b.op("vector", lambda E: E.reduce_max(
                    gsc[:, 20:21], gsc[:, 4:20], axis=AX),
                    reads=[r_gsc], writes=[r_gsc])
                b.op("vector", lambda E: E.tensor_scalar(
                    out=gsc[:, 4:20], in0=glog[:, :], scalar1=gsc[:, 20:21],
                    scalar2=None, op0=ALU.is_ge),
                    reads=[r_glog, r_gsc], writes=[r_gsc])
                b.op("vector", lambda E: E.tensor_tensor(
                    glog[:, :], glog[:, :], gsc[:, 4:20], op=ALU.mult),
                    reads=[r_glog, r_gsc], writes=[r_glog])
                b.op("vector", lambda E: E.tensor_copy(cbf[:, :], glog[:, :]),
                     reads=[r_glog], writes=[r_cbf])
                for eh in range(2):
                    def mmtr(E, eh=eh):
                        for j in range(8):
                            ins = E.transpose(
                                bQ.ap().bitcast(BF16)[0:1, j * 128:(j + 1) * 128],
                                cbf[:, eh * 8 + j:eh * 8 + j + 1], ident[:, :])
                        return ins
                    b.op("tensor", mmtr, reads=[r_cbf, r_ident], writes=[rbQ])
                    def evtr(E, eh=eh):
                        for j in range(8):
                            ins = E.tensor_copy(
                                crs[0:1, j * 128:(j + 1) * 128],
                                bQ.ap().bitcast(BF16)[0:1, j * 128:(j + 1) * 128])
                        return ins
                    b.op("vector", evtr, reads=[rbQ], writes=[r_crs])
                    b.op("scalar", lambda E, tt=tt, eh=eh: E.dma_start(
                        dram["cr"].ap()[eh * 8:(eh + 1) * 8,
                                        tt * 128:(tt + 1) * 128].unsqueeze(0),
                        crs[0:1, :].rearrange("o (j n) -> o j n", j=8)),
                        reads=[r_crs], writes=[rD["cr"]], dma=True)

            for e in range(NEXP):
                hooks = G.pop("c_hooks", None) if e == 0 else None
                if e < NE:
                    src1 = DI["w1S"].ap()[e]
                    src3 = DI["w3S"].ap()[e]
                    src2 = DI["w2S"].ap()[e]
                else:
                    sh = e - NE
                    src1 = DI["sw1S"].ap()[sh]
                    src3 = DI["sw3S"].ap()[sh]
                    src2 = DI["sw2S"].ap()[sh]

                if e < NE:
                    b.op("sync", lambda E, e=e: E.dma_start(
                        crl[0:1, :], dram["cr"].ap()[e:e + 1, :]),
                        reads=[rD["cr"]], writes=[r_crl], dma=True)
                    def mmcb(E):
                        E.matmul(bS[:, 0:512], ones_r[0:1, :],
                                 crl[0:1, 0:512], start=True, stop=True)
                        return E.matmul(bQ[:, 0:512], ones_r[0:1, :],
                                        crl[0:1, 512:1024], start=True, stop=True)
                    b.op("tensor", mmcb, reads=[r_ones_r, r_crl],
                         writes=[rbS, rbQ])
                    b.op("vector", lambda E: E.tensor_copy(
                        cbc[:, 0:512], bS[:, 0:512]), reads=[rbS], writes=[r_cbc])
                    b.op("vector", lambda E: E.tensor_copy(
                        cbc[:, 512:1024], bQ[:, 0:512]), reads=[rbQ], writes=[r_cbc])

                for it in range(IT):
                    if hooks and it in hooks:
                        hooks.pop(it)()
                    gi = e * IT + it
                    w1t, r_w1t = w1b[gi % 3]
                    w3t, r_w3t = w3b[gi % 3]
                    if it >= 2 or (e == 0 and cc == 0) or not PREFETCH_C:
                        b.op("gpsimd", lambda E, w1t=w1t, src1=src1, it=it: E.dma_start(
                            w1t[:, :], src1[it]), writes=[r_w1t], dma=True)
                        b.op("gpsimd", lambda E, w3t=w3t, src3=src3, it=it: E.dma_start(
                            w3t[:, :], src3[it]), writes=[r_w3t], dma=True)
                    for sub in range(2):
                        p1, rp1 = bH1[(2 * it + sub) % 2]
                        p3, rp3 = bH3[(2 * it + sub) % 2]
                        def mmh(E, w1t=w1t, w3t=w3t, sub=sub, p1=p1, p3=p3):
                            for k in range(KT):
                                E.matmul(p1[:, :], w1t[:, k * 128:(k + 1) * 128],
                                         t2c[:, k * 1024 + sub * 512:
                                             k * 1024 + sub * 512 + 512],
                                         start=(k == 0), stop=(k == KT - 1))
                            for k in range(KT):
                                ins = E.matmul(
                                    p3[:, :], w3t[:, k * 128:(k + 1) * 128],
                                    t2c[:, k * 1024 + sub * 512:
                                        k * 1024 + sub * 512 + 512],
                                    start=(k == 0), stop=(k == KT - 1))
                            return ins
                        b.op("tensor", mmh, reads=[r_w1t, r_w3t, r_t2c],
                             writes=[rp1, rp3])
                        sl, r_sl = silu[sub]
                        b.op("scalar", lambda E, p1=p1, sl=sl: E.activation(
                            sl[:, :], p1[:, :], AF.Silu),
                            reads=[rp1], writes=[r_sl])
                        hoff = it * 1024 + sub * 512
                        if e < NE:
                            b.op("vector", lambda E, sl=sl, p3=p3:
                                 E.tensor_tensor(sl[:, :], sl[:, :], p3[:, :],
                                                 op=ALU.mult),
                                 reads=[r_sl, rp3], writes=[r_sl])
                            b.op("vector", lambda E, sl=sl, hoff=hoff, sub=sub:
                                 E.tensor_tensor(
                                     hsb[:, hoff:hoff + 512], sl[:, :],
                                     cbc[:, sub * 512:sub * 512 + 512],
                                     op=ALU.mult),
                                 reads=[r_sl, r_cbc], writes=[r_hsb])
                        else:
                            b.op("vector", lambda E, sl=sl, p3=p3, hoff=hoff:
                                 E.tensor_tensor(hsb[:, hoff:hoff + 512],
                                                 sl[:, :], p3[:, :], op=ALU.mult),
                                 reads=[r_sl, rp3], writes=[r_hsb])
                if e == NEXP - 1 and cc == 0:
                    load_t2c(1)
                if PREFETCH_C:
                    en, ccn = (e + 1, cc) if e + 1 < NEXP else (0, cc + 1)
                    if ccn < 2 and not (en == 0 and ccn == 0):
                        if en < NE:
                            nsrc1 = DI["w1S"].ap()[en]
                            nsrc3 = DI["w3S"].ap()[en]
                        else:
                            nsrc1 = DI["sw1S"].ap()[en - NE]
                            nsrc3 = DI["sw3S"].ap()[en - NE]
                        for itn in range(2):
                            gin = en * IT + itn
                            nw1, r_nw1 = w1b[gin % 3]
                            nw3, r_nw3 = w3b[gin % 3]
                            b.op("gpsimd", lambda E, nw1=nw1, nsrc1=nsrc1, itn=itn:
                                 E.dma_start(nw1[:, :], nsrc1[itn]),
                                 writes=[r_nw1], dma=True)
                            b.op("gpsimd", lambda E, nw3=nw3, nsrc3=nsrc3, itn=itn:
                                 E.dma_start(nw3[:, :], nsrc3[itn]),
                                 writes=[r_nw3], dma=True)
                for q8 in range(8):
                    w2t, r_w2t = w2b[q8 % 2]
                    b.op("gpsimd", lambda E, w2t=w2t, src2=src2, q8=q8: E.dma_start(
                        w2t[:, :], src2[q8]), writes=[r_w2t], dma=True)
                    for dt in range(2):
                        dtg = q8 * 2 + dt
                        for sub in range(2):
                            py, rpy = bY[(2 * dt + sub) % 2]
                            def mmy(E, dt=dt, sub=sub, py=py, w2t=w2t):
                                for i in range(IT):
                                    ins = E.matmul(
                                        py[:, :],
                                        w2t[:, i * 256 + dt * 128:
                                            i * 256 + dt * 128 + 128],
                                        hsb[:, i * 1024 + sub * 512:
                                            i * 1024 + sub * 512 + 512],
                                        start=(i == 0), stop=(i == IT - 1))
                                return ins
                            b.op("tensor", mmy, reads=[r_w2t, r_hsb], writes=[rpy])
                            yoff = dtg * 1024 + sub * 512
                            if e == 0:
                                b.op("vector", lambda E, py=py, yoff=yoff:
                                     E.tensor_copy(ysb[:, yoff:yoff + 512],
                                                   py[:, :]),
                                     reads=[rpy], writes=[r_ysb])
                            else:
                                b.op("vector", lambda E, py=py, yoff=yoff:
                                     E.tensor_tensor(ysb[:, yoff:yoff + 512],
                                                     ysb[:, yoff:yoff + 512],
                                                     py[:, :], op=ALU.add),
                                     reads=[rpy, r_ysb], writes=[r_ysb])

            def ln4_a(sub, cc=cc):
                srow, r_srow = G["srow"], G["r_srow"]
                cstl, r_cstl = G["cst"], G["r_cst"]
                for k in range(KT):
                    yb, r_yb = ybf[k % 2]
                    yoff = k * 1024 + sub * 512
                    b.op("scalar", lambda E, yb=yb, yoff=yoff: E.copy(
                        yb[:, :], ysb[:, yoff:yoff + 512]),
                        reads=[r_ysb], writes=[r_yb])
                    b.op("tensor", lambda E, yb=yb, k=k: E.matmul(
                        bS[0:1, 0:512], ones_c[:, :], yb[:, :],
                        start=(k == 0), stop=(k == KT - 1)),
                        reads=[r_yb, r_ones_c], writes=[rbS])
                    b.op("vector", lambda E, yb=yb: E.tensor_tensor(
                        yb[:, :], yb[:, :], yb[:, :], op=ALU.mult),
                        reads=[r_yb], writes=[r_yb])
                    b.op("tensor", lambda E, yb=yb, k=k: E.matmul(
                        bQ[0:1, 0:512], ones_c[:, :], yb[:, :],
                        start=(k == 0), stop=(k == KT - 1)),
                        reads=[r_yb, r_ones_c], writes=[rbQ])
                b.op("vector", lambda E: E.tensor_scalar(
                    out=srow[0:1, 1024:1024 + 512], in0=bS[0:1, 0:512],
                    scalar1=1.0 / D, scalar2=None, op0=ALU.mult),
                    reads=[rbS], writes=[r_srow])
                b.op("vector", lambda E: E.tensor_scalar(
                    out=srow[0:1, 0:512], in0=bQ[0:1, 0:512],
                    scalar1=1.0 / D, scalar2=None, op0=ALU.mult),
                    reads=[rbQ], writes=[r_srow])
                b.op("vector", lambda E: E.tensor_tensor(
                    srow[0:1, 512:1024], srow[0:1, 1024:1536],
                    srow[0:1, 1024:1536], op=ALU.mult),
                    reads=[r_srow], writes=[r_srow])
                b.op("vector", lambda E: E.tensor_tensor(
                    srow[0:1, 0:512], srow[0:1, 0:512], srow[0:1, 512:1024],
                    op=ALU.subtract), reads=[r_srow], writes=[r_srow])
                b.op("scalar", lambda E: E.activation(
                    srow[0:1, 512:1024], srow[0:1, 0:512], AF.Ln,
                    bias=cstl[0:1, 1:2], scale=1.0),
                    reads=[r_srow, r_cstl], writes=[r_srow])
                b.op("scalar", lambda E: E.activation(
                    srow[0:1, 1536:2048], srow[0:1, 512:1024], AF.Exp,
                    bias=cstl[0:1, 2:3], scale=-0.5),
                    reads=[r_srow, r_cstl], writes=[r_srow])
                b.op("vector", lambda E: E.tensor_tensor(
                    srow[0:1, 2048:2560], srow[0:1, 1024:1536],
                    srow[0:1, 1536:2048], op=ALU.mult),
                    reads=[r_srow], writes=[r_srow])

            def ln4_b(sub, cc=cc):
                scols = slice(1024 * cc + 512 * sub, 1024 * cc + 512 * sub + 512)
                rb, r_rb, mb, r_mb = ln_bcast(G, 512, bS, rbS, bQ, rbQ, f32=True)
                for dt in range(KT):
                    xt, r_xt = xo[dt % 2]
                    ot, r_ot = oo[dt % 2]
                    b.op("sync", lambda E, xt=xt, dt=dt, scols=scols: E.dma_start(
                        xt[:, :], dram["x2"].ap()[128 * dt:128 * dt + 128, scols]),
                        reads=[rD["x2"]], writes=[r_xt], dma=True)
                    yoff = dt * 1024 + sub * 512
                    b.op("vector", lambda E, ot=ot, yoff=yoff, rb=rb:
                         E.tensor_tensor(ot[:, :], ysb[:, yoff:yoff + 512],
                                         rb[:, 0:512], op=ALU.mult),
                         reads=[r_ysb, r_rb], writes=[r_ot])
                    b.op("vector", lambda E, ot=ot, mb=mb: E.tensor_tensor(
                        ot[:, :], ot[:, :], mb[:, 0:512], op=ALU.subtract),
                        reads=[r_ot, r_mb], writes=[r_ot])
                    b.op("vector", lambda E, ot=ot, xt=xt: E.tensor_tensor(
                        ot[:, :], ot[:, :], xt[:, :], op=ALU.add),
                        reads=[r_ot, r_xt], writes=[r_ot])
                    b.op("scalar", lambda E, ot=ot, dt=dt, scols=scols: E.dma_start(
                        dram["out"].ap()[128 * dt:128 * dt + 128, scols], ot[:, :]),
                        reads=[r_ot], writes=[rD["out"]], dma=True)

            if cc == 0:
                G["c_hooks"] = {1: lambda: ln4_a(0), 3: lambda: ln4_b(0),
                                6: lambda: ln4_a(1), 8: lambda: ln4_b(1)}
            else:
                for sub in range(2):
                    ln4_a(sub)
                    ln4_b(sub)



def _host_prep(inputs):
    import ml_dtypes
    x = np.asarray(inputs["L0_attn_x"], dtype=np.float32)
    shared = {}

    def tile_w1like(w):
        E_ = w.shape[0]
        r = w.reshape(E_, IT, 128, KT, 128)
        r = r.transpose(0, 1, 4, 3, 2)
        return np.ascontiguousarray(r.reshape(E_, IT, 128, KT * 128))

    def tile_w2like(w):
        E_ = w.shape[0]
        r = w.reshape(E_, 8, 256, IT, 128)
        r = r.transpose(0, 1, 4, 3, 2)
        return np.ascontiguousarray(r.reshape(E_, 8, 128, IT * 256))

    wkT = np.asarray(inputs["L0_attn_Wk"], dtype=np.float32).T
    wkS = wkT.reshape(KT, 128, 16, 128).transpose(2, 1, 0, 3).reshape(16, 128, KT * 128)
    shared["wkS4"] = np.ascontiguousarray(
        wkS.reshape(4, 4, 128, KT * 128).transpose(0, 2, 1, 3).reshape(4, 128, 4 * KT * 128))
    wvT = np.asarray(inputs["L0_attn_Wv"], dtype=np.float32).T
    shared["wvS"] = np.ascontiguousarray(
        wvT.reshape(KT, 128, 4, 512).transpose(2, 1, 0, 3).reshape(4, 128, KT * 512))
    shared["woT"] = np.ascontiguousarray(np.asarray(inputs["L0_attn_Wo"], dtype=np.float32).T)
    wqT = np.asarray(inputs["L0_attn_Wq"], dtype=np.float32).T
    r = wqT.reshape(KT, 128, 8, 256).transpose(2, 1, 0, 3)
    shared["wqS"] = np.ascontiguousarray(r.reshape(8, 128, KT * 256))
    gwT = np.asarray(inputs["L0_ffn_gate_w"], dtype=np.float32).T
    shared["gwS"] = np.ascontiguousarray(
        gwT.reshape(KT, 128, 16).transpose(1, 0, 2).reshape(128, KT * 16))
    shared["w1S"] = tile_w1like(np.asarray(inputs["L0_ffn_w1"], dtype=np.float32))
    shared["w3S"] = tile_w1like(np.asarray(inputs["L0_ffn_w3"], dtype=np.float32))
    shared["w2S"] = tile_w2like(np.asarray(inputs["L0_ffn_w2"], dtype=np.float32))
    sw1 = np.asarray(inputs["L0_ffn_sw1"], dtype=np.float32).reshape(2, INTER, D)
    sw3 = np.asarray(inputs["L0_ffn_sw3"], dtype=np.float32).reshape(2, INTER, D)
    shared["sw1S"] = tile_w1like(sw1)
    shared["sw3S"] = tile_w1like(sw3)
    sw2 = np.asarray(inputs["L0_ffn_sw2"], dtype=np.float32)
    sw2e = np.stack([sw2[:, 0:INTER], sw2[:, INTER:2 * INTER]], axis=0)
    shared["sw2S"] = tile_w2like(sw2e)

    half = 64
    pos = np.arange(T, dtype=np.float32)
    inv_freq = 1.0 / (10000.0 ** (np.arange(half, dtype=np.float32) / half))
    ang = inv_freq[:, None] * pos[None, :]
    sin, cos = np.sin(ang), np.cos(ang)
    Ck = np.ascontiguousarray(np.concatenate([cos, cos], axis=0).astype(np.float32))
    Sk = np.ascontiguousarray(np.concatenate([-sin, sin], axis=0).astype(np.float32))
    shared["ropeCk"] = Ck
    shared["ropeSk"] = Sk
    shared["identity"] = np.eye(128, dtype=ml_dtypes.bfloat16)

    per_core = []
    for c in range(8):
        bidx, o = c // 2, c % 2
        idx = np.concatenate([np.arange(512 * (2 * qc + o), 512 * (2 * qc + o) + 512)
                              for qc in range(4)])
        m = dict(shared)
        xb = x[bidx].reshape(8, 512, KT, 128)
        m["xTc"] = np.ascontiguousarray(
            xb.transpose(0, 3, 2, 1).reshape(8, 128, KT * 512))
        xq = x[bidx][idx].reshape(4, 512, KT, 128)
        m["xTqc"] = np.ascontiguousarray(
            xq.transpose(0, 3, 2, 1).reshape(4, 128, KT * 512))
        m["ropeCq"] = np.ascontiguousarray(Ck[:, idx])
        m["ropeSq"] = np.ascontiguousarray(Sk[:, idx])
        j = np.arange(8)[:, None, None]
        p = np.arange(128)[None, :, None]
        n = np.arange(512)[None, None, :]
        mask = np.where(n >= p + 128 * j - 512 * o, 0.0, -1e9)
        m["maskT"] = np.ascontiguousarray(mask.astype(ml_dtypes.bfloat16))
        per_core.append((m, idx))
    return per_core


def kernel(**inputs):
    per_core = _host_prep(inputs)
    nc = build()
    in_maps = [m for (m, _) in per_core]
    res = run_bass_kernel_spmd(nc, in_maps, core_ids=list(range(8)))
    out = np.empty((NB, T, D), dtype=np.float32)
    for c in range(8):
        bidx = c // 2
        _, idx = per_core[c]
        out[bidx, idx, :] = np.asarray(res.results[c]["outT"], dtype=np.float32).T
    return out
